# version 03
# baseline (speedup 1.0000x reference)
import contextlib
import numpy as np
import concourse.bass as bass
import concourse.mybir as mybir
from concourse.bass_utils import run_bass_kernel_spmd

F32 = mybir.dt.float32
BF16 = mybir.dt.bfloat16
AF = mybir.ActivationFunctionType
ALU = mybir.AluOpType
AX = mybir.AxisListType

NCORE = 8
TOK = 1024
NT = 8
HID = 2048
NCH = 16
D = 128
H = 16
FD = 768
EPS = 1e-8
LNEPS = 1e-5


class Sem:
    __slots__ = ("name", "count", "h")

    def __init__(self, name):
        self.name = name
        self.count = 0
        self.h = None

    def bump(self, n=1):
        self.count += n
        return self.count


class Prog:
    ENGINES = ("tensor", "vector", "scalar", "gpsimd", "sync")

    def __init__(self):
        self.q = {e: [] for e in self.ENGINES}
        self.sems = []
        self.barsems = {}

    def sem(self, name):
        s = Sem(name)
        self.sems.append(s)
        return s

    def emit(self, eng, fn):
        self.q[eng].append(fn)

    def wait(self, eng, sem, val=None):
        v = sem.count if val is None else val
        if v <= 0:
            return
        self.emit(eng, lambda e, s=sem, vv=v: e.wait_ge(s.h, vv))

    def op(self, eng, builder, sem=None, inc=1):
        if sem is not None:
            sem.bump(inc)
        extra = []

        def f(e, b=builder, s=sem, n=inc, x=extra):
            inst = b(e)
            if s is not None:
                inst.then_inc(s.h, n)
            for (xs, xn) in x:
                inst.then_inc(xs.h, xn)

        self.emit(eng, f)
        self.lastop = getattr(self, 'lastop', {})
        self.lastop[eng] = (extra, sem, sem.count if sem is not None else 0)

    def nop(self, eng):
        self.emit(eng, lambda e: e.nop())

    def drain(self, eng):
        self.emit(eng, lambda e: e.drain())

    def bar(self, eng):
        extra, psem, pval = self.lastop[eng]
        if psem is not None:
            self.emit(eng, lambda e, ss=psem, v=pval: e.wait_ge(ss.h, v))
            return
        s = self.barsems.get(eng)
        if s is None:
            s = self.sem(f"bar_{eng}")
            self.barsems[eng] = s
        c = s.bump(1)
        extra.append((s, 1))
        self.emit(eng, lambda e, ss=s, v=c: e.wait_ge(ss.h, v))

    def dma(self, qeng, out_ap, in_ap, sem, slow=False):
        sem.bump(16)
        self.emit(qeng, lambda e, o=out_ap, i=in_ap, s=sem, sl=slow:
                  e.dma_start(out=o, in_=i, allow_slow_non_contiguous=sl).then_inc(s.h, 16))


def build_nc(debug=False):
    nc = bass.Bass("TRN2", target_bir_lowering=False)
    P = Prog()

    xh = nc.dram_tensor("x_part", [TOK, HID], F32, kind="ExternalInput").ap()
    wkT = nc.dram_tensor("WkT", [HID, 512], F32, kind="ExternalInput").ap()
    wvT = nc.dram_tensor("WvT", [HID, 512], F32, kind="ExternalInput").ap()
    wmT = nc.dram_tensor("WmT", [HID, HID], F32, kind="ExternalInput").ap()
    wqT = nc.dram_tensor("WqT", [HID, HID], F32, kind="ExternalInput").ap()
    wgT = nc.dram_tensor("WgT", [HID, HID], F32, kind="ExternalInput").ap()
    wfT = nc.dram_tensor("WfT", [HID, HID], F32, kind="ExternalInput").ap()
    bk = nc.dram_tensor("bk", [1, 512], F32, kind="ExternalInput").ap()
    bv = nc.dram_tensor("bv", [1, 512], F32, kind="ExternalInput").ap()
    bm = nc.dram_tensor("bm", [1, HID], F32, kind="ExternalInput").ap()
    bq = nc.dram_tensor("bq", [1, HID], F32, kind="ExternalInput").ap()
    bg = nc.dram_tensor("bg", [1, HID], F32, kind="ExternalInput").ap()
    bfh = nc.dram_tensor("bf", [1, HID], F32, kind="ExternalInput").ap()
    memh = nc.dram_tensor("memory", [H, FD, D], F32, kind="ExternalInput").ap()
    normh = nc.dram_tensor("norm96", [96, 128], F32, kind="ExternalInput").ap()
    identh = nc.dram_tensor("ident", [128, 128], F32, kind="ExternalInput").ap()
    outh = nc.dram_tensor("out_part", [TOK, HID], F32, kind="ExternalOutput").ap()

    q_hbm = nc.dram_tensor("q_scr", [NT, 128, HID], BF16).ap()
    ht_hbm = nc.dram_tensor("ht_scr", [NT, 128, NCH * 128], BF16).ap()
    y_hbm = nc.dram_tensor("y_scr", [NT, 128, HID], F32).ap()
    d_loc = [nc.dram_tensor(f"d_loc{i}", [6, 128, 1032], F32).ap() for i in range(2)]
    d_red = [nc.dram_tensor(f"d_red{i}", [6, 128, 1032], F32, addr_space="Shared").ap() for i in range(2)]

    dbg = {}
    if debug:
        def dbgt(name, shape, dt=F32):
            dbg[name] = nc.dram_tensor("dbg_" + name, shape, dt, kind="ExternalOutput").ap()
        dbgt("hs0", [128, HID], BF16)
        dbgt("msb", [128, 16])
        dbgt("lns0", [128, 28])
        dbgt("bn0", [128, 24])
        dbgt("x0", [128, HID])
        dbgt("lsnap", [128, 28])
        dbgt("kv0", [128, 512], BF16)
        dbgt("mb0", [128, 1024], BF16)
        dbgt("q0", [128, 1024], BF16)
        dbgt("F0", [128, 2, FD], BF16)
        dbgt("nst0", [128, 24])
        dbgt("wmv0", [128, 8, 128], BF16)
        dbgt("coefn0", [128, 8], BF16)
        dbgt("memN", [128, 4, 6, 512], BF16)
        dbgt("normN", [128, 4, 6, 4], BF16)
        dbgt("mhg0", [128, HID])
        dbgt("gate0", [128, HID])
        dbgt("y0", [128, HID])
        dbgt("alpha", [1, HID])
        dbgt("z0", [128, HID])

    xt = xh.rearrange("(n p) d -> n p d", p=128)

    es = contextlib.ExitStack()
    with es:
        def sb(name, shape, dt):
            return es.enter_context(nc.sbuf_tensor(name, shape, dt))

        def pst(name, shape, dt):
            return es.enter_context(nc.psum_tensor(name, shape, dt))

        ps_a = pst("ps_a", [128, 2560], F32)
        ps_n = pst("ps_n", [128, 512], F32)
        ps_dot = pst("ps_dot", [128, 8], F32)
        ps_tr = pst("ps_tr", [128, 8, 128], BF16)

        ident_bf = sb("ident_bf", [128, 128], BF16)
        ident_f = sb("ident_f", [128, 128], F32)
        ones_r = sb("ones_r", [1, 128], BF16)
        ones_rf = sb("ones_rf", [1, 128], F32)
        ones_c = sb("ones_c", [128, 1], BF16)
        normF = sb("normF", [128, 96], F32)
        msb = sb("msb", [128, 16], F32)
        bn24 = sb("bn24", [128, 24], F32)
        lns = sb("lns", [128, 28], F32)
        sq_scr = sb("sq_scr", [128, 2, FD], F32)
        sstat = sb("sstat", [128, 24], F32)
        mv_scr = sb("mv_scr", [128, 4, 128], F32)
        lsnap = sb("lsnap", [128, 28], F32)
        hsb_sb = sb("hsb_sb", [128, HID], BF16)
        hsT = sb("hsT", [128, 2, NCH, 128], BF16)
        memN_a = sb("memN_a", [128, 2, 6, 512], BF16)
        memNs = [memN_a, None]
        normN = sb("normN", [128, 4, 6, 4], BF16)
        mstage = sb("mstage", [128, 1, 6, 128], F32)
        dstage2 = sb("dstage2", [128, 1, 6, 129], F32)
        norm96 = sb("norm96s", [96, 128], F32)
        x_es = contextlib.ExitStack()
        x_sb = x_es.enter_context(nc.sbuf_tensor("x_sb", [128, 2, HID], F32))

        s_const = P.sem("const")
        s_konst = P.sem("konst")
        s_ntr = P.sem("ntr")
        s_endA1 = P.sem("endA1")
        s_endA2 = P.sem("endA2")
        s_endB1 = P.sem("endB1")
        s_cc = [P.sem("cc0"), P.sem("cc1")]
        s_out = P.sem("out")
        s_x = [P.sem("x_s0"), P.sem("x_s1")]
        s_ht = [P.sem("ht_s0"), P.sem("ht_s1")]
        s_htw = [P.sem("htw_s0"), P.sem("htw_s1")]
        s_hs = P.sem("hs")
        s_hsb = P.sem("hsb")
        s_tr = P.sem("tr")
        s_te = P.sem("te")
        s_lnv = P.sem("lnv")
        s_lsn = P.sem("lsn")
        s_wg = P.sem("wg")
        s_mst = P.sem("mst")
        s_memN = P.sem("memN")
        s_lnq = P.sem("lnq")
        s_lnd = P.sem("lnd")

        P.dma("gpsimd", ident_bf[:], identh, s_const)
        P.dma("gpsimd", ident_f[:], identh, s_const)
        P.op("vector", lambda v: v.memset(ones_r[:], 1.0), s_konst)
        P.op("vector", lambda v: v.memset(ones_rf[:], 1.0), s_konst)
        P.op("vector", lambda v: v.memset(ones_c[:], 1.0), s_konst)
        P.op("vector", lambda v: v.memset(lns[:], 0.0), s_konst)
        P.op("vector", lambda v: v.memset(bn24[:], 0.0), s_konst)
        P.op("vector", lambda v: v.memset(sstat[:], 0.0), s_konst)
        P.op("vector", lambda v: v.memset(msb[:], 0.0), s_konst)

        def ln_stats(src_ap, rstd_col, negms_col):
            for i in range(4):
                P.op("vector", lambda v, i=i, a=src_ap: v.bn_stats(
                    bn24[:, 6 * i:6 * i + 6], a[:, 512 * i:512 * (i + 1)]))
            m8 = bn24.ap()[:, 1:24:3]
            M28 = bn24.ap()[:, 2:24:3]
            P.bar("vector")
            P.op("vector", lambda v, m8=m8: v.tensor_tensor(lns[:, 8:16], m8, m8, ALU.mult))
            P.op("vector", lambda v, m8=m8: v.reduce_sum(lns[:, 1:2], m8, axis=AX.X))
            P.bar("vector")
            P.op("vector", lambda v, M28=M28: v.scalar_tensor_tensor(
                lns[:, 20:28], lns[:, 8:16], 256.0, M28, ALU.mult, ALU.add))
            P.op("vector", lambda v: v.tensor_scalar(
                lns[:, 2:3], lns[:, 1:2], 0.125, None, ALU.mult))
            P.bar("vector")
            P.op("vector", lambda v: v.reduce_sum(lns[:, 0:1], lns[:, 20:28], axis=AX.X))
            P.op("vector", lambda v: v.tensor_tensor(
                lns[:, 3:4], lns[:, 2:3], lns[:, 2:3], ALU.mult))
            P.bar("vector")
            P.op("vector", lambda v: v.tensor_scalar(
                lns[:, 4:5], lns[:, 0:1], 1.0 / HID, LNEPS, ALU.mult, ALU.add))
            P.bar("vector")
            P.op("vector", lambda v: v.tensor_tensor(
                lns[:, 18:19], lns[:, 4:5], lns[:, 3:4], ALU.subtract))
            P.bar("vector")
            P.op("vector", lambda v: v.reciprocal(lns[:, 19:20], lns[:, 18:19]), s_lnv)
            P.wait("scalar", s_lnv)
            P.op("scalar", lambda s, rc=rstd_col: s.activation(rc, lns[:, 19:20], AF.Sqrt), s_lnq)
            P.wait("vector", s_lnq)
            P.op("vector", lambda v, rc=rstd_col, nm=negms_col: v.scalar_tensor_tensor(
                nm, lns[:, 2:3], -1.0, rc, ALU.mult, ALU.mult), s_lnd)

        def transposes(srcs, dst, swaits=()):
            n = len(srcs)
            r0 = 0
            while r0 < n:
                rn = min(8, n - r0)
                P.wait("tensor", s_te)
                for i in range(rn):
                    P.op("tensor", lambda t, i=i, a=srcs[r0 + i]: t.transpose(
                        ps_tr[:, i, :], a, ident_bf[:]), s_tr if i == rn - 1 else None)
                P.wait("scalar", s_tr)
                for sw in swaits:
                    P.wait("scalar", sw[0], sw[1])
                swaits = ()
                P.op("scalar", lambda s, r0=r0, rn=rn, d=dst: s.copy(
                    d[:, r0:r0 + rn, :], ps_tr[:, 0:rn, :]), s_te)
                r0 += rn
            P.wait("tensor", s_te)
            return s_te.count

        def build_head_dma(hh):
            P.wait("gpsimd", s_cc[hh // 8], 1)
            if hh >= 1:
                P.wait("gpsimd", s_memN, hh)
            hl = hh % 8
            P.dma("gpsimd", mstage[:, 0, :, :],
                  memh.rearrange("h (c p) d -> h c p d", p=128)[hh].rearrange("c p d -> p c d"),
                  s_mst)
            P.dma("gpsimd", dstage2[:, 0, :, 0:128],
                  d_red[hh // 8][:, :, 128 * hl:128 * (hl + 1)].rearrange("c p d -> p c d"),
                  s_mst)
            P.dma("gpsimd", dstage2[:, 0, :, 128:129],
                  d_red[hh // 8][:, :, 1024 + hl:1025 + hl].rearrange("c p d -> p c d"),
                  s_mst, slow=True)

        def build_head_v(hh):
            P.wait("vector", s_mst, 48 * (hh + 1))
            P.op("vector", lambda v, hh=hh: v.tensor_tensor(
                memNs[hh // 8][:, (hh // 4) % 2, :, 128 * (hh % 4):128 * (hh % 4 + 1)],
                mstage[:, 0, :, :], dstage2[:, 0, :, 0:128], ALU.add))
            P.op("vector", lambda v, hh=hh: v.tensor_tensor(
                normN[:, hh // 4, :, (hh % 4):(hh % 4) + 1],
                normF.ap()[:, 6 * hh:6 * (hh + 1)].rearrange("p (c o) -> p c o", o=1),
                dstage2[:, 0, :, 128:129], ALU.add), s_memN)

        P.dma("gpsimd", norm96[:], normh, s_const)
        P.wait("tensor", s_const)
        P.wait("tensor", s_konst)
        P.op("tensor", lambda t: t.transpose(
            ps_n.ap()[:, 0:96], norm96[0:96, :], ident_f[0:96, 0:96]), s_ntr)
        P.wait("scalar", s_ntr)
        P.op("scalar", lambda s: s.copy(normF[:], ps_n.ap()[:, 0:96]), s_ntr)

        wg_pre = [None, None]
        esA = contextlib.ExitStack()
        with esA:
            def sbA(name, shape, dt):
                return esA.enter_context(nc.sbuf_tensor(name, shape, dt))

            wkvmb = sbA("wkvmb", [128, NCH, 1536], BF16)
            wq_sb = sbA("wq_sb", [128, NCH, 1024], BF16)
            bias_kvmb = sbA("bias_kvmb", [1, 1536], BF16)
            bias_q = sbA("bias_q", [1, 1024], BF16)
            memA = sbA("memA", [128, 2, 6, 512], BF16)
            normA = sbA("normA", [128, 2, 6, 4], BF16)
            kv_bf = sbA("kv_bf", [128, 512], BF16)
            mb_bf = sbA("mb_bf", [128, 1024], BF16)
            q_bf = sbA("q_bf", [128, 1, 1024], BF16)
            xe2 = sbA("xe2", [128, 2, 262], BF16)
            F_sb = sbA("F_sb", [128, NT, 2, FD], BF16)
            fT = sbA("fT", [128, 12, 128], BF16)
            wmv = sbA("wmv", [128, NT, 8, 128], BF16)
            coefn = sbA("coefn", [128, NT, 8], BF16)
            dstage = sbA("dstage", [128, 1, 1032], F32)

            s_q = [P.sem("q_s0")]
            half_sems = {}

            for half in range(2):
                s_wA = P.sem(f"wA{half}")
                half_sems[half] = {}
                s_nA = P.sem(f"nA{half}")
                s_proj = P.sem(f"proj{half}")
                s_ev1 = P.sem(f"ev1_{half}")
                s_F = P.sem(f"F{half}")
                s_relu = P.sem(f"relu{half}")
                s_ssq = P.sem(f"ssq{half}")
                s_num = P.sem(f"num{half}")
                s_chain = P.sem(f"chain{half}")
                s_stats = P.sem(f"stats{half}")
                half_sems[half]["proj"] = s_proj
                half_sems[half]["num"] = s_num
                half_sems[half]["chain"] = s_chain

                if half == 1:
                    P.wait("gpsimd", half_sems[0]["proj"], 8)
                P.dma("gpsimd", wkvmb[:, :, 0:256],
                      wkT.rearrange("(c p) j -> p c j", p=128)[:, :, 256 * half:256 * (half + 1)], s_wA)
                P.dma("gpsimd", wkvmb[:, :, 256:512],
                      wvT.rearrange("(c p) j -> p c j", p=128)[:, :, 256 * half:256 * (half + 1)], s_wA)
                P.dma("gpsimd", wkvmb[:, :, 512:1536],
                      wmT.rearrange("(c p) j -> p c j", p=128)[:, :, 1024 * half:1024 * (half + 1)], s_wA)
                P.dma("gpsimd", wq_sb[:],
                      wqT.rearrange("(c p) j -> p c j", p=128)[:, :, 1024 * half:1024 * (half + 1)], s_wA)
                P.dma("gpsimd", bias_kvmb[:, 0:256], bk[:, 256 * half:256 * (half + 1)], s_wA)
                P.dma("gpsimd", bias_kvmb[:, 256:512], bv[:, 256 * half:256 * (half + 1)], s_wA)
                P.dma("gpsimd", bias_kvmb[:, 512:1536], bm[:, 1024 * half:1024 * (half + 1)], s_wA)
                P.dma("gpsimd", bias_q[:], bq[:, 1024 * half:1024 * (half + 1)], s_wA)
                if half == 1:
                    P.wait("gpsimd", half_sems[0]["num"], 32)
                for hl in range(8):
                    hh = 8 * half + hl
                    P.dma("gpsimd", memA[:, hl // 4, :, 128 * (hl % 4):128 * (hl % 4 + 1)],
                          memh.rearrange("h (c p) d -> h c p d", p=128)[hh].rearrange("c p d -> p c d"),
                          s_wA)
                if half == 1:
                    for hh in range(0, 8):
                        build_head_dma(hh)
                P.wait("vector", s_ntr, 2)
                if half == 1:
                    P.wait("vector", half_sems[0]["num"], 32)
                P.op("vector", lambda v, half=half: v.tensor_copy(
                    normA[:],
                    normF.ap()[:, 48 * half:48 * (half + 1)]
                    .rearrange("p (j hl c) -> p j c hl", j=2, hl=4, c=6)), s_nA)

                for t in range(NT):
                    gt = 8 * half + t
                    slot = t % 2
                    if half == 0:
                        if gt >= 2:
                            P.wait("sync", s_hs, gt - 1)
                        P.dma("sync", x_sb[:, slot, :], xt[t], s_x[slot])
                        if debug and gt == 1:
                            P.wait("vector", s_out, s_out.count)
                            P.wait("scalar", s_out, s_out.count)
                        P.wait("vector", s_x[slot], 16 * (gt // 2 + 1))
                        ln_stats(x_sb.ap()[:, slot, :], msb[:, 2 * t:2 * t + 1],
                                 msb[:, 2 * t + 1:2 * t + 2])
                        if debug and gt == 0:
                            P.bar("vector")
                            P.op("vector", lambda v: v.tensor_copy(lsnap[:], lns[:, 0:28]), s_lsn)
                        P.op("vector", lambda v: v.nop(), s_stats)
                        P.wait("scalar", s_x[slot], 16 * (gt // 2 + 1))
                        P.wait("scalar", s_lnd, s_lnd.count)
                        if gt >= 1:
                            P.wait("scalar", s_te, s_te.count)
                        P.op("scalar", lambda s, slot=slot, t=t: s.activation(
                            hsb_sb[:], x_sb[:, slot, :], AF.Identity,
                            bias=msb[:, 2 * t + 1:2 * t + 2], scale=msb[:, 2 * t:2 * t + 1]), s_hs)
                        if debug and gt == 0:
                            P.wait("sync", s_hs, 1)
                            P.dma("sync", dbg["hs0"], hsb_sb[:], s_out)
                            P.dma("sync", dbg["msb"], msb[:, 0:16], s_out)
                            P.dma("sync", dbg["lns0"], lns[:, 0:28], s_out)
                            P.dma("sync", dbg["bn0"], bn24[:, 0:24], s_out)
                            P.dma("sync", dbg["x0"], x_sb[:, 0, :], s_out)
                            P.wait("sync", s_lsn, 1)
                            P.dma("sync", dbg["lsnap"], lsnap[:], s_out)
                        P.wait("tensor", s_hs, gt + 1)
                        htswaits = []
                        if t >= 2:
                            htswaits = [(s_proj, t - 1), (s_htw[slot], 16 * (t // 2))]
                        transposes([hsb_sb.ap()[:, 128 * c:128 * (c + 1)] for c in range(NCH)],
                                   hsT.ap()[:, slot], swaits=htswaits)
                        P.wait("sync", s_te, s_te.count)
                        P.dma("sync", ht_hbm[t], hsT[:, slot], s_htw[slot])
                    else:
                        if t < 2:
                            P.wait("sync", s_htw[slot])
                        else:
                            P.wait("sync", s_proj, t - 1)
                        P.dma("sync", hsT[:, slot], ht_hbm[t], s_ht[slot])
                        if t == 0:
                            P.wait("tensor", half_sems[0]["dev"], 6)
                        P.wait("tensor", s_ht[slot], 16 * (t // 2 + 1))
                    P.wait("tensor", s_wA)
                    if t > 0:
                        P.wait("tensor", s_ev1, 2 * t)
                    wins = [(0, 512, 0), (512, 1024, 0), (1024, 1536, 0),
                            (1536, 2048, 1), (2048, 2560, 1)]
                    for c in range(NCH + 1):
                        last = c == NCH
                        lhs = ones_r[:] if last else hsT[:, slot, c, :]
                        for wi, (a, b, isq) in enumerate(wins):
                            if last:
                                rhs = bias_q[:, a - 1536:b - 1536] if isq else bias_kvmb[:, a:b]
                            else:
                                rhs = wq_sb[:, c, a - 1536:b - 1536] if isq else wkvmb[:, c, a:b]
                            P.op("tensor", lambda t_, lhs=lhs, rhs=rhs, a=a, b=b, c=c, last=last:
                                 t_.matmul(ps_a[:, a:b], lhs, rhs, start=(c == 0), stop=last),
                                 s_proj if (last and wi == 4) else None)
                    P.wait("scalar", s_proj, t + 1)
                    if t > 0:
                        P.wait("scalar", s_chain, 2 * t)
                    if half == 1 and t == 0:
                        P.wait("scalar", s_endA1)
                        P.wait("scalar", half_sems[0]["dwr"][0], 96)
                    P.op("scalar", lambda s: s.copy(kv_bf[:], ps_a[:, 0:512]))
                    P.op("scalar", lambda s: s.activation(mb_bf[:], ps_a[:, 512:1536], AF.Sigmoid), s_ev1)
                    P.wait("scalar", s_q[0], 16 * gt)
                    P.op("scalar", lambda s: s.copy(q_bf[:, 0, :], ps_a[:, 1536:2560]), s_ev1)
                    P.wait("sync", s_ev1, 2 * (t + 1))
                    P.dma("sync", q_hbm[t, :, 1024 * half:1024 * (half + 1)], q_bf[:, 0, :], s_q[0])
                    if debug and gt == 0:
                        P.dma("sync", dbg["kv0"], kv_bf[:], s_out)
                        P.dma("sync", dbg["mb0"], mb_bf[:], s_out)
                        P.dma("sync", dbg["q0"], q_bf[:, 0, :], s_out)
                    if t > 0:
                        P.wait("scalar", s_F, t)
                    P.wait("scalar", s_ev1, 2 * (t + 1))
                    kview = kv_bf.ap().rearrange("p (j d) -> p j d", j=4)
                    P.op("scalar", lambda s, kview=kview: s.activation(
                        xe2[:, :, 3:131], kview[:, 0:2, :], AF.Relu))
                    P.op("scalar", lambda s, kview=kview: s.activation(
                        xe2[:, :, 131:259], kview[:, 0:2, :], AF.Relu, scale=-1.0), s_relu)
                    P.wait("vector", s_relu, t + 1)
                    P.op("vector", lambda v: v.tensor_copy(xe2[:, :, 0:3], xe2[:, :, 256:259]))
                    P.bar("vector")
                    if half == 1 and t == 0:
                        P.wait("vector", s_endA1)
                    for j in (1, 2, 3):
                        P.op("vector", lambda v, j=j, t=t: v.tensor_tensor(
                            F_sb[:, t, :, 256 * (j - 1):256 * j], xe2[:, :, 3:259],
                            xe2[:, :, 3 - j:259 - j], ALU.mult), s_F if j == 3 else None)
                    if debug and gt == 0:
                        P.wait("sync", s_F, 1)
                        P.dma("sync", dbg["F0"], F_sb[:, 0, :, :], s_out)
                    P.wait("scalar", s_F, t + 1)
                    if t > 0:
                        P.wait("scalar", s_chain, 2 * t)
                    for j in range(2):
                        P.op("scalar", lambda s, j=j, t=t: s.activation(
                            sq_scr[:, j, :], F_sb[:, t, j, :], AF.Square, accum_out=sstat[:, j:j + 1]))
                    P.bar("scalar")
                    P.op("scalar", lambda s: s.activation(sstat[:, 2:4], sstat[:, 0:2], AF.Sqrt), s_ssq)
                    P.wait("vector", s_ssq, 2 * t + 1)
                    P.op("vector", lambda v: v.tensor_scalar(
                        sstat[:, 22:24], sstat[:, 2:4], EPS, None, ALU.mult))
                    P.op("vector", lambda v: v.reciprocal(sstat[:, 4:6], sstat[:, 2:4]), s_ssq)
                    P.bar("vector")
                    P.wait("tensor", s_F, t + 1)
                    fswaits = [(s_num, 4 * t)] if t > 0 else []
                    transposes([F_sb.ap()[:, t, j, 128 * c:128 * (c + 1)]
                                for j in range(2) for c in range(6)], fT, swaits=fswaits)
                    P.wait("tensor", s_nA)
                    if half == 0 and t == 0:
                        P.wait("tensor", s_ntr, 2)
                    for j in range(2):
                        if t > 0 or j > 0:
                            P.wait("tensor", s_chain, 2 * t + j)
                        for c in range(6):
                            P.op("tensor", lambda t_, j=j, c=c: t_.matmul(
                                ps_n[:], fT[:, 6 * j + c, :], memA[:, j, c, :],
                                start=(c == 0), stop=(c == 5)), s_num if c == 5 else None)
                        for c in range(6):
                            P.op("tensor", lambda t_, j=j, c=c: t_.matmul(
                                ps_dot[:, 4 * j:4 * j + 4], fT[:, 6 * j + c, :], normA[:, j, c, :],
                                start=(c == 0), stop=(c == 5)), s_num if c == 5 else None)
                        P.wait("vector", s_num, 4 * t + 2 * (j + 1))
                        P.wait("vector", s_ssq, 2 * (t + 1))
                        invn = sstat.ap()[:, 4 + j:5 + j]
                        epsn = sstat.ap()[:, 22 + j:23 + j]
                        den = sstat.ap()[:, 6:10]
                        P.op("vector", lambda v, j=j, epsn=epsn, den=den: v.tensor_scalar(
                            den, ps_dot[:, 4 * j:4 * j + 4], 1.0, epsn, ALU.mult, ALU.add))
                        P.bar("vector")
                        P.op("vector", lambda v, den=den: v.reciprocal(sstat[:, 10:14], den))
                        P.bar("vector")
                        P.op("vector", lambda v: v.tensor_scalar(
                            sstat[:, 14:18], sstat[:, 10:14], -1.0, None, ALU.mult))
                        P.bar("vector")
                        def stt_mv(hl, j=j, t=t):
                            P.op("vector", lambda v, hl=hl, j=j: v.scalar_tensor_tensor(
                                mv_scr[:, hl, :], ps_n[:, 128 * hl:128 * (hl + 1)],
                                sstat[:, 14 + hl:15 + hl],
                                kv_bf[:, 256 + 128 * j:384 + 128 * j], ALU.mult, ALU.add))
                        def stt_wmv(hl, j=j, t=t, invn=invn):
                            h8 = 4 * j + hl
                            P.op("vector", lambda v, hl=hl, h8=h8, t=t, invn=invn:
                                 v.scalar_tensor_tensor(
                                wmv[:, t, h8, :], mv_scr[:, hl, :], invn,
                                mb_bf[:, 128 * h8:128 * (h8 + 1)], ALU.mult, ALU.mult))
                        stt_mv(0)
                        stt_mv(1)
                        stt_mv(2)
                        stt_mv(3)
                        P.op("vector", lambda v, den=den, invn=invn: v.tensor_scalar(
                            sstat[:, 18:22], den, invn, None, ALU.mult))
                        P.bar("vector")
                        stt_wmv(0)
                        stt_wmv(1)
                        stt_wmv(2)
                        stt_wmv(3)
                        P.op("vector", lambda v: v.tensor_scalar(
                            sstat[:, 14:18], sstat[:, 18:22], -1.0, 1.0, ALU.mult, ALU.add))
                        P.bar("vector")
                        P.op("vector", lambda v: v.tensor_scalar(
                            sstat[:, 18:22], sstat[:, 14:18], 0.0, 1.0, ALU.max, ALU.min))
                        P.bar("vector")
                        P.op("vector", lambda v, t=t, j=j, invn=invn: v.tensor_scalar(
                            coefn[:, t, 4 * j:4 * j + 4], sstat[:, 18:22], invn,
                            None, ALU.mult), s_chain)
                    if debug and gt == 0:
                        P.wait("sync", s_chain, 2)
                        P.dma("sync", dbg["nst0"], sstat[:, 0:24], s_out)
                        P.dma("sync", dbg["wmv0"], wmv[:, 0, :, :], s_out)
                        P.dma("sync", dbg["coefn0"], coefn[:, 0, :], s_out)
                    if half == 1 and t == 3:
                        for hh in range(0, 8):
                            build_head_v(hh)

                P.wait("tensor", s_chain)
                P.wait("tensor", s_ev1)
                s_dmm = P.sem(f"dmm{half}")
                s_dev = P.sem(f"dev{half}")
                s_dwr = [P.sem(f"dwr{half}_0"), P.sem(f"dwr{half}_1")]
                half_sems[half]["dev"] = s_dev
                half_sems[half]["dwr"] = s_dwr
                for c in range(6):
                    reg = (c % 2) * 1024
                    if c >= 1:
                        P.wait("tensor", s_dev, c)
                    for tt in range(NT):
                        for j in range(2):
                            P.op("tensor", lambda t_, c=c, tt=tt, j=j, reg=reg: t_.matmul(
                                ps_a[:, reg + 512 * j:reg + 512 * (j + 1)],
                                F_sb[:, tt, j, 128 * c:128 * (c + 1)],
                                wmv[:, tt, 4 * j:4 * j + 4, :],
                                start=(tt == 0), stop=(tt == NT - 1)), None)
                            dps = ps_dot[:, 0:4] if j == 0 else ps_n.ap()[:, 0:4]
                            P.op("tensor", lambda t_, c=c, tt=tt, j=j, dps=dps: t_.matmul(
                                dps, F_sb[:, tt, j, 128 * c:128 * (c + 1)],
                                coefn[:, tt, 4 * j:4 * j + 4],
                                start=(tt == 0), stop=(tt == NT - 1)),
                                s_dmm if (tt == NT - 1 and j == 1) else None)
                    P.wait("scalar", s_dmm, c + 1)
                    if c >= 1:
                        P.wait("scalar", s_dwr[0], 16 * c)
                    P.op("scalar", lambda s, c=c, reg=reg: s.copy(
                        dstage[:, 0, 0:1024], ps_a[:, reg:reg + 1024]))
                    P.op("scalar", lambda s, c=c: s.copy(dstage[:, 0, 1024:1028], ps_dot[:, 0:4]))
                    P.op("scalar", lambda s, c=c: s.copy(dstage[:, 0, 1028:1032], ps_n.ap()[:, 0:4]), s_dev)
                    P.wait("sync", s_dev, c + 1)
                    P.dma("sync", d_loc[half][c], dstage[:, 0, :], s_dwr[0])
                if half == 1:
                    P.wait("gpsimd", s_proj, 8)
                    s_wg.bump(16)
                    P.emit("gpsimd", lambda e: e.dma_start(
                        out=wg_pre[0][:], in_=wgT.rearrange("(c p) j -> p c j", p=128)
                    ).then_inc(s_wg.h, 16))
                    s_wg.bump(16)
                    P.emit("gpsimd", lambda e: e.dma_start(
                        out=wg_pre[1][:], in_=bg).then_inc(s_wg.h, 16))
                P.wait("gpsimd", s_dwr[0])
                P.wait("gpsimd", s_dwr[1])
                P.op("gpsimd", lambda g, half=half: g.collective_compute(
                    "AllReduce", ALU.add, replica_groups=[list(range(NCORE))],
                    ins=[d_loc[half]], outs=[d_red[half]]), s_cc[half])
                P.op("tensor", lambda t: t.nop(), s_endA1 if half == 0 else s_endA2)
                if half == 1:
                    P.wait("sync", s_q[0])
                    P.op("sync", lambda e: e.nop(), s_endA2)

        esB = contextlib.ExitStack()
        with esB:
            def sbB(name, shape, dt):
                return esB.enter_context(nc.sbuf_tensor(name, shape, dt))

            wg_sb = sbB("wg_sb", [128, NCH, HID], BF16)
            bias_g = sbB("bias_g", [1, HID], BF16)
            wg_pre[0] = wg_sb
            wg_pre[1] = bias_g
            memN_b = sbB("memN_b", [128, 2, 6, 512], BF16)
            memNs[1] = memN_b
            q_in = sbB("q_in", [128, 2, HID], BF16)
            xe4 = sbB("xe4", [128, 2, 4, 262], BF16)
            Fq = sbB("Fq", [128, 2, 4, FD], BF16)
            fTq = sbB("fTq", [128, 2, 24, 128], BF16)
            gate_sb = sbB("gate_sb", [128, HID], F32)
            hs_sb = sbB("hs_sb", [128, HID], F32)
            mhg = sbB("mhg", [128, HID], F32)
            ln2_sb = mhg
            y_sb = sbB("y_sb", [128, HID], F32)

            s_hs2 = P.sem("hs2")
            s_gproj = P.sem("gproj")
            s_gate = P.sem("gate")
            s_qrd = [P.sem("qrd0"), P.sem("qrd1")]
            s_relu2 = P.sem("relu2")
            s_Fq = P.sem("Fq")
            s_ssq2 = P.sem("ssq2")
            s_numa = P.sem("numa")
            s_mhg = P.sem("mhg")
            s_attn = P.sem("attn")
            s_ln2 = P.sem("ln2")
            s_yv = P.sem("yv")
            s_ywr = [P.sem("ywr0"), P.sem("ywr1")]

            for hh in range(8, 16):
                build_head_dma(hh)

            fq_te = {}
            for t in range(NT):
                slot = t % 2
                if t < 2:
                    P.wait("sync", s_hs, 8)
                else:
                    P.wait("sync", s_yv, t - 1)
                P.dma("sync", x_sb[:, slot, :], xt[t], s_x[slot])
                if t >= 2:
                    P.wait("sync", s_relu2, 4 * t - 4)
                if t == 0:
                    P.wait("sync", s_endA2)
                P.dma("sync", q_in[:, slot, :], q_hbm[t], s_qrd[slot])
                if t >= 2:
                    P.wait("sync", s_gproj, t - 1)
                P.dma("sync", hsT[:, slot], ht_hbm[t], s_ht[slot])
                if debug and t in (1, 2):
                    P.wait("scalar", s_out, s_out.count)
                    P.wait("vector", s_out, s_out.count)
                if t == 0:
                    P.wait("scalar", s_endA2)
                P.wait("tensor", s_wg)
                P.wait("tensor", s_ht[slot], 16 * (4 + t // 2 + 1))
                if t == 0:
                    P.wait("tensor", half_sems[1]["dev"], 6)
                    P.wait("tensor", s_endA2)
                if t > 0:
                    P.wait("tensor", s_gate, t)
                for c in range(NCH + 1):
                    last = c == NCH
                    lhs = ones_r[:] if last else hsT[:, slot, c, :]
                    for wi in range(4):
                        rhs = (bias_g[:, 512 * wi:512 * (wi + 1)] if last
                               else wg_sb[:, c, 512 * wi:512 * (wi + 1)])
                        P.op("tensor", lambda t_, lhs=lhs, rhs=rhs, wi=wi, c=c, last=last:
                             t_.matmul(ps_a[:, 512 * wi:512 * (wi + 1)], lhs, rhs,
                                       start=(c == 0), stop=last),
                             s_gproj if (last and wi == 3) else None)
                P.wait("scalar", s_gproj, t + 1)
                if t > 0:
                    P.wait("scalar", s_mhg, 4 * t)
                P.op("scalar", lambda s: s.activation(gate_sb[:], ps_a[:, 0:2048], AF.Sigmoid), s_gate)
                if debug and t == 0:
                    P.wait("sync", s_gate, 1)
                    P.dma("sync", dbg["gate0"], gate_sb[:], s_out)
                for g in range(4):
                    gi = 4 * t + g
                    par = g % 2
                    if t == 0 and g == 2:
                        for hh in range(8, 12):
                            build_head_v(hh)
                    if t == 0 and g == 3:
                        for hh in range(12, 16):
                            build_head_v(hh)
                    P.wait("scalar", s_qrd[slot], 16 * (t // 2 + 1))
                    if gi > 1:
                        P.wait("scalar", s_Fq, gi - 1)
                    qv = q_in.ap()[:, slot, :].rearrange("p (g4 d) -> p g4 d", d=128)
                    P.op("scalar", lambda s, g=g, par=par, qv=qv: s.activation(
                        xe4[:, par, :, 3:131], qv[:, 4 * g:4 * g + 4, :], AF.Relu))
                    P.op("scalar", lambda s, g=g, par=par, qv=qv: s.activation(
                        xe4[:, par, :, 131:259], qv[:, 4 * g:4 * g + 4, :], AF.Relu, scale=-1.0), s_relu2)
                    P.wait("vector", s_relu2, gi + 1)
                    P.op("vector", lambda v, par=par: v.tensor_copy(
                        xe4[:, par, :, 0:3], xe4[:, par, :, 256:259]))
                    P.bar("vector")
                    if gi > 1:
                        P.wait("vector", s_te, fq_te.get(par, 0))
                        P.wait("vector", s_ssq2, gi - 1)
                    for j in (1, 2, 3):
                        P.op("vector", lambda v, j=j, par=par: v.tensor_tensor(
                            Fq[:, par, :, 256 * (j - 1):256 * j], xe4[:, par, :, 3:259],
                            xe4[:, par, :, 3 - j:259 - j], ALU.mult), s_Fq if j == 3 else None)
                    P.wait("scalar", s_Fq, gi + 1)
                    if gi > 1:
                        P.wait("scalar", s_mhg, gi - 1)
                    sreg = 8 * par
                    for hl in range(4):
                        P.op("scalar", lambda s, hl=hl, sreg=sreg, par=par: s.activation(
                            sq_scr[:, hl % 2, :], Fq[:, par, hl, :], AF.Square,
                            accum_out=sstat[:, sreg + hl:sreg + hl + 1]))
                        if hl == 1:
                            P.bar("scalar")
                    P.bar("scalar")
                    P.op("scalar", lambda s, sreg=sreg: s.activation(
                        sstat[:, sreg + 4:sreg + 8], sstat[:, sreg:sreg + 4], AF.Sqrt), s_ssq2)
                    P.wait("tensor", s_Fq, gi + 1)
                    fqswaits = [(s_numa, gi - 1)] if gi > 1 else []
                    transposes([Fq.ap()[:, par, hl, 128 * c:128 * (c + 1)]
                                for hl in range(4) for c in range(6)],
                               fTq.ap()[:, par], swaits=fqswaits)
                    fq_te[par] = s_te.count
                    if gi == 0:
                        P.wait("tensor", half_sems[1]["dev"], 6)
                    if gi > 1:
                        P.wait("tensor", s_mhg, gi - 1)
                    P.wait("tensor", s_memN, 4 * (g + 1))
                    memg = memNs[g // 2]
                    nbank = (ps_n.ap() if par == 0 else ps_a.ap()[:, 2048:2560])
                    for hl in range(4):
                        for c in range(6):
                            P.op("tensor", lambda t_, g=g, c=c, hl=hl, memg=memg, nb=nbank, par=par: t_.matmul(
                                nb[:, 128 * hl:128 * (hl + 1)], fTq[:, par, 6 * hl + c, :],
                                memg[:, g % 2, c, 128 * hl:128 * (hl + 1)],
                                start=(c == 0), stop=(c == 5)), None)
                    if gi > 0:
                        P.wait("tensor", s_mhg, gi)
                    for hl in range(4):
                        for c in range(6):
                            P.op("tensor", lambda t_, g=g, c=c, hl=hl, par=par: t_.matmul(
                                ps_dot[:, 4 * par + hl:4 * par + hl + 1], fTq[:, par, 6 * hl + c, :],
                                normN[:, g, c, hl:hl + 1],
                                start=(c == 0), stop=(c == 5)),
                                s_numa if (hl == 3 and c == 5) else None)
                    P.wait("vector", s_numa, gi + 1)
                    P.wait("vector", s_ssq2, gi + 1)
                    P.wait("vector", s_gate, t + 1)
                    n4 = sstat.ap()[:, sreg + 4:sreg + 8]
                    den = sstat.ap()[:, 16:20]
                    P.op("vector", lambda v, n4=n4, den=den, par=par: v.scalar_tensor_tensor(
                        den, n4, EPS, ps_dot[:, 4 * par:4 * par + 4], ALU.mult, ALU.add))
                    P.bar("vector")
                    P.op("vector", lambda v, den=den: v.reciprocal(sstat[:, 20:24], den))
                    P.bar("vector")
                    for hl in range(4):
                        hcol = 128 * (4 * g + hl)
                        P.op("vector", lambda v, hl=hl, hcol=hcol, nb=nbank: v.scalar_tensor_tensor(
                            mhg[:, hcol:hcol + 128], nb[:, 128 * hl:128 * (hl + 1)],
                            sstat[:, 20 + hl:21 + hl], gate_sb[:, hcol:hcol + 128],
                            ALU.mult, ALU.mult), s_mhg if hl == 3 else None)
                P.wait("scalar", s_x[slot], 16 * (4 + t // 2 + 1))
                if t >= 1:
                    P.wait("scalar", s_attn, t)
                P.op("scalar", lambda s, slot=slot, t=t: s.activation(
                    hs_sb[:], x_sb[:, slot, :], AF.Identity,
                    bias=msb[:, 2 * t + 1:2 * t + 2], scale=msb[:, 2 * t:2 * t + 1]), s_hs2)
                if debug and t == 0:
                    P.wait("vector", s_out, s_out.count)
                if t > 0:
                    P.wait("vector", s_ln2, t)
                P.wait("vector", s_hs2, t + 1)
                P.bar("vector")
                P.op("vector", lambda v: v.tensor_tensor(gate_sb[:], mhg[:], hs_sb[:], ALU.add), s_attn)
                P.bar("vector")
                if debug and t == 0:
                    P.wait("sync", s_attn, 1)
                    P.dma("sync", dbg["mhg0"], gate_sb[:], s_out)
                ln_stats(gate_sb.ap(), lns[:, 5:6], lns[:, 6:7])
                P.wait("scalar", s_lnd)
                P.op("scalar", lambda s: s.activation(
                    ln2_sb[:], gate_sb[:], AF.Identity, bias=lns[:, 6:7], scale=lns[:, 5:6]), s_ln2)
                P.wait("vector", s_ln2, t + 1)
                P.wait("vector", s_ywr[0], 16 * t)
                P.op("vector", lambda v, slot=slot: v.tensor_tensor(
                    y_sb[:], ln2_sb[:], x_sb[:, slot, :], ALU.add), s_yv)
                P.wait("sync", s_yv, t + 1)
                P.dma("sync", y_hbm[t], y_sb[:], s_ywr[0])
                if debug and t == 0:
                    P.dma("sync", dbg["y0"], y_sb[:], s_out)
            if debug:
                P.wait("sync", s_memN, 16)
                P.dma("sync", dbg["memN"].rearrange("p g c x -> p (g c x)")
                      .rearrange("p (g2 r) -> p g2 r", g2=2)[:, 0, :], memN_a.ap()
                      .rearrange("p g c x -> p (g c x)"), s_out)
                P.dma("sync", dbg["memN"].rearrange("p g c x -> p (g c x)")
                      .rearrange("p (g2 r) -> p g2 r", g2=2)[:, 1, :], memN_b.ap()
                      .rearrange("p g c x -> p (g c x)"), s_out)
                P.dma("sync", dbg["normN"], normN[:], s_out)
            P.op("tensor", lambda t: t.nop(), s_endB1)
            P.op("vector", lambda v: v.nop(), s_endB1)
            P.op("scalar", lambda s: s.nop(), s_endB1)
            P.wait("sync", s_ywr[0])
            P.wait("sync", s_qrd[0])
            P.wait("sync", s_qrd[1])
            if debug:
                P.wait("sync", s_out, s_out.count)
            P.op("sync", lambda e: e.nop(), s_endB1)

        x_es.close()
        esC = contextlib.ExitStack()
        with esC:
            def sbC(name, shape, dt):
                return esC.enter_context(nc.sbuf_tensor(name, shape, dt))

            sT = sbC("sT", [128, NCH, HID], BF16)
            wf_st = sbC("wf_st", [128, 2, HID], F32)
            absw = sbC("absw", [128, HID], BF16)
            alpha_row = sbC("alpha_row", [1, HID], F32)
            inva = sbC("inva", [1, HID], F32)
            biasf_sb = sbC("biasf_sb", [1, HID], F32)
            bias_pre = sbC("bias_pre", [1, HID], BF16)
            alpha_bc = sbC("alpha_bc", [128, HID], F32)
            y_in = sbC("y_in", [128, 2, HID], F32)
            hs2b = sbC("hs2b", [128, HID], BF16)
            z_sb = sbC("z_sb", [128, HID], F32)
            ln4_sb = sbC("ln4_sb", [128, HID], F32)
            out_sb = sbC("out_sb", [128, HID], F32)

            s_wf = [P.sem("wf0"), P.sem("wf1")]
            s_bfl = P.sem("bfl")
            s_sign = P.sem("sign")
            s_amm = P.sem("amm")
            s_aev = P.sem("aev")
            s_bc = P.sem("bc")
            s_yrd = [P.sem("yrd0"), P.sem("yrd1")]
            s_hs3 = P.sem("hs3")
            s_fmm = P.sem("fmm")
            s_z = P.sem("z")
            s_ln4 = P.sem("ln4")
            s_outv = P.sem("outv")
            s_od = [P.sem("od0"), P.sem("od1")]

            P.wait("gpsimd", s_endB1)
            P.dma("gpsimd", biasf_sb[:], bfh, s_bfl)
            for c in range(NCH):
                slot = c % 2
                if c >= 2:
                    P.wait("gpsimd", s_sign, c - 1)
                P.dma("gpsimd", wf_st[:, slot, :],
                      wfT.rearrange("(c p) j -> p c j", p=128)[:, c, :], s_wf[slot])
                P.wait("scalar", s_wf[slot], 16 * (c // 2 + 1))
                if c == 0:
                    P.wait("scalar", s_endB1, 4)
                if c >= 1:
                    P.wait("scalar", s_amm, c)
                P.op("scalar", lambda s, c=c, slot=slot: s.activation(
                    sT[:, c, :], wf_st[:, slot, :], AF.Sign))
                P.op("scalar", lambda s, c=c, slot=slot: s.activation(
                    absw[:], wf_st[:, slot, :], AF.Abs), s_sign)
                P.wait("tensor", s_sign, c + 1)
                if c == 0:
                    P.wait("tensor", s_endB1, 4)
                    P.wait("tensor", s_z, 0)
                for wi in range(4):
                    P.op("tensor", lambda t_, wi=wi, c=c: t_.matmul(
                        ps_a.ap()[0:1, 512 * wi:512 * (wi + 1)], ones_c[:],
                        absw[:, 512 * wi:512 * (wi + 1)],
                        start=(c == 0), stop=(c == NCH - 1)),
                        s_amm if wi == 3 else None)
            P.wait("scalar", s_amm, NCH)
            P.op("scalar", lambda s: s.activation(
                alpha_row[:], ps_a.ap()[0:1, 0:2048], AF.Identity, scale=1.0 / HID), s_aev)
            P.wait("vector", s_aev)
            P.wait("vector", s_endB1, 4)
            P.op("vector", lambda v: v.reciprocal(inva[:], alpha_row[:]))
            P.wait("vector", s_bfl, 16)
            P.bar("vector")
            P.op("vector", lambda v: v.tensor_tensor(bias_pre[:], biasf_sb[:], inva[:], ALU.mult), s_aev)
            P.wait("tensor", s_aev, 2)
            for wi in range(4):
                P.op("tensor", lambda t_, wi=wi: t_.matmul(
                    ps_a[:, 512 * wi:512 * (wi + 1)], ones_rf[:],
                    alpha_row[:, 512 * wi:512 * (wi + 1)], start=True, stop=True),
                    s_bc if wi == 3 else None)
            P.wait("scalar", s_bc)
            P.op("scalar", lambda s: s.copy(alpha_bc[:], ps_a[:, 0:2048]), s_bc)
            if debug:
                P.wait("sync", s_aev, 2)
                P.dma("sync", dbg["alpha"], alpha_row[:], s_out)

            def b2_yload(t):
                slot = t % 2
                if t == 0:
                    P.wait("sync", s_endB1, 4)
                if t >= 2:
                    P.wait("sync", s_outv, t - 1)
                if t == 0:
                    P.wait("sync", s_ywr[0], 16 * 8)
                P.dma("sync", y_in[:, slot, :], y_hbm[t], s_yrd[slot])

            def b2_ln3(t):
                slot = t % 2
                if debug and t == 1:
                    P.wait("vector", s_out, s_out.count)
                P.wait("vector", s_yrd[slot], 16 * (t // 2 + 1))
                ln_stats(y_in.ap()[:, slot, :], lns[:, 5:6], lns[:, 6:7])
                P.wait("scalar", s_lnd)
                if t >= 1:
                    P.wait("scalar", s_te, s_te.count)
                P.op("scalar", lambda s, slot=slot: s.activation(
                    hs2b[:], y_in[:, slot, :], AF.Identity,
                    bias=lns[:, 6:7], scale=lns[:, 5:6]), s_hs3)

            b2_yload(0)
            b2_ln3(0)
            for t in range(NT):
                slot = t % 2
                P.wait("tensor", s_hs3, t + 1)
                if t == 0:
                    P.wait("tensor", s_bc, 2)
                b2swaits = [(s_fmm, t - 1)] if t >= 2 else []
                transposes([hs2b.ap()[:, 128 * c:128 * (c + 1)] for c in range(NCH)],
                           hsT.ap()[:, slot], swaits=b2swaits)
                if t > 0:
                    P.wait("tensor", s_z, t)
                for c in range(NCH + 1):
                    last = c == NCH
                    lhs = ones_r[:] if last else hsT[:, slot, c, :]
                    for wi in range(4):
                        rhs = (bias_pre[:, 512 * wi:512 * (wi + 1)] if last
                               else sT[:, c, 512 * wi:512 * (wi + 1)])
                        P.op("tensor", lambda t_, lhs=lhs, rhs=rhs, wi=wi, c=c, last=last:
                             t_.matmul(ps_a[:, 512 * wi:512 * (wi + 1)], lhs, rhs,
                                       start=(c == 0), stop=last),
                             s_fmm if (last and wi == 3) else None)
                if t + 1 < NT:
                    b2_yload(t + 1)
                P.wait("vector", s_fmm, t + 1)
                P.wait("vector", s_bc, 2)
                if t > 0:
                    P.wait("vector", s_ln4, t)
                P.op("vector", lambda v: v.tensor_tensor(z_sb[:], ps_a[:, 0:2048], alpha_bc[:], ALU.mult), s_z)
                if debug and t == 0:
                    P.wait("sync", s_z, 1)
                    P.dma("sync", dbg["z0"], z_sb[:], s_out)
                if t + 1 < NT:
                    b2_ln3(t + 1)
                P.wait("vector", s_z, t + 1)
                P.bar("vector")
                ln_stats(z_sb.ap(), lns[:, 16:17], lns[:, 17:18])
                P.wait("scalar", s_lnd)
                P.op("scalar", lambda s: s.activation(
                    ln4_sb[:], z_sb[:], AF.Identity, bias=lns[:, 17:18], scale=lns[:, 16:17]), s_ln4)
                P.wait("vector", s_ln4, t + 1)
                if t >= 1:
                    P.wait("vector", s_od[0], 16 * t)
                P.op("vector", lambda v, slot=slot: v.tensor_tensor(
                    out_sb[:], ln4_sb[:], y_in[:, slot, :], ALU.add), s_outv)
                P.wait("sync", s_outv, t + 1)
                P.dma("sync", outh.rearrange("(n p) d -> n p d", p=128)[t], out_sb[:], s_od[0])

            P.wait("gpsimd", s_od[0])
            if debug:
                P.wait("gpsimd", s_out)

        blk = es.enter_context(nc.Block())
        for s in P.sems:
            s.h = es.enter_context(nc.semaphore(s.name))

        @blk.tensor
        def _(e):
            for f in P.q["tensor"]:
                f(e)

        @blk.vector
        def _(e):
            for f in P.q["vector"]:
                f(e)

        @blk.scalar
        def _(e):
            for f in P.q["scalar"]:
                f(e)

        @blk.gpsimd
        def _(e):
            for f in P.q["gpsimd"]:
                f(e)

        @blk.sync
        def _(e):
            for f in P.q["sync"]:
                f(e)

    return nc, dbg


_NC_CACHE = {}


def _get_nc(debug=False):
    key = bool(debug)
    if key not in _NC_CACHE:
        _NC_CACHE[key] = build_nc(debug)
    return _NC_CACHE[key]


def make_in_maps(inputs):
    x = np.ascontiguousarray(np.asarray(inputs["L0_attn_hidden_states"], np.float32).reshape(8192, HID))
    com = {
        "WkT": np.ascontiguousarray(np.asarray(inputs["L0_attn_Wk"], np.float32).T),
        "WvT": np.ascontiguousarray(np.asarray(inputs["L0_attn_Wv"], np.float32).T),
        "WmT": np.ascontiguousarray(np.asarray(inputs["L0_attn_Wmb"], np.float32).T),
        "WqT": np.ascontiguousarray(np.asarray(inputs["L0_attn_Wq"], np.float32).T),
        "WgT": np.ascontiguousarray(np.asarray(inputs["L0_attn_Wg"], np.float32).T),
        "WfT": np.ascontiguousarray(np.asarray(inputs["L0_ffn_weight"], np.float32).T),
        "bk": np.asarray(inputs["L0_attn_bk"], np.float32).reshape(1, 512),
        "bv": np.asarray(inputs["L0_attn_bv"], np.float32).reshape(1, 512),
        "bm": np.asarray(inputs["L0_attn_bmb"], np.float32).reshape(1, HID),
        "bq": np.asarray(inputs["L0_attn_bq"], np.float32).reshape(1, HID),
        "bg": np.asarray(inputs["L0_attn_bg"], np.float32).reshape(1, HID),
        "bf": np.asarray(inputs["L0_ffn_bias"], np.float32).reshape(1, HID),
        "memory": np.ascontiguousarray(np.asarray(inputs["L0_attn_memory"], np.float32)),
        "norm96": np.ascontiguousarray(np.asarray(inputs["L0_attn_norm"], np.float32).reshape(96, 128)),
        "ident": np.eye(128, dtype=np.float32),
    }
    in_maps = []
    for c in range(NCORE):
        m = dict(com)
        m["x_part"] = np.ascontiguousarray(x[c * TOK:(c + 1) * TOK])
        in_maps.append(m)
    return in_maps


def kernel(**inputs):
    nc, _ = _get_nc(False)
    in_maps = make_in_maps(inputs)
    res = run_bass_kernel_spmd(nc, in_maps, core_ids=list(range(NCORE)))
    out = np.concatenate([r["out_part"] for r in res.results], axis=0)
    return out.reshape(2, 4096, HID)


# version 04
# speedup vs baseline: 1.0272x; 1.0272x over previous
import contextlib
import numpy as np
import concourse.bass as bass
import concourse.mybir as mybir
from concourse.bass_utils import run_bass_kernel_spmd

F32 = mybir.dt.float32
BF16 = mybir.dt.bfloat16
AF = mybir.ActivationFunctionType
ALU = mybir.AluOpType
AX = mybir.AxisListType

NCORE = 8
TOK = 1024
NT = 8
HID = 2048
NCH = 16
D = 128
H = 16
FD = 768
EPS = 1e-8
LNEPS = 1e-5


class Sem:
    __slots__ = ("name", "count", "h")

    def __init__(self, name):
        self.name = name
        self.count = 0
        self.h = None

    def bump(self, n=1):
        self.count += n
        return self.count


class Prog:
    ENGINES = ("tensor", "vector", "scalar", "gpsimd", "sync")

    def __init__(self):
        self.q = {e: [] for e in self.ENGINES}
        self.sems = []
        self.barsems = {}

    def sem(self, name):
        s = Sem(name)
        self.sems.append(s)
        return s

    def emit(self, eng, fn):
        self.q[eng].append(fn)

    def wait(self, eng, sem, val=None):
        v = sem.count if val is None else val
        if v <= 0:
            return
        self.emit(eng, lambda e, s=sem, vv=v: e.wait_ge(s.h, vv))

    def op(self, eng, builder, sem=None, inc=1):
        if sem is not None:
            sem.bump(inc)
        extra = []

        def f(e, b=builder, s=sem, n=inc, x=extra):
            inst = b(e)
            if s is not None:
                inst.then_inc(s.h, n)
            for (xs, xn) in x:
                inst.then_inc(xs.h, xn)

        self.emit(eng, f)
        self.lastop = getattr(self, 'lastop', {})
        self.lastop[eng] = (extra, sem, sem.count if sem is not None else 0)

    def nop(self, eng):
        self.emit(eng, lambda e: e.nop())

    def drain(self, eng):
        self.emit(eng, lambda e: e.drain())

    def bar(self, eng):
        extra, psem, pval = self.lastop[eng]
        if psem is not None:
            self.emit(eng, lambda e, ss=psem, v=pval: e.wait_ge(ss.h, v))
            return
        s = self.barsems.get(eng)
        if s is None:
            s = self.sem(f"bar_{eng}")
            self.barsems[eng] = s
        c = s.bump(1)
        extra.append((s, 1))
        self.emit(eng, lambda e, ss=s, v=c: e.wait_ge(ss.h, v))

    def dma(self, qeng, out_ap, in_ap, sem, slow=False):
        sem.bump(16)
        self.emit(qeng, lambda e, o=out_ap, i=in_ap, s=sem, sl=slow:
                  e.dma_start(out=o, in_=i, allow_slow_non_contiguous=sl).then_inc(s.h, 16))


def build_nc(debug=False):
    nc = bass.Bass("TRN2", target_bir_lowering=False)
    P = Prog()

    xh = nc.dram_tensor("x_part", [TOK, HID], F32, kind="ExternalInput").ap()
    waH = [nc.dram_tensor(f"Wa{h}", [128, NCH, 2560], F32, kind="ExternalInput").ap()
           for h in range(2)]
    wgP = nc.dram_tensor("WgP", [128, NCH, HID], F32, kind="ExternalInput").ap()
    memP4h = nc.dram_tensor("memP4", [4, 128, 6, 512], F32, kind="ExternalInput").ap()
    memPh = nc.dram_tensor("memPh", [16, 128, 6, 128], F32, kind="ExternalInput").ap()
    wfT = nc.dram_tensor("WfT", [HID, HID], F32, kind="ExternalInput").ap()
    bk = nc.dram_tensor("bk", [1, 512], F32, kind="ExternalInput").ap()
    bv = nc.dram_tensor("bv", [1, 512], F32, kind="ExternalInput").ap()
    bm = nc.dram_tensor("bm", [1, HID], F32, kind="ExternalInput").ap()
    bq = nc.dram_tensor("bq", [1, HID], F32, kind="ExternalInput").ap()
    bg = nc.dram_tensor("bg", [1, HID], F32, kind="ExternalInput").ap()
    bfh = nc.dram_tensor("bf", [1, HID], F32, kind="ExternalInput").ap()
    normh = nc.dram_tensor("norm96", [96, 128], F32, kind="ExternalInput").ap()
    identh = nc.dram_tensor("ident", [128, 128], F32, kind="ExternalInput").ap()
    outh = nc.dram_tensor("out_part", [TOK, HID], F32, kind="ExternalOutput").ap()

    q_hbm = nc.dram_tensor("q_scr", [NT, 128, HID], BF16).ap()
    ht_hbm = nc.dram_tensor("ht_scr", [NT, 128, NCH * 128], BF16).ap()
    y_hbm = nc.dram_tensor("y_scr", [NT, 128, HID], F32).ap()
    d_loc = [nc.dram_tensor(f"d_loc{i}", [6, 128, 1032], F32).ap() for i in range(2)]
    d_red = [nc.dram_tensor(f"d_red{i}", [6, 128, 1032], F32, addr_space="Shared").ap() for i in range(2)]

    dbg = {}
    if debug:
        def dbgt(name, shape, dt=F32):
            dbg[name] = nc.dram_tensor("dbg_" + name, shape, dt, kind="ExternalOutput").ap()
        dbgt("hs0", [128, HID], BF16)
        dbgt("msb", [128, 16])
        dbgt("lns0", [128, 28])
        dbgt("bn0", [128, 24])
        dbgt("x0", [128, HID])
        dbgt("lsnap", [128, 28])
        dbgt("kv0", [128, 512], BF16)
        dbgt("mb0", [128, 1024], BF16)
        dbgt("q0", [128, 1024], BF16)
        dbgt("F0", [128, 2, FD], BF16)
        dbgt("nst0", [128, 24])
        dbgt("wmv0", [128, 8, 128], BF16)
        dbgt("coefn0", [128, 8], BF16)
        dbgt("memN", [128, 4, 6, 512], BF16)
        dbgt("normN", [128, 4, 6, 4], BF16)
        dbgt("mhg0", [128, HID])
        dbgt("gate0", [128, HID])
        dbgt("y0", [128, HID])
        dbgt("alpha", [1, HID])
        dbgt("z0", [128, HID])

    xt = xh.rearrange("(n p) d -> n p d", p=128)

    es = contextlib.ExitStack()
    with es:
        def sb(name, shape, dt):
            return es.enter_context(nc.sbuf_tensor(name, shape, dt))

        def pst(name, shape, dt):
            return es.enter_context(nc.psum_tensor(name, shape, dt))

        ps_a = pst("ps_a", [128, 2560], F32)
        ps_n = pst("ps_n", [128, 512], F32)
        ps_dot = pst("ps_dot", [128, 8], F32)
        ps_tr = pst("ps_tr", [128, 8, 128], BF16)

        ident_bf = sb("ident_bf", [128, 128], BF16)
        ident_f = sb("ident_f", [128, 128], F32)
        ones_r = sb("ones_r", [1, 128], BF16)
        ones_rf = sb("ones_rf", [1, 128], F32)
        ones_c = sb("ones_c", [128, 1], BF16)
        normF = sb("normF", [128, 96], F32)
        msb = sb("msb", [128, 16], F32)
        bn24 = sb("bn24", [128, 24], F32)
        lns = sb("lns", [128, 28], F32)
        sq_scr = sb("sq_scr", [128, 2, FD], F32)
        sstat = sb("sstat", [128, 24], F32)
        mv_scr = sb("mv_scr", [128, 4, 128], F32)
        lsnap = sb("lsnap", [128, 28], F32)
        hsb_sb = sb("hsb_sb", [128, HID], BF16)
        hsT = sb("hsT", [128, 2, NCH, 128], BF16)
        memN_a = sb("memN_a", [128, 2, 6, 512], BF16)
        memNs = [memN_a, None]
        normN = sb("normN", [128, 4, 6, 4], BF16)
        mstage = sb("mstage", [128, 1, 6, 128], F32)
        dstage2 = sb("dstage2", [128, 1, 6, 129], F32)
        norm96 = sb("norm96s", [96, 128], F32)
        x_es = contextlib.ExitStack()
        x_sb = x_es.enter_context(nc.sbuf_tensor("x_sb", [128, 2, HID], F32))

        s_const = P.sem("const")
        s_konst = P.sem("konst")
        s_ntr = P.sem("ntr")
        s_endA1 = P.sem("endA1")
        s_endA2 = P.sem("endA2")
        s_endB1 = P.sem("endB1")
        s_cc = [P.sem("cc0"), P.sem("cc1")]
        s_out = P.sem("out")
        s_x = [P.sem("x_s0"), P.sem("x_s1")]
        s_ht = [P.sem("ht_s0"), P.sem("ht_s1")]
        s_htw = [P.sem("htw_s0"), P.sem("htw_s1")]
        s_hs = P.sem("hs")
        s_hsb = P.sem("hsb")
        s_tr = P.sem("tr")
        s_te = P.sem("te")
        s_lnv = P.sem("lnv")
        s_lsn = P.sem("lsn")
        s_wg = P.sem("wg")
        s_mst = P.sem("mst")
        s_memN = P.sem("memN")
        s_lnq = P.sem("lnq")
        s_lnd = P.sem("lnd")

        P.dma("gpsimd", ident_bf[:], identh, s_const)
        P.dma("gpsimd", ident_f[:], identh, s_const)
        P.op("vector", lambda v: v.memset(ones_r[:], 1.0), s_konst)
        P.op("vector", lambda v: v.memset(ones_rf[:], 1.0), s_konst)
        P.op("vector", lambda v: v.memset(ones_c[:], 1.0), s_konst)
        P.op("vector", lambda v: v.memset(lns[:], 0.0), s_konst)
        P.op("vector", lambda v: v.memset(bn24[:], 0.0), s_konst)
        P.op("vector", lambda v: v.memset(sstat[:], 0.0), s_konst)
        P.op("vector", lambda v: v.memset(msb[:], 0.0), s_konst)

        def ln_stats(src_ap, rstd_col, negms_col):
            for i in range(4):
                P.op("vector", lambda v, i=i, a=src_ap: v.bn_stats(
                    bn24[:, 6 * i:6 * i + 6], a[:, 512 * i:512 * (i + 1)]))
            m8 = bn24.ap()[:, 1:24:3]
            M28 = bn24.ap()[:, 2:24:3]
            P.bar("vector")
            P.op("vector", lambda v, m8=m8: v.tensor_tensor(lns[:, 8:16], m8, m8, ALU.mult))
            P.op("vector", lambda v, m8=m8: v.reduce_sum(lns[:, 1:2], m8, axis=AX.X))
            P.bar("vector")
            P.op("vector", lambda v, M28=M28: v.scalar_tensor_tensor(
                lns[:, 20:28], lns[:, 8:16], 256.0, M28, ALU.mult, ALU.add))
            P.op("vector", lambda v: v.tensor_scalar(
                lns[:, 2:3], lns[:, 1:2], 0.125, None, ALU.mult))
            P.bar("vector")
            P.op("vector", lambda v: v.reduce_sum(lns[:, 0:1], lns[:, 20:28], axis=AX.X))
            P.op("vector", lambda v: v.tensor_tensor(
                lns[:, 3:4], lns[:, 2:3], lns[:, 2:3], ALU.mult))
            P.bar("vector")
            P.op("vector", lambda v: v.tensor_scalar(
                lns[:, 4:5], lns[:, 0:1], 1.0 / HID, LNEPS, ALU.mult, ALU.add))
            P.bar("vector")
            P.op("vector", lambda v: v.tensor_tensor(
                lns[:, 18:19], lns[:, 4:5], lns[:, 3:4], ALU.subtract))
            P.bar("vector")
            P.op("vector", lambda v: v.reciprocal(lns[:, 19:20], lns[:, 18:19]), s_lnv)
            P.wait("scalar", s_lnv)
            P.op("scalar", lambda s, rc=rstd_col: s.activation(rc, lns[:, 19:20], AF.Sqrt), s_lnq)
            P.wait("vector", s_lnq)
            P.op("vector", lambda v, rc=rstd_col, nm=negms_col: v.scalar_tensor_tensor(
                nm, lns[:, 2:3], -1.0, rc, ALU.mult, ALU.mult), s_lnd)

        def transposes(srcs, dst, swaits=()):
            n = len(srcs)
            r0 = 0
            while r0 < n:
                rn = min(8, n - r0)
                P.wait("tensor", s_te)
                for i in range(rn):
                    P.op("tensor", lambda t, i=i, a=srcs[r0 + i]: t.transpose(
                        ps_tr[:, i, :], a, ident_bf[:]), s_tr if i == rn - 1 else None)
                P.wait("scalar", s_tr)
                for sw in swaits:
                    P.wait("scalar", sw[0], sw[1])
                swaits = ()
                P.op("scalar", lambda s, r0=r0, rn=rn, d=dst: s.copy(
                    d[:, r0:r0 + rn, :], ps_tr[:, 0:rn, :]), s_te)
                r0 += rn
            P.wait("tensor", s_te)
            return s_te.count

        def build_head_dma(hh):
            P.wait("gpsimd", s_cc[hh // 8], 1)
            if hh >= 1:
                P.wait("gpsimd", s_memN, hh)
            hl = hh % 8
            P.dma("gpsimd", mstage[:, 0, :, :], memPh[hh], s_mst)
            P.dma("gpsimd", dstage2[:, 0, :, 0:128],
                  d_red[hh // 8][:, :, 128 * hl:128 * (hl + 1)].rearrange("c p d -> p c d"),
                  s_mst)
            P.dma("gpsimd", dstage2[:, 0, :, 128:129],
                  d_red[hh // 8][:, :, 1024 + hl:1025 + hl].rearrange("c p d -> p c d"),
                  s_mst, slow=True)

        def build_head_v(hh):
            P.wait("vector", s_mst, 48 * (hh + 1))
            P.op("vector", lambda v, hh=hh: v.tensor_tensor(
                memNs[hh // 8][:, (hh // 4) % 2, :, 128 * (hh % 4):128 * (hh % 4 + 1)],
                mstage[:, 0, :, :], dstage2[:, 0, :, 0:128], ALU.add))
            P.op("vector", lambda v, hh=hh: v.tensor_tensor(
                normN[:, hh // 4, :, (hh % 4):(hh % 4) + 1],
                normF.ap()[:, 6 * hh:6 * (hh + 1)].rearrange("p (c o) -> p c o", o=1),
                dstage2[:, 0, :, 128:129], ALU.add), s_memN)

        P.dma("gpsimd", norm96[:], normh, s_const)
        P.wait("tensor", s_const)
        P.wait("tensor", s_konst)
        P.op("tensor", lambda t: t.transpose(
            ps_n.ap()[:, 0:96], norm96[0:96, :], ident_f[0:96, 0:96]), s_ntr)
        P.wait("scalar", s_ntr)
        P.op("scalar", lambda s: s.copy(normF[:], ps_n.ap()[:, 0:96]), s_ntr)

        wg_pre = [None, None]
        esA = contextlib.ExitStack()
        with esA:
            def sbA(name, shape, dt):
                return esA.enter_context(nc.sbuf_tensor(name, shape, dt))

            wA = sbA("wA", [128, NCH, 2560], BF16)
            bias_kvmb = sbA("bias_kvmb", [1, 1536], BF16)
            bias_q = sbA("bias_q", [1, 1024], BF16)
            memA = sbA("memA", [128, 2, 6, 512], BF16)
            normA = sbA("normA", [128, 2, 6, 4], BF16)
            kv_bf = sbA("kv_bf", [128, 512], BF16)
            mb_bf = sbA("mb_bf", [128, 1024], BF16)
            q_bf = sbA("q_bf", [128, 1, 1024], BF16)
            xe2 = sbA("xe2", [128, 2, 262], BF16)
            F_sb = sbA("F_sb", [128, NT, 2, FD], BF16)
            fT = sbA("fT", [128, 12, 128], BF16)
            wmv = sbA("wmv", [128, NT, 8, 128], BF16)
            coefn = sbA("coefn", [128, NT, 8], BF16)
            dstage = sbA("dstage", [128, 1, 1032], F32)

            s_q = [P.sem("q_s0")]
            half_sems = {}

            s_w4 = [P.sem(f"w4_{i}") for i in range(4)]
            for half in range(2):
                s_wA = P.sem(f"wA{half}")
                half_sems[half] = {}
                s_nA = P.sem(f"nA{half}")
                s_proj = P.sem(f"proj{half}")
                s_ev1 = P.sem(f"ev1_{half}")
                s_F = P.sem(f"F{half}")
                s_relu = P.sem(f"relu{half}")
                s_ssq = P.sem(f"ssq{half}")
                s_num = P.sem(f"num{half}")
                s_chain = P.sem(f"chain{half}")
                s_stats = P.sem(f"stats{half}")
                half_sems[half]["proj"] = s_proj
                half_sems[half]["num"] = s_num
                half_sems[half]["chain"] = s_chain

                s_p7 = P.sem(f"p7_{half}")
                s_wB = P.sem(f"wB{half}")
                s_mA = P.sem(f"mA{half}")
                half_sems[half]["p7"] = s_p7
                if half == 0:
                    P.dma("gpsimd", bias_kvmb[:, 0:256], bk[:, 0:256], s_wB)
                    P.dma("gpsimd", bias_kvmb[:, 256:512], bv[:, 0:256], s_wB)
                    P.dma("gpsimd", bias_kvmb[:, 512:1536], bm[:, 0:1024], s_wB)
                    P.dma("gpsimd", bias_q[:], bq[:, 0:1024], s_wB)
                for c in range(NCH):
                    if half == 1:
                        P.wait("gpsimd", half_sems[0]["p7"], c + 1)
                    P.dma("gpsimd", wA[:, c, :], waH[half][:, c, :], s_w4[c // 4])
                if half == 1:
                    P.wait("gpsimd", half_sems[0]["proj"], 8)
                    P.dma("gpsimd", bias_kvmb[:, 0:256], bk[:, 256:512], s_wB)
                    P.dma("gpsimd", bias_kvmb[:, 256:512], bv[:, 256:512], s_wB)
                    P.dma("gpsimd", bias_kvmb[:, 512:1536], bm[:, 1024:2048], s_wB)
                    P.dma("gpsimd", bias_q[:], bq[:, 1024:2048], s_wB)
                for i in range(4):
                    P.wait("gpsimd", s_w4[i], 64 * (half + 1))
                P.op("gpsimd", lambda g: g.nop(), s_wA)
                if half == 1:
                    P.wait("gpsimd", half_sems[0]["num"], 32)
                for j in range(2):
                    P.dma("gpsimd", memA[:, j, :, :], memP4h[2 * half + j], s_mA)
                if half == 1:
                    for hh in range(0, 8):
                        build_head_dma(hh)
                P.wait("vector", s_ntr, 2)
                if half == 1:
                    P.wait("vector", half_sems[0]["num"], 32)
                P.op("vector", lambda v, half=half: v.tensor_copy(
                    normA[:],
                    normF.ap()[:, 48 * half:48 * (half + 1)]
                    .rearrange("p (j hl c) -> p j c hl", j=2, hl=4, c=6)), s_nA)

                for t in range(NT):
                    gt = 8 * half + t
                    slot = t % 2
                    if half == 0:
                        if gt >= 2:
                            P.wait("sync", s_hs, gt - 1)
                        P.dma("sync", x_sb[:, slot, :], xt[t], s_x[slot])
                        if debug and gt == 1:
                            P.wait("vector", s_out, s_out.count)
                            P.wait("scalar", s_out, s_out.count)
                        P.wait("vector", s_x[slot], 16 * (gt // 2 + 1))
                        ln_stats(x_sb.ap()[:, slot, :], msb[:, 2 * t:2 * t + 1],
                                 msb[:, 2 * t + 1:2 * t + 2])
                        if debug and gt == 0:
                            P.bar("vector")
                            P.op("vector", lambda v: v.tensor_copy(lsnap[:], lns[:, 0:28]), s_lsn)
                        P.op("vector", lambda v: v.nop(), s_stats)
                        P.wait("scalar", s_x[slot], 16 * (gt // 2 + 1))
                        P.wait("scalar", s_lnd, s_lnd.count)
                        if gt >= 1:
                            P.wait("scalar", s_te, s_te.count)
                        P.op("scalar", lambda s, slot=slot, t=t: s.activation(
                            hsb_sb[:], x_sb[:, slot, :], AF.Identity,
                            bias=msb[:, 2 * t + 1:2 * t + 2], scale=msb[:, 2 * t:2 * t + 1]), s_hs)
                        if debug and gt == 0:
                            P.wait("sync", s_hs, 1)
                            P.dma("sync", dbg["hs0"], hsb_sb[:], s_out)
                            P.dma("sync", dbg["msb"], msb[:, 0:16], s_out)
                            P.dma("sync", dbg["lns0"], lns[:, 0:28], s_out)
                            P.dma("sync", dbg["bn0"], bn24[:, 0:24], s_out)
                            P.dma("sync", dbg["x0"], x_sb[:, 0, :], s_out)
                            P.wait("sync", s_lsn, 1)
                            P.dma("sync", dbg["lsnap"], lsnap[:], s_out)
                        P.wait("tensor", s_hs, gt + 1)
                        htswaits = []
                        if t >= 2:
                            htswaits = [(s_proj, t - 1), (s_htw[slot], 16 * (t // 2))]
                        transposes([hsb_sb.ap()[:, 128 * c:128 * (c + 1)] for c in range(NCH)],
                                   hsT.ap()[:, slot], swaits=htswaits)
                        P.wait("sync", s_te, s_te.count)
                        P.dma("sync", ht_hbm[t], hsT[:, slot], s_htw[slot])
                    else:
                        if t < 2:
                            P.wait("sync", s_htw[slot])
                        else:
                            P.wait("sync", s_proj, t - 1)
                        P.dma("sync", hsT[:, slot], ht_hbm[t], s_ht[slot])
                        if t == 0:
                            P.wait("tensor", half_sems[0]["dev"], 6)
                        P.wait("tensor", s_ht[slot], 16 * (t // 2 + 1))
                    if t > 0:
                        P.wait("tensor", s_wA, 1)
                        P.wait("tensor", s_ev1, 2 * t)
                    wins = [(0, 512, 0), (512, 1024, 0), (1024, 1536, 0),
                            (1536, 2048, 1), (2048, 2560, 1)]
                    for c in range(NCH + 1):
                        last = c == NCH
                        lhs = ones_r[:] if last else hsT[:, slot, c, :]
                        if t == 0:
                            if last:
                                P.wait("tensor", s_wB)
                            else:
                                P.wait("tensor", s_w4[c // 4], 64 * (half + 1))
                        for wi, (a, b, isq) in enumerate(wins):
                            if last:
                                rhs = bias_q[:, a - 1536:b - 1536] if isq else bias_kvmb[:, a:b]
                            else:
                                rhs = wA[:, c, a:b]
                            psem = None
                            if last and wi == 4:
                                psem = s_proj
                            elif t == NT - 1 and wi == 4:
                                psem = s_p7
                            P.op("tensor", lambda t_, lhs=lhs, rhs=rhs, a=a, b=b, c=c, last=last:
                                 t_.matmul(ps_a[:, a:b], lhs, rhs, start=(c == 0), stop=last),
                                 psem)
                    P.wait("scalar", s_proj, t + 1)
                    if t > 0:
                        P.wait("scalar", s_chain, 2 * t)
                    if half == 1 and t == 0:
                        P.wait("scalar", s_endA1)
                        P.wait("scalar", half_sems[0]["dwr"][0], 96)
                    P.op("scalar", lambda s: s.copy(kv_bf[:], ps_a[:, 0:512]))
                    P.op("scalar", lambda s: s.activation(mb_bf[:], ps_a[:, 512:1536], AF.Sigmoid), s_ev1)
                    P.wait("scalar", s_q[0], 16 * gt)
                    P.op("scalar", lambda s: s.copy(q_bf[:, 0, :], ps_a[:, 1536:2560]), s_ev1)
                    P.wait("sync", s_ev1, 2 * (t + 1))
                    P.dma("sync", q_hbm[t, :, 1024 * half:1024 * (half + 1)], q_bf[:, 0, :], s_q[0])
                    if debug and gt == 0:
                        P.dma("sync", dbg["kv0"], kv_bf[:], s_out)
                        P.dma("sync", dbg["mb0"], mb_bf[:], s_out)
                        P.dma("sync", dbg["q0"], q_bf[:, 0, :], s_out)
                    if t > 0:
                        P.wait("scalar", s_F, t)
                    P.wait("scalar", s_ev1, 2 * (t + 1))
                    kview = kv_bf.ap().rearrange("p (j d) -> p j d", j=4)
                    P.op("scalar", lambda s, kview=kview: s.activation(
                        xe2[:, :, 3:131], kview[:, 0:2, :], AF.Relu))
                    P.op("scalar", lambda s, kview=kview: s.activation(
                        xe2[:, :, 131:259], kview[:, 0:2, :], AF.Relu, scale=-1.0), s_relu)
                    P.wait("vector", s_relu, t + 1)
                    P.op("vector", lambda v: v.tensor_copy(xe2[:, :, 0:3], xe2[:, :, 256:259]))
                    P.bar("vector")
                    if half == 1 and t == 0:
                        P.wait("vector", s_endA1)
                    for j in (1, 2, 3):
                        P.op("vector", lambda v, j=j, t=t: v.tensor_tensor(
                            F_sb[:, t, :, 256 * (j - 1):256 * j], xe2[:, :, 3:259],
                            xe2[:, :, 3 - j:259 - j], ALU.mult), s_F if j == 3 else None)
                    if debug and gt == 0:
                        P.wait("sync", s_F, 1)
                        P.dma("sync", dbg["F0"], F_sb[:, 0, :, :], s_out)
                    P.wait("scalar", s_F, t + 1)
                    if t > 0:
                        P.wait("scalar", s_chain, 2 * t)
                    for j in range(2):
                        P.op("scalar", lambda s, j=j, t=t: s.activation(
                            sq_scr[:, j, :], F_sb[:, t, j, :], AF.Square, accum_out=sstat[:, j:j + 1]))
                    P.bar("scalar")
                    P.op("scalar", lambda s: s.activation(sstat[:, 2:4], sstat[:, 0:2], AF.Sqrt), s_ssq)
                    P.wait("vector", s_ssq, 2 * t + 1)
                    P.op("vector", lambda v: v.tensor_scalar(
                        sstat[:, 22:24], sstat[:, 2:4], EPS, None, ALU.mult))
                    P.op("vector", lambda v: v.reciprocal(sstat[:, 4:6], sstat[:, 2:4]), s_ssq)
                    P.bar("vector")
                    P.wait("tensor", s_F, t + 1)
                    fswaits = [(s_num, 4 * t)] if t > 0 else []
                    transposes([F_sb.ap()[:, t, j, 128 * c:128 * (c + 1)]
                                for j in range(2) for c in range(6)], fT, swaits=fswaits)
                    P.wait("tensor", s_nA)
                    if t == 0:
                        P.wait("tensor", s_mA)
                    if half == 0 and t == 0:
                        P.wait("tensor", s_ntr, 2)
                    for j in range(2):
                        if t > 0 or j > 0:
                            P.wait("tensor", s_chain, 2 * t + j)
                        for c in range(6):
                            P.op("tensor", lambda t_, j=j, c=c: t_.matmul(
                                ps_n[:], fT[:, 6 * j + c, :], memA[:, j, c, :],
                                start=(c == 0), stop=(c == 5)), s_num if c == 5 else None)
                        for c in range(6):
                            P.op("tensor", lambda t_, j=j, c=c: t_.matmul(
                                ps_dot[:, 4 * j:4 * j + 4], fT[:, 6 * j + c, :], normA[:, j, c, :],
                                start=(c == 0), stop=(c == 5)), s_num if c == 5 else None)
                        P.wait("vector", s_num, 4 * t + 2 * (j + 1))
                        P.wait("vector", s_ssq, 2 * (t + 1))
                        invn = sstat.ap()[:, 4 + j:5 + j]
                        epsn = sstat.ap()[:, 22 + j:23 + j]
                        den = sstat.ap()[:, 6:10]
                        P.op("vector", lambda v, j=j, epsn=epsn, den=den: v.tensor_scalar(
                            den, ps_dot[:, 4 * j:4 * j + 4], 1.0, epsn, ALU.mult, ALU.add))
                        P.bar("vector")
                        P.op("vector", lambda v, den=den: v.reciprocal(sstat[:, 10:14], den))
                        P.bar("vector")
                        P.op("vector", lambda v: v.tensor_scalar(
                            sstat[:, 14:18], sstat[:, 10:14], -1.0, None, ALU.mult))
                        P.bar("vector")
                        def stt_mv(hl, j=j, t=t):
                            P.op("vector", lambda v, hl=hl, j=j: v.scalar_tensor_tensor(
                                mv_scr[:, hl, :], ps_n[:, 128 * hl:128 * (hl + 1)],
                                sstat[:, 14 + hl:15 + hl],
                                kv_bf[:, 256 + 128 * j:384 + 128 * j], ALU.mult, ALU.add))
                        def stt_wmv(hl, j=j, t=t, invn=invn):
                            h8 = 4 * j + hl
                            P.op("vector", lambda v, hl=hl, h8=h8, t=t, invn=invn:
                                 v.scalar_tensor_tensor(
                                wmv[:, t, h8, :], mv_scr[:, hl, :], invn,
                                mb_bf[:, 128 * h8:128 * (h8 + 1)], ALU.mult, ALU.mult))
                        stt_mv(0)
                        stt_mv(1)
                        stt_mv(2)
                        stt_mv(3)
                        P.op("vector", lambda v, den=den, invn=invn: v.tensor_scalar(
                            sstat[:, 18:22], den, invn, None, ALU.mult))
                        P.bar("vector")
                        stt_wmv(0)
                        stt_wmv(1)
                        stt_wmv(2)
                        stt_wmv(3)
                        P.op("vector", lambda v: v.tensor_scalar(
                            sstat[:, 14:18], sstat[:, 18:22], -1.0, 1.0, ALU.mult, ALU.add))
                        P.bar("vector")
                        P.op("vector", lambda v: v.tensor_scalar(
                            sstat[:, 18:22], sstat[:, 14:18], 0.0, 1.0, ALU.max, ALU.min))
                        P.bar("vector")
                        P.op("vector", lambda v, t=t, j=j, invn=invn: v.tensor_scalar(
                            coefn[:, t, 4 * j:4 * j + 4], sstat[:, 18:22], invn,
                            None, ALU.mult), s_chain)
                    if debug and gt == 0:
                        P.wait("sync", s_chain, 2)
                        P.dma("sync", dbg["nst0"], sstat[:, 0:24], s_out)
                        P.dma("sync", dbg["wmv0"], wmv[:, 0, :, :], s_out)
                        P.dma("sync", dbg["coefn0"], coefn[:, 0, :], s_out)
                    if half == 1 and t == 3:
                        for hh in range(0, 8):
                            build_head_v(hh)

                P.wait("tensor", s_chain)
                P.wait("tensor", s_ev1)
                s_dmm = P.sem(f"dmm{half}")
                s_dev = P.sem(f"dev{half}")
                s_dwr = [P.sem(f"dwr{half}_0"), P.sem(f"dwr{half}_1")]
                half_sems[half]["dev"] = s_dev
                half_sems[half]["dwr"] = s_dwr
                for c in range(6):
                    reg = (c % 2) * 1024
                    if c >= 1:
                        P.wait("tensor", s_dev, c)
                    for tt in range(NT):
                        for j in range(2):
                            P.op("tensor", lambda t_, c=c, tt=tt, j=j, reg=reg: t_.matmul(
                                ps_a[:, reg + 512 * j:reg + 512 * (j + 1)],
                                F_sb[:, tt, j, 128 * c:128 * (c + 1)],
                                wmv[:, tt, 4 * j:4 * j + 4, :],
                                start=(tt == 0), stop=(tt == NT - 1)), None)
                            dps = ps_dot[:, 0:4] if j == 0 else ps_n.ap()[:, 0:4]
                            P.op("tensor", lambda t_, c=c, tt=tt, j=j, dps=dps: t_.matmul(
                                dps, F_sb[:, tt, j, 128 * c:128 * (c + 1)],
                                coefn[:, tt, 4 * j:4 * j + 4],
                                start=(tt == 0), stop=(tt == NT - 1)),
                                s_dmm if (tt == NT - 1 and j == 1) else None)
                    P.wait("scalar", s_dmm, c + 1)
                    if c >= 1:
                        P.wait("scalar", s_dwr[0], 16 * c)
                    P.op("scalar", lambda s, c=c, reg=reg: s.copy(
                        dstage[:, 0, 0:1024], ps_a[:, reg:reg + 1024]))
                    P.op("scalar", lambda s, c=c: s.copy(dstage[:, 0, 1024:1028], ps_dot[:, 0:4]))
                    P.op("scalar", lambda s, c=c: s.copy(dstage[:, 0, 1028:1032], ps_n.ap()[:, 0:4]), s_dev)
                    P.wait("sync", s_dev, c + 1)
                    P.dma("sync", d_loc[half][c], dstage[:, 0, :], s_dwr[0])
                if half == 1:
                    P.wait("gpsimd", s_proj, 8)
                    s_wg.bump(16)
                    P.emit("gpsimd", lambda e: e.dma_start(
                        out=wg_pre[0][:], in_=wgP).then_inc(s_wg.h, 16))
                    s_wg.bump(16)
                    P.emit("gpsimd", lambda e: e.dma_start(
                        out=wg_pre[1][:], in_=bg).then_inc(s_wg.h, 16))
                P.wait("gpsimd", s_dwr[0])
                P.wait("gpsimd", s_dwr[1])
                P.op("gpsimd", lambda g, half=half: g.collective_compute(
                    "AllReduce", ALU.add, replica_groups=[list(range(NCORE))],
                    ins=[d_loc[half]], outs=[d_red[half]]), s_cc[half])
                P.op("tensor", lambda t: t.nop(), s_endA1 if half == 0 else s_endA2)
                if half == 1:
                    P.wait("sync", s_q[0])
                    P.op("sync", lambda e: e.nop(), s_endA2)

        esB = contextlib.ExitStack()
        with esB:
            def sbB(name, shape, dt):
                return esB.enter_context(nc.sbuf_tensor(name, shape, dt))

            wg_sb = sbB("wg_sb", [128, NCH, HID], BF16)
            bias_g = sbB("bias_g", [1, HID], BF16)
            wg_pre[0] = wg_sb
            wg_pre[1] = bias_g
            memN_b = sbB("memN_b", [128, 2, 6, 512], BF16)
            memNs[1] = memN_b
            q_in = sbB("q_in", [128, 2, HID], BF16)
            xe4 = sbB("xe4", [128, 2, 4, 262], BF16)
            Fq = sbB("Fq", [128, 2, 4, FD], BF16)
            fTq = sbB("fTq", [128, 2, 24, 128], BF16)
            gate_sb = sbB("gate_sb", [128, HID], F32)
            hs_sb = sbB("hs_sb", [128, HID], F32)
            mhg = sbB("mhg", [128, HID], F32)
            ln2_sb = mhg
            y_sb = sbB("y_sb", [128, HID], F32)

            s_hs2 = P.sem("hs2")
            s_den = P.sem("den")
            s_gproj = P.sem("gproj")
            s_gate = P.sem("gate")
            s_qrd = [P.sem("qrd0"), P.sem("qrd1")]
            s_relu2 = P.sem("relu2")
            s_Fq = P.sem("Fq")
            s_ssq2 = P.sem("ssq2")
            s_numa = P.sem("numa")
            s_mhg = P.sem("mhg")
            s_attn = P.sem("attn")
            s_ln2 = P.sem("ln2")
            s_yv = P.sem("yv")
            s_ywr = [P.sem("ywr0"), P.sem("ywr1")]

            for hh in range(8, 16):
                build_head_dma(hh)

            fq_te = {}
            for t in range(NT):
                slot = t % 2
                if t < 2:
                    P.wait("sync", s_hs, 8)
                else:
                    P.wait("sync", s_yv, t - 1)
                P.dma("sync", x_sb[:, slot, :], xt[t], s_x[slot])
                if t >= 2:
                    P.wait("sync", s_relu2, 4 * t - 4)
                if t == 0:
                    P.wait("sync", s_endA2)
                P.dma("sync", q_in[:, slot, :], q_hbm[t], s_qrd[slot])
                if t >= 2:
                    P.wait("sync", s_gproj, t - 1)
                P.dma("sync", hsT[:, slot], ht_hbm[t], s_ht[slot])
                if debug and t in (1, 2):
                    P.wait("scalar", s_out, s_out.count)
                    P.wait("vector", s_out, s_out.count)
                if t == 0:
                    P.wait("scalar", s_endA2)
                P.wait("tensor", s_wg)
                P.wait("tensor", s_ht[slot], 16 * (4 + t // 2 + 1))
                if t == 0:
                    P.wait("tensor", half_sems[1]["dev"], 6)
                    P.wait("tensor", s_endA2)
                if t > 0:
                    P.wait("tensor", s_gate, t)
                for c in range(NCH + 1):
                    last = c == NCH
                    lhs = ones_r[:] if last else hsT[:, slot, c, :]
                    for wi in range(4):
                        rhs = (bias_g[:, 512 * wi:512 * (wi + 1)] if last
                               else wg_sb[:, c, 512 * wi:512 * (wi + 1)])
                        P.op("tensor", lambda t_, lhs=lhs, rhs=rhs, wi=wi, c=c, last=last:
                             t_.matmul(ps_a[:, 512 * wi:512 * (wi + 1)], lhs, rhs,
                                       start=(c == 0), stop=last),
                             s_gproj if (last and wi == 3) else None)
                P.wait("scalar", s_gproj, t + 1)
                if t > 0:
                    P.wait("scalar", s_mhg, 4 * t)
                P.op("scalar", lambda s: s.activation(gate_sb[:], ps_a[:, 0:2048], AF.Sigmoid), s_gate)
                if debug and t == 0:
                    P.wait("sync", s_gate, 1)
                    P.dma("sync", dbg["gate0"], gate_sb[:], s_out)
                for g in range(4):
                    gi = 4 * t + g
                    par = g % 2
                    if t == 0 and g == 2:
                        for hh in range(8, 12):
                            build_head_v(hh)
                    if t == 0 and g == 3:
                        for hh in range(12, 16):
                            build_head_v(hh)
                    P.wait("scalar", s_qrd[slot], 16 * (t // 2 + 1))
                    if gi > 1:
                        P.wait("scalar", s_Fq, gi - 1)
                    qv = q_in.ap()[:, slot, :].rearrange("p (g4 d) -> p g4 d", d=128)
                    P.op("scalar", lambda s, g=g, par=par, qv=qv: s.activation(
                        xe4[:, par, :, 3:131], qv[:, 4 * g:4 * g + 4, :], AF.Relu))
                    P.op("scalar", lambda s, g=g, par=par, qv=qv: s.activation(
                        xe4[:, par, :, 131:259], qv[:, 4 * g:4 * g + 4, :], AF.Relu, scale=-1.0), s_relu2)
                    P.wait("vector", s_relu2, gi + 1)
                    P.op("vector", lambda v, par=par: v.tensor_copy(
                        xe4[:, par, :, 0:3], xe4[:, par, :, 256:259]))
                    P.bar("vector")
                    if gi > 1:
                        P.wait("vector", s_te, fq_te.get(par, 0))
                        P.wait("vector", s_ssq2, gi - 1)
                    for j in (1, 2, 3):
                        P.op("vector", lambda v, j=j, par=par: v.tensor_tensor(
                            Fq[:, par, :, 256 * (j - 1):256 * j], xe4[:, par, :, 3:259],
                            xe4[:, par, :, 3 - j:259 - j], ALU.mult), s_Fq if j == 3 else None)
                    P.wait("scalar", s_Fq, gi + 1)
                    if gi > 1:
                        P.wait("scalar", s_mhg, gi - 1)
                    sreg = 8 * par
                    for hl in range(4):
                        P.op("scalar", lambda s, hl=hl, sreg=sreg, par=par: s.activation(
                            sq_scr[:, hl % 2, :], Fq[:, par, hl, :], AF.Square,
                            accum_out=sstat[:, sreg + hl:sreg + hl + 1]))
                        if hl == 1:
                            P.bar("scalar")
                    P.bar("scalar")
                    P.op("scalar", lambda s, sreg=sreg: s.activation(
                        sstat[:, sreg + 4:sreg + 8], sstat[:, sreg:sreg + 4], AF.Sqrt), s_ssq2)
                    P.wait("tensor", s_Fq, gi + 1)
                    fqswaits = [(s_numa, gi - 1)] if gi > 1 else []
                    transposes([Fq.ap()[:, par, hl, 128 * c:128 * (c + 1)]
                                for hl in range(4) for c in range(6)],
                               fTq.ap()[:, par], swaits=fqswaits)
                    fq_te[par] = s_te.count
                    if gi == 0:
                        P.wait("tensor", half_sems[1]["dev"], 6)
                    if gi > 1:
                        P.wait("tensor", s_mhg, gi - 1)
                    P.wait("tensor", s_memN, 4 * (g + 1))
                    memg = memNs[g // 2]
                    nbank = (ps_n.ap() if par == 0 else ps_a.ap()[:, 2048:2560])
                    for hl in range(4):
                        for c in range(6):
                            P.op("tensor", lambda t_, g=g, c=c, hl=hl, memg=memg, nb=nbank, par=par: t_.matmul(
                                nb[:, 128 * hl:128 * (hl + 1)], fTq[:, par, 6 * hl + c, :],
                                memg[:, g % 2, c, 128 * hl:128 * (hl + 1)],
                                start=(c == 0), stop=(c == 5)), None)
                    if gi > 0:
                        P.wait("tensor", s_den, gi)
                    for hl in range(4):
                        for c in range(6):
                            P.op("tensor", lambda t_, g=g, c=c, hl=hl, par=par: t_.matmul(
                                ps_dot[:, 4 * par + hl:4 * par + hl + 1], fTq[:, par, 6 * hl + c, :],
                                normN[:, g, c, hl:hl + 1],
                                start=(c == 0), stop=(c == 5)),
                                s_numa if (hl == 3 and c == 5) else None)
                    P.wait("vector", s_numa, gi + 1)
                    P.wait("vector", s_ssq2, gi + 1)
                    n4 = sstat.ap()[:, sreg + 4:sreg + 8]
                    den = sstat.ap()[:, 16:20]
                    P.op("vector", lambda v, n4=n4, den=den, par=par: v.scalar_tensor_tensor(
                        den, n4, EPS, ps_dot[:, 4 * par:4 * par + 4], ALU.mult, ALU.add), s_den)
                    P.bar("vector")
                    P.op("vector", lambda v, den=den: v.reciprocal(sstat[:, 20:24], den))
                    P.bar("vector")
                    P.wait("vector", s_gate, t + 1)
                    for hl in range(4):
                        hcol = 128 * (4 * g + hl)
                        P.op("vector", lambda v, hl=hl, hcol=hcol, nb=nbank: v.scalar_tensor_tensor(
                            mhg[:, hcol:hcol + 128], nb[:, 128 * hl:128 * (hl + 1)],
                            sstat[:, 20 + hl:21 + hl], gate_sb[:, hcol:hcol + 128],
                            ALU.mult, ALU.mult), s_mhg if hl == 3 else None)
                P.wait("scalar", s_x[slot], 16 * (4 + t // 2 + 1))
                if t >= 1:
                    P.wait("scalar", s_attn, t)
                P.op("scalar", lambda s, slot=slot, t=t: s.activation(
                    hs_sb[:], x_sb[:, slot, :], AF.Identity,
                    bias=msb[:, 2 * t + 1:2 * t + 2], scale=msb[:, 2 * t:2 * t + 1]), s_hs2)
                if debug and t == 0:
                    P.wait("vector", s_out, s_out.count)
                if t > 0:
                    P.wait("vector", s_ln2, t)
                P.wait("vector", s_hs2, t + 1)
                P.bar("vector")
                P.op("vector", lambda v: v.tensor_tensor(gate_sb[:], mhg[:], hs_sb[:], ALU.add), s_attn)
                P.bar("vector")
                if debug and t == 0:
                    P.wait("sync", s_attn, 1)
                    P.dma("sync", dbg["mhg0"], gate_sb[:], s_out)
                ln_stats(gate_sb.ap(), lns[:, 5:6], lns[:, 6:7])
                P.wait("scalar", s_lnd)
                P.op("scalar", lambda s: s.activation(
                    ln2_sb[:], gate_sb[:], AF.Identity, bias=lns[:, 6:7], scale=lns[:, 5:6]), s_ln2)
                P.wait("vector", s_ln2, t + 1)
                P.wait("vector", s_ywr[0], 16 * t)
                P.op("vector", lambda v, slot=slot: v.tensor_tensor(
                    y_sb[:], ln2_sb[:], x_sb[:, slot, :], ALU.add), s_yv)
                P.wait("sync", s_yv, t + 1)
                P.dma("sync", y_hbm[t], y_sb[:], s_ywr[0])
                if debug and t == 0:
                    P.dma("sync", dbg["y0"], y_sb[:], s_out)
            if debug:
                P.wait("sync", s_memN, 16)
                P.dma("sync", dbg["memN"].rearrange("p g c x -> p (g c x)")
                      .rearrange("p (g2 r) -> p g2 r", g2=2)[:, 0, :], memN_a.ap()
                      .rearrange("p g c x -> p (g c x)"), s_out)
                P.dma("sync", dbg["memN"].rearrange("p g c x -> p (g c x)")
                      .rearrange("p (g2 r) -> p g2 r", g2=2)[:, 1, :], memN_b.ap()
                      .rearrange("p g c x -> p (g c x)"), s_out)
                P.dma("sync", dbg["normN"], normN[:], s_out)
            P.op("tensor", lambda t: t.nop(), s_endB1)
            P.op("vector", lambda v: v.nop(), s_endB1)
            P.op("scalar", lambda s: s.nop(), s_endB1)
            P.wait("sync", s_ywr[0])
            P.wait("sync", s_qrd[0])
            P.wait("sync", s_qrd[1])
            if debug:
                P.wait("sync", s_out, s_out.count)
            P.op("sync", lambda e: e.nop(), s_endB1)

        x_es.close()
        esC = contextlib.ExitStack()
        with esC:
            def sbC(name, shape, dt):
                return esC.enter_context(nc.sbuf_tensor(name, shape, dt))

            sT = sbC("sT", [128, NCH, HID], BF16)
            wf_st = sbC("wf_st", [128, 2, HID], F32)
            absw = sbC("absw", [128, HID], BF16)
            alpha_row = sbC("alpha_row", [1, HID], F32)
            inva = sbC("inva", [1, HID], F32)
            biasf_sb = sbC("biasf_sb", [1, HID], F32)
            bias_pre = sbC("bias_pre", [1, HID], BF16)
            alpha_bc = sbC("alpha_bc", [128, HID], F32)
            y_in = sbC("y_in", [128, 2, HID], F32)
            hs2b = sbC("hs2b", [128, HID], BF16)
            z_sb = sbC("z_sb", [128, HID], F32)
            ln4_sb = sbC("ln4_sb", [128, HID], F32)
            out_sb = sbC("out_sb", [128, HID], F32)

            s_wf = [P.sem("wf0"), P.sem("wf1")]
            s_bfl = P.sem("bfl")
            s_sign = P.sem("sign")
            s_amm = P.sem("amm")
            s_aev = P.sem("aev")
            s_bc = P.sem("bc")
            s_yrd = [P.sem("yrd0"), P.sem("yrd1")]
            s_hs3 = P.sem("hs3")
            s_fmm = P.sem("fmm")
            s_z = P.sem("z")
            s_ln4 = P.sem("ln4")
            s_outv = P.sem("outv")
            s_od = [P.sem("od0"), P.sem("od1")]

            P.wait("gpsimd", s_endB1)
            P.dma("gpsimd", biasf_sb[:], bfh, s_bfl)
            for c in range(NCH):
                slot = c % 2
                if c >= 2:
                    P.wait("gpsimd", s_sign, c - 1)
                P.dma("gpsimd", wf_st[:, slot, :],
                      wfT.rearrange("(c p) j -> p c j", p=128)[:, c, :], s_wf[slot])
                P.wait("scalar", s_wf[slot], 16 * (c // 2 + 1))
                if c == 0:
                    P.wait("scalar", s_endB1, 4)
                if c >= 1:
                    P.wait("scalar", s_amm, c)
                P.op("scalar", lambda s, c=c, slot=slot: s.activation(
                    sT[:, c, :], wf_st[:, slot, :], AF.Sign))
                P.op("scalar", lambda s, c=c, slot=slot: s.activation(
                    absw[:], wf_st[:, slot, :], AF.Abs), s_sign)
                P.wait("tensor", s_sign, c + 1)
                if c == 0:
                    P.wait("tensor", s_endB1, 4)
                    P.wait("tensor", s_z, 0)
                for wi in range(4):
                    P.op("tensor", lambda t_, wi=wi, c=c: t_.matmul(
                        ps_a.ap()[0:1, 512 * wi:512 * (wi + 1)], ones_c[:],
                        absw[:, 512 * wi:512 * (wi + 1)],
                        start=(c == 0), stop=(c == NCH - 1)),
                        s_amm if wi == 3 else None)
            P.wait("scalar", s_amm, NCH)
            P.op("scalar", lambda s: s.activation(
                alpha_row[:], ps_a.ap()[0:1, 0:2048], AF.Identity, scale=1.0 / HID), s_aev)
            P.wait("vector", s_aev)
            P.wait("vector", s_endB1, 4)
            P.op("vector", lambda v: v.reciprocal(inva[:], alpha_row[:]))
            P.wait("vector", s_bfl, 16)
            P.bar("vector")
            P.op("vector", lambda v: v.tensor_tensor(bias_pre[:], biasf_sb[:], inva[:], ALU.mult), s_aev)
            P.wait("tensor", s_aev, 2)
            for wi in range(4):
                P.op("tensor", lambda t_, wi=wi: t_.matmul(
                    ps_a[:, 512 * wi:512 * (wi + 1)], ones_rf[:],
                    alpha_row[:, 512 * wi:512 * (wi + 1)], start=True, stop=True),
                    s_bc if wi == 3 else None)
            P.wait("scalar", s_bc)
            P.op("scalar", lambda s: s.copy(alpha_bc[:], ps_a[:, 0:2048]), s_bc)
            if debug:
                P.wait("sync", s_aev, 2)
                P.dma("sync", dbg["alpha"], alpha_row[:], s_out)

            def b2_yload(t):
                slot = t % 2
                if t == 0:
                    P.wait("sync", s_endB1, 4)
                if t >= 2:
                    P.wait("sync", s_outv, t - 1)
                if t == 0:
                    P.wait("sync", s_ywr[0], 16 * 8)
                P.dma("sync", y_in[:, slot, :], y_hbm[t], s_yrd[slot])

            def b2_ln3(t):
                slot = t % 2
                if debug and t == 1:
                    P.wait("vector", s_out, s_out.count)
                P.wait("vector", s_yrd[slot], 16 * (t // 2 + 1))
                ln_stats(y_in.ap()[:, slot, :], lns[:, 5:6], lns[:, 6:7])
                P.wait("scalar", s_lnd)
                if t >= 1:
                    P.wait("scalar", s_te, s_te.count)
                P.op("scalar", lambda s, slot=slot: s.activation(
                    hs2b[:], y_in[:, slot, :], AF.Identity,
                    bias=lns[:, 6:7], scale=lns[:, 5:6]), s_hs3)

            b2_yload(0)
            b2_ln3(0)
            for t in range(NT):
                slot = t % 2
                P.wait("tensor", s_hs3, t + 1)
                if t == 0:
                    P.wait("tensor", s_bc, 2)
                b2swaits = [(s_fmm, t - 1)] if t >= 2 else []
                transposes([hs2b.ap()[:, 128 * c:128 * (c + 1)] for c in range(NCH)],
                           hsT.ap()[:, slot], swaits=b2swaits)
                if t > 0:
                    P.wait("tensor", s_z, t)
                for c in range(NCH + 1):
                    last = c == NCH
                    lhs = ones_r[:] if last else hsT[:, slot, c, :]
                    for wi in range(4):
                        rhs = (bias_pre[:, 512 * wi:512 * (wi + 1)] if last
                               else sT[:, c, 512 * wi:512 * (wi + 1)])
                        P.op("tensor", lambda t_, lhs=lhs, rhs=rhs, wi=wi, c=c, last=last:
                             t_.matmul(ps_a[:, 512 * wi:512 * (wi + 1)], lhs, rhs,
                                       start=(c == 0), stop=last),
                             s_fmm if (last and wi == 3) else None)
                if t + 1 < NT:
                    b2_yload(t + 1)
                P.wait("vector", s_fmm, t + 1)
                P.wait("vector", s_bc, 2)
                if t > 0:
                    P.wait("vector", s_ln4, t)
                P.op("vector", lambda v: v.tensor_tensor(z_sb[:], ps_a[:, 0:2048], alpha_bc[:], ALU.mult), s_z)
                if debug and t == 0:
                    P.wait("sync", s_z, 1)
                    P.dma("sync", dbg["z0"], z_sb[:], s_out)
                if t + 1 < NT:
                    b2_ln3(t + 1)
                P.wait("vector", s_z, t + 1)
                P.bar("vector")
                ln_stats(z_sb.ap(), lns[:, 16:17], lns[:, 17:18])
                P.wait("scalar", s_lnd)
                P.op("scalar", lambda s: s.activation(
                    ln4_sb[:], z_sb[:], AF.Identity, bias=lns[:, 17:18], scale=lns[:, 16:17]), s_ln4)
                P.wait("vector", s_ln4, t + 1)
                if t >= 1:
                    P.wait("vector", s_od[0], 16 * t)
                P.op("vector", lambda v, slot=slot: v.tensor_tensor(
                    out_sb[:], ln4_sb[:], y_in[:, slot, :], ALU.add), s_outv)
                P.wait("sync", s_outv, t + 1)
                P.dma("sync", outh.rearrange("(n p) d -> n p d", p=128)[t], out_sb[:], s_od[0])

            P.wait("gpsimd", s_od[0])
            if debug:
                P.wait("gpsimd", s_out)

        blk = es.enter_context(nc.Block())
        for s in P.sems:
            s.h = es.enter_context(nc.semaphore(s.name))

        @blk.tensor
        def _(e):
            for f in P.q["tensor"]:
                f(e)

        @blk.vector
        def _(e):
            for f in P.q["vector"]:
                f(e)

        @blk.scalar
        def _(e):
            for f in P.q["scalar"]:
                f(e)

        @blk.gpsimd
        def _(e):
            for f in P.q["gpsimd"]:
                f(e)

        @blk.sync
        def _(e):
            for f in P.q["sync"]:
                f(e)

    return nc, dbg


_NC_CACHE = {}


def _get_nc(debug=False):
    key = bool(debug)
    if key not in _NC_CACHE:
        _NC_CACHE[key] = build_nc(debug)
    return _NC_CACHE[key]


def make_in_maps(inputs):
    x = np.ascontiguousarray(np.asarray(inputs["L0_attn_hidden_states"], np.float32).reshape(8192, HID))
    kc = np.asarray(inputs["L0_attn_Wk"], np.float32).T.reshape(NCH, 128, 512)
    vc = np.asarray(inputs["L0_attn_Wv"], np.float32).T.reshape(NCH, 128, 512)
    mc = np.asarray(inputs["L0_attn_Wmb"], np.float32).T.reshape(NCH, 128, HID)
    qc = np.asarray(inputs["L0_attn_Wq"], np.float32).T.reshape(NCH, 128, HID)
    mem = np.asarray(inputs["L0_attn_memory"], np.float32)
    com = {
        "WfT": np.ascontiguousarray(np.asarray(inputs["L0_ffn_weight"], np.float32).T),
        "WgP": np.ascontiguousarray(
            np.asarray(inputs["L0_attn_Wg"], np.float32).T.reshape(NCH, 128, HID)
            .transpose(1, 0, 2)),
        "memP4": np.ascontiguousarray(
            mem.reshape(2, 2, 4, 6, 128, 128).transpose(0, 1, 4, 3, 2, 5)
            .reshape(4, 128, 6, 512)),
        "memPh": np.ascontiguousarray(mem.reshape(16, 6, 128, 128).transpose(0, 2, 1, 3)),
        "bk": np.asarray(inputs["L0_attn_bk"], np.float32).reshape(1, 512),
        "bv": np.asarray(inputs["L0_attn_bv"], np.float32).reshape(1, 512),
        "bm": np.asarray(inputs["L0_attn_bmb"], np.float32).reshape(1, HID),
        "bq": np.asarray(inputs["L0_attn_bq"], np.float32).reshape(1, HID),
        "bg": np.asarray(inputs["L0_attn_bg"], np.float32).reshape(1, HID),
        "bf": np.asarray(inputs["L0_ffn_bias"], np.float32).reshape(1, HID),
        "norm96": np.ascontiguousarray(np.asarray(inputs["L0_attn_norm"], np.float32).reshape(96, 128)),
        "ident": np.eye(128, dtype=np.float32),
    }
    for h in range(2):
        com[f"Wa{h}"] = np.ascontiguousarray(np.concatenate(
            [kc[:, :, 256 * h:256 * (h + 1)], vc[:, :, 256 * h:256 * (h + 1)],
             mc[:, :, 1024 * h:1024 * (h + 1)], qc[:, :, 1024 * h:1024 * (h + 1)]],
            axis=2).transpose(1, 0, 2))
    in_maps = []
    for c in range(NCORE):
        m = dict(com)
        m["x_part"] = np.ascontiguousarray(x[c * TOK:(c + 1) * TOK])
        in_maps.append(m)
    return in_maps


def kernel(**inputs):
    nc, _ = _get_nc(False)
    in_maps = make_in_maps(inputs)
    res = run_bass_kernel_spmd(nc, in_maps, core_ids=list(range(NCORE)))
    out = np.concatenate([r["out_part"] for r in res.results], axis=0)
    return out.reshape(2, 4096, HID)


# version 05
# speedup vs baseline: 1.1300x; 1.1001x over previous
import contextlib
import numpy as np
import concourse.bass as bass
import concourse.mybir as mybir
from concourse.bass_utils import run_bass_kernel_spmd

F32 = mybir.dt.float32
BF16 = mybir.dt.bfloat16
AF = mybir.ActivationFunctionType
ALU = mybir.AluOpType
AX = mybir.AxisListType

NCORE = 8
TOK = 1024
NT = 8
HID = 2048
NCH = 16
D = 128
H = 16
FD = 768
EPS = 1e-8
LNEPS = 1e-5


class Sem:
    __slots__ = ("name", "count", "h")

    def __init__(self, name):
        self.name = name
        self.count = 0
        self.h = None

    def bump(self, n=1):
        self.count += n
        return self.count


class Prog:
    ENGINES = ("tensor", "vector", "scalar", "gpsimd", "sync")

    def __init__(self):
        self.q = {e: [] for e in self.ENGINES}
        self.sems = []
        self.barsems = {}

    def sem(self, name):
        s = Sem(name)
        self.sems.append(s)
        return s

    def emit(self, eng, fn):
        self.q[eng].append(fn)

    def wait(self, eng, sem, val=None):
        v = sem.count if val is None else val
        if v <= 0:
            return
        self.emit(eng, lambda e, s=sem, vv=v: e.wait_ge(s.h, vv))

    def op(self, eng, builder, sem=None, inc=1):
        if sem is not None:
            sem.bump(inc)
        extra = []

        def f(e, b=builder, s=sem, n=inc, x=extra):
            inst = b(e)
            if s is not None:
                inst.then_inc(s.h, n)
            for (xs, xn) in x:
                inst.then_inc(xs.h, xn)

        self.emit(eng, f)
        self.lastop = getattr(self, 'lastop', {})
        self.lastop[eng] = (extra, sem, sem.count if sem is not None else 0)

    def nop(self, eng):
        self.emit(eng, lambda e: e.nop())

    def drain(self, eng):
        self.emit(eng, lambda e: e.drain())

    def bar(self, eng):
        extra, psem, pval = self.lastop[eng]
        if psem is not None:
            self.emit(eng, lambda e, ss=psem, v=pval: e.wait_ge(ss.h, v))
            return
        s = self.barsems.get(eng)
        if s is None:
            s = self.sem(f"bar_{eng}")
            self.barsems[eng] = s
        c = s.bump(1)
        extra.append((s, 1))
        self.emit(eng, lambda e, ss=s, v=c: e.wait_ge(ss.h, v))

    def dma(self, qeng, out_ap, in_ap, sem, slow=False):
        sem.bump(16)
        self.emit(qeng, lambda e, o=out_ap, i=in_ap, s=sem, sl=slow:
                  e.dma_start(out=o, in_=i, allow_slow_non_contiguous=sl).then_inc(s.h, 16))


def build_nc(debug=False):
    nc = bass.Bass("TRN2", target_bir_lowering=False)
    P = Prog()

    xh = nc.dram_tensor("x_part", [TOK, HID], F32, kind="ExternalInput").ap()
    waH = [nc.dram_tensor(f"Wa{h}", [128, NCH, 2560], F32, kind="ExternalInput").ap()
           for h in range(2)]
    wgP = nc.dram_tensor("WgP", [128, NCH, HID], F32, kind="ExternalInput").ap()
    memP4h = nc.dram_tensor("memP4", [4, 128, 6, 512], F32, kind="ExternalInput").ap()
    memPh = nc.dram_tensor("memPh", [16, 128, 6, 128], F32, kind="ExternalInput").ap()
    wfT = nc.dram_tensor("WfT", [HID, HID], F32, kind="ExternalInput").ap()
    bk = nc.dram_tensor("bk", [1, 512], F32, kind="ExternalInput").ap()
    bv = nc.dram_tensor("bv", [1, 512], F32, kind="ExternalInput").ap()
    bm = nc.dram_tensor("bm", [1, HID], F32, kind="ExternalInput").ap()
    bq = nc.dram_tensor("bq", [1, HID], F32, kind="ExternalInput").ap()
    bg = nc.dram_tensor("bg", [1, HID], F32, kind="ExternalInput").ap()
    bfh = nc.dram_tensor("bf", [1, HID], F32, kind="ExternalInput").ap()
    normh = nc.dram_tensor("norm96", [96, 128], F32, kind="ExternalInput").ap()
    identh = nc.dram_tensor("ident", [128, 128], F32, kind="ExternalInput").ap()
    outh = nc.dram_tensor("out_part", [TOK, HID], F32, kind="ExternalOutput").ap()

    q_hbm = nc.dram_tensor("q_scr", [NT, 128, HID], BF16).ap()
    ht_hbm = nc.dram_tensor("ht_scr", [NT, 128, NCH * 128], BF16).ap()
    y_hbm = nc.dram_tensor("y_scr", [NT, 128, HID], F32).ap()
    d_loc = [nc.dram_tensor(f"d_loc{i}", [6, 128, 1032], F32).ap() for i in range(2)]
    d_red = [nc.dram_tensor(f"d_red{i}", [6, 128, 1032], F32, addr_space="Shared").ap() for i in range(2)]

    dbg = {}
    if debug:
        def dbgt(name, shape, dt=F32):
            dbg[name] = nc.dram_tensor("dbg_" + name, shape, dt, kind="ExternalOutput").ap()
        dbgt("hs0", [128, HID], BF16)
        dbgt("msb", [128, 16])
        dbgt("lns0", [128, 28])
        dbgt("bn0", [128, 24])
        dbgt("x0", [128, HID])
        dbgt("lsnap", [128, 28])
        dbgt("kv0", [128, 512], BF16)
        dbgt("mb0", [128, 1024], BF16)
        dbgt("q0", [128, 1024], BF16)
        dbgt("F0", [128, 2, FD], BF16)
        dbgt("nst0", [128, 24])
        dbgt("wmv0", [128, 8, 128], BF16)
        dbgt("coefn0", [128, 8], BF16)
        dbgt("memN", [128, 4, 6, 512], BF16)
        dbgt("normN", [128, 4, 6, 4], BF16)
        dbgt("mhg0", [128, HID])
        dbgt("gate0", [128, HID])
        dbgt("y0", [128, HID])
        dbgt("alpha", [1, HID])
        dbgt("z0", [128, HID])

    xt = xh.rearrange("(n p) d -> n p d", p=128)

    es = contextlib.ExitStack()
    with es:
        def sb(name, shape, dt):
            return es.enter_context(nc.sbuf_tensor(name, shape, dt))

        def pst(name, shape, dt):
            return es.enter_context(nc.psum_tensor(name, shape, dt))

        ps_a = pst("ps_a", [128, 2560], F32)
        ps_n = pst("ps_n", [128, 512], F32)
        ps_dot = pst("ps_dot", [128, 8], F32)
        ps_tr = pst("ps_tr", [128, 8, 128], BF16)

        ident_bf = sb("ident_bf", [128, 128], BF16)
        ident_f = sb("ident_f", [128, 128], F32)
        ones_r = sb("ones_r", [1, 128], BF16)
        ones_rf = sb("ones_rf", [1, 128], F32)
        ones_c = sb("ones_c", [128, 1], BF16)
        normF = sb("normF", [128, 96], F32)
        msb = sb("msb", [128, 16], F32)
        bn24 = sb("bn24", [128, 24], F32)
        lns = sb("lns", [128, 28], F32)
        sq_scr = sb("sq_scr", [128, 2, FD], F32)
        sstat = sb("sstat", [128, 24], F32)
        mv_scr = sb("mv_scr", [128, 4, 128], F32)
        lsnap = sb("lsnap", [128, 28], F32)
        hsb_sb = sb("hsb_sb", [128, HID], BF16)
        hsT = sb("hsT", [128, 2, NCH, 128], BF16)
        memN_a = sb("memN_a", [128, 2, 6, 512], BF16)
        memNs = [memN_a, None]
        normN = sb("normN", [128, 4, 6, 4], BF16)
        mstage = sb("mstage", [128, 1, 6, 128], F32)
        dstage2 = sb("dstage2", [128, 1, 6, 129], F32)
        norm96 = sb("norm96s", [96, 128], F32)
        x_es = contextlib.ExitStack()
        x_sb = x_es.enter_context(nc.sbuf_tensor("x_sb", [128, 2, HID], F32))

        s_const = P.sem("const")
        s_konst = P.sem("konst")
        s_ntr = P.sem("ntr")
        s_endA1 = P.sem("endA1")
        s_endA2 = P.sem("endA2")
        s_endB1 = P.sem("endB1")
        s_cc = [P.sem("cc0"), P.sem("cc1")]
        s_out = P.sem("out")
        s_x = [P.sem("x_s0"), P.sem("x_s1")]
        s_ht = [P.sem("ht_s0"), P.sem("ht_s1")]
        s_htw = [P.sem("htw_s0"), P.sem("htw_s1")]
        s_hs = P.sem("hs")
        s_hsb = P.sem("hsb")
        s_tr = P.sem("tr")
        s_te = P.sem("te")
        s_lnv = P.sem("lnv")
        s_lsn = P.sem("lsn")
        s_wg = P.sem("wg")
        s_mst = P.sem("mst")
        s_memN = P.sem("memN")
        s_lnq = P.sem("lnq")
        s_lnd = P.sem("lnd")

        P.dma("gpsimd", ident_bf[:], identh, s_const)
        P.dma("gpsimd", ident_f[:], identh, s_const)
        P.op("vector", lambda v: v.memset(ones_r[:], 1.0), s_konst)
        P.op("vector", lambda v: v.memset(ones_rf[:], 1.0), s_konst)
        P.op("vector", lambda v: v.memset(ones_c[:], 1.0), s_konst)
        P.op("vector", lambda v: v.memset(lns[:], 0.0), s_konst)
        P.op("vector", lambda v: v.memset(bn24[:], 0.0), s_konst)
        P.op("vector", lambda v: v.memset(sstat[:], 0.0), s_konst)
        P.op("vector", lambda v: v.memset(msb[:], 0.0), s_konst)

        def ln_stats(src_ap, rstd_col, negms_col):
            for i in range(4):
                P.op("vector", lambda v, i=i, a=src_ap: v.bn_stats(
                    bn24[:, 6 * i:6 * i + 6], a[:, 512 * i:512 * (i + 1)]))
            m8 = bn24.ap()[:, 1:24:3]
            M28 = bn24.ap()[:, 2:24:3]
            P.bar("vector")
            P.op("vector", lambda v, m8=m8: v.tensor_tensor(lns[:, 8:16], m8, m8, ALU.mult))
            P.op("vector", lambda v, m8=m8: v.reduce_sum(lns[:, 1:2], m8, axis=AX.X))
            P.bar("vector")
            P.op("vector", lambda v, M28=M28: v.scalar_tensor_tensor(
                lns[:, 20:28], lns[:, 8:16], 256.0, M28, ALU.mult, ALU.add))
            P.op("vector", lambda v: v.tensor_scalar(
                lns[:, 2:3], lns[:, 1:2], 0.125, None, ALU.mult))
            P.bar("vector")
            P.op("vector", lambda v: v.reduce_sum(lns[:, 0:1], lns[:, 20:28], axis=AX.X))
            P.op("vector", lambda v: v.tensor_tensor(
                lns[:, 3:4], lns[:, 2:3], lns[:, 2:3], ALU.mult))
            P.bar("vector")
            P.op("vector", lambda v: v.tensor_scalar(
                lns[:, 4:5], lns[:, 0:1], 1.0 / HID, LNEPS, ALU.mult, ALU.add))
            P.bar("vector")
            P.op("vector", lambda v: v.tensor_tensor(
                lns[:, 18:19], lns[:, 4:5], lns[:, 3:4], ALU.subtract))
            P.bar("vector")
            P.op("vector", lambda v: v.reciprocal(lns[:, 19:20], lns[:, 18:19]), s_lnv)
            P.wait("scalar", s_lnv)
            P.op("scalar", lambda s, rc=rstd_col: s.activation(rc, lns[:, 19:20], AF.Sqrt), s_lnq)
            P.wait("vector", s_lnq)
            P.op("vector", lambda v, rc=rstd_col, nm=negms_col: v.scalar_tensor_tensor(
                nm, lns[:, 2:3], -1.0, rc, ALU.mult, ALU.mult), s_lnd)

        def transposes(srcs, dst, swaits=()):
            n = len(srcs)
            r0 = 0
            while r0 < n:
                rn = min(8, n - r0)
                P.wait("tensor", s_te)
                for i in range(rn):
                    P.op("tensor", lambda t, i=i, a=srcs[r0 + i]: t.transpose(
                        ps_tr[:, i, :], a, ident_bf[:]), s_tr if i == rn - 1 else None)
                P.wait("scalar", s_tr)
                for sw in swaits:
                    P.wait("scalar", sw[0], sw[1])
                swaits = ()
                P.op("scalar", lambda s, r0=r0, rn=rn, d=dst: s.copy(
                    d[:, r0:r0 + rn, :], ps_tr[:, 0:rn, :]), s_te)
                r0 += rn
            P.wait("tensor", s_te)
            return s_te.count

        def build_head_dma(hh):
            P.wait("gpsimd", s_cc[hh // 8], 1)
            if hh >= 1:
                P.wait("gpsimd", s_memN, hh)
            hl = hh % 8
            P.dma("gpsimd", mstage[:, 0, :, :], memPh[hh], s_mst)
            P.dma("gpsimd", dstage2[:, 0, :, 0:128],
                  d_red[hh // 8][:, :, 128 * hl:128 * (hl + 1)].rearrange("c p d -> p c d"),
                  s_mst)
            P.dma("gpsimd", dstage2[:, 0, :, 128:129],
                  d_red[hh // 8][:, :, 1024 + hl:1025 + hl].rearrange("c p d -> p c d"),
                  s_mst, slow=True)

        def build_head_v(hh):
            P.wait("vector", s_mst, 48 * (hh + 1))
            P.op("vector", lambda v, hh=hh: v.tensor_tensor(
                memNs[hh // 8][:, (hh // 4) % 2, :, 128 * (hh % 4):128 * (hh % 4 + 1)],
                mstage[:, 0, :, :], dstage2[:, 0, :, 0:128], ALU.add))
            P.op("vector", lambda v, hh=hh: v.tensor_tensor(
                normN[:, hh // 4, :, (hh % 4):(hh % 4) + 1],
                normF.ap()[:, 6 * hh:6 * (hh + 1)].rearrange("p (c o) -> p c o", o=1),
                dstage2[:, 0, :, 128:129], ALU.add), s_memN)

        P.dma("gpsimd", norm96[:], normh, s_const)
        P.wait("tensor", s_const)
        P.wait("tensor", s_konst)
        P.op("tensor", lambda t: t.transpose(
            ps_n.ap()[:, 0:96], norm96[0:96, :], ident_f[0:96, 0:96]), s_ntr)
        P.wait("scalar", s_ntr)
        P.op("scalar", lambda s: s.copy(normF[:], ps_n.ap()[:, 0:96]), s_ntr)

        wg_pre = [None, None]
        esA = contextlib.ExitStack()
        with esA:
            def sbA(name, shape, dt):
                return esA.enter_context(nc.sbuf_tensor(name, shape, dt))

            wA = sbA("wA", [128, NCH, 2560], BF16)
            bias_kvmb = sbA("bias_kvmb", [1, 1536], BF16)
            bias_q = sbA("bias_q", [1, 1024], BF16)
            memA = sbA("memA", [128, 2, 6, 512], BF16)
            normA = sbA("normA", [128, 2, 6, 4], BF16)
            kv_bf = sbA("kv_bf", [128, 512], BF16)
            mb_bf = sbA("mb_bf", [128, 1024], BF16)
            q_bf = sbA("q_bf", [128, 1, 1024], BF16)
            xe2 = sbA("xe2", [128, 2, 262], BF16)
            F_sb = sbA("F_sb", [128, NT, 2, FD], BF16)
            fT = sbA("fT", [128, 12, 128], BF16)
            wmv = sbA("wmv", [128, NT, 8, 128], BF16)
            coefn = sbA("coefn", [128, NT, 8], BF16)
            dstage = sbA("dstage", [128, 1, 1032], F32)

            s_q = [P.sem("q_s0")]
            half_sems = {}

            s_w4 = [P.sem(f"w4_{i}") for i in range(4)]
            for half in range(2):
                s_wA = P.sem(f"wA{half}")
                half_sems[half] = {}
                s_nA = P.sem(f"nA{half}")
                s_proj = P.sem(f"proj{half}")
                s_ev1 = P.sem(f"ev1_{half}")
                s_F = P.sem(f"F{half}")
                s_relu = P.sem(f"relu{half}")
                s_ssq = P.sem(f"ssq{half}")
                s_num = P.sem(f"num{half}")
                s_chain = P.sem(f"chain{half}")
                s_stats = P.sem(f"stats{half}")
                half_sems[half]["proj"] = s_proj
                half_sems[half]["num"] = s_num
                half_sems[half]["chain"] = s_chain

                s_p7 = P.sem(f"p7_{half}")
                s_wB = P.sem(f"wB{half}")
                s_mA = P.sem(f"mA{half}")
                half_sems[half]["p7"] = s_p7
                if half == 0:
                    P.dma("gpsimd", bias_kvmb[:, 0:256], bk[:, 0:256], s_wB)
                    P.dma("gpsimd", bias_kvmb[:, 256:512], bv[:, 0:256], s_wB)
                    P.dma("gpsimd", bias_kvmb[:, 512:1536], bm[:, 0:1024], s_wB)
                    P.dma("gpsimd", bias_q[:], bq[:, 0:1024], s_wB)
                for c in range(NCH):
                    if half == 1:
                        P.wait("gpsimd", half_sems[0]["p7"], c + 1)
                    P.dma("gpsimd", wA[:, c, :], waH[half][:, c, :], s_w4[c // 4])
                if half == 1:
                    P.wait("gpsimd", half_sems[0]["proj"], 8)
                    P.dma("gpsimd", bias_kvmb[:, 0:256], bk[:, 256:512], s_wB)
                    P.dma("gpsimd", bias_kvmb[:, 256:512], bv[:, 256:512], s_wB)
                    P.dma("gpsimd", bias_kvmb[:, 512:1536], bm[:, 1024:2048], s_wB)
                    P.dma("gpsimd", bias_q[:], bq[:, 1024:2048], s_wB)
                for i in range(4):
                    P.wait("gpsimd", s_w4[i], 64 * (half + 1))
                P.op("gpsimd", lambda g: g.nop(), s_wA)
                if half == 1:
                    P.wait("gpsimd", half_sems[0]["num"], 32)
                for j in range(2):
                    for c6 in range(6):
                        P.dma("gpsimd", memA[:, j, c6, :], memP4h[2 * half + j][:, c6, :], s_mA)
                if half == 1:
                    for hh in range(0, 8):
                        build_head_dma(hh)
                P.wait("vector", s_ntr, 2)
                if half == 1:
                    P.wait("vector", half_sems[0]["num"], 32)
                P.op("vector", lambda v, half=half: v.tensor_copy(
                    normA[:],
                    normF.ap()[:, 48 * half:48 * (half + 1)]
                    .rearrange("p (j hl c) -> p j c hl", j=2, hl=4, c=6)), s_nA)

                if half == 0:
                    P.dma("sync", x_sb[:, 0, :], xt[0], s_x[0])
                else:
                    P.wait("sync", s_htw[0])
                    P.dma("sync", hsT[:, 0], ht_hbm[0], s_ht[0])
                    P.wait("sync", s_htw[1])
                    P.dma("sync", hsT[:, 1], ht_hbm[1], s_ht[1])
                for t in range(NT):
                    gt = 8 * half + t
                    slot = t % 2
                    if half == 0:
                        if t + 1 < NT:
                            if debug and t + 1 == 2:
                                P.wait("sync", s_out, s_out.count)
                            if t + 1 >= 2:
                                P.wait("sync", s_hs, t)
                            P.dma("sync", x_sb[:, (t + 1) % 2, :], xt[t + 1], s_x[(t + 1) % 2])
                        if debug and gt == 1:
                            P.wait("vector", s_out, s_out.count)
                            P.wait("scalar", s_out, s_out.count)
                        P.wait("vector", s_x[slot], 16 * (gt // 2 + 1))
                        ln_stats(x_sb.ap()[:, slot, :], msb[:, 2 * t:2 * t + 1],
                                 msb[:, 2 * t + 1:2 * t + 2])
                        if debug and gt == 0:
                            P.bar("vector")
                            P.op("vector", lambda v: v.tensor_copy(lsnap[:], lns[:, 0:28]), s_lsn)
                        P.op("vector", lambda v: v.nop(), s_stats)
                        P.wait("scalar", s_x[slot], 16 * (gt // 2 + 1))
                        P.wait("scalar", s_lnd, s_lnd.count)
                        if gt >= 1:
                            P.wait("scalar", s_te, s_te.count)
                        P.op("scalar", lambda s, slot=slot, t=t: s.activation(
                            hsb_sb[:], x_sb[:, slot, :], AF.Identity,
                            bias=msb[:, 2 * t + 1:2 * t + 2], scale=msb[:, 2 * t:2 * t + 1]), s_hs)
                        if debug and gt == 0:
                            P.wait("sync", s_hs, 1)
                            P.dma("sync", dbg["hs0"], hsb_sb[:], s_out)
                            P.dma("sync", dbg["msb"], msb[:, 0:16], s_out)
                            P.dma("sync", dbg["lns0"], lns[:, 0:28], s_out)
                            P.dma("sync", dbg["bn0"], bn24[:, 0:24], s_out)
                            P.dma("sync", dbg["x0"], x_sb[:, 0, :], s_out)
                            P.wait("sync", s_lsn, 1)
                            P.dma("sync", dbg["lsnap"], lsnap[:], s_out)
                        P.wait("tensor", s_hs, gt + 1)
                        htswaits = []
                        if t >= 2:
                            htswaits = [(s_proj, t - 1), (s_htw[slot], 16 * (t // 2))]
                        transposes([hsb_sb.ap()[:, 128 * c:128 * (c + 1)] for c in range(NCH)],
                                   hsT.ap()[:, slot], swaits=htswaits)
                        P.wait("sync", s_te, s_te.count)
                        P.dma("sync", ht_hbm[t], hsT[:, slot], s_htw[slot])
                    else:
                        if 2 <= t + 1 < NT:
                            P.wait("sync", s_proj, t)
                            P.dma("sync", hsT[:, (t + 1) % 2], ht_hbm[t + 1], s_ht[(t + 1) % 2])
                        if t == 0:
                            P.wait("tensor", half_sems[0]["dev"], 6)
                        P.wait("tensor", s_ht[slot], 16 * (t // 2 + 1))
                    if t > 0:
                        P.wait("tensor", s_wA, 1)
                        P.wait("tensor", s_ev1, 2 * t)
                    wins = [(0, 512, 0), (512, 1024, 0), (1024, 1536, 0),
                            (1536, 2048, 1), (2048, 2560, 1)]
                    for c in range(NCH + 1):
                        last = c == NCH
                        lhs = ones_r[:] if last else hsT[:, slot, c, :]
                        if t == 0:
                            if last:
                                P.wait("tensor", s_wB)
                            else:
                                P.wait("tensor", s_w4[c // 4], 64 * (half + 1))
                        for wi, (a, b, isq) in enumerate(wins):
                            if last:
                                rhs = bias_q[:, a - 1536:b - 1536] if isq else bias_kvmb[:, a:b]
                            else:
                                rhs = wA[:, c, a:b]
                            psem = None
                            if last and wi == 4:
                                psem = s_proj
                            elif t == NT - 1 and wi == 4:
                                psem = s_p7
                            P.op("tensor", lambda t_, lhs=lhs, rhs=rhs, a=a, b=b, c=c, last=last:
                                 t_.matmul(ps_a[:, a:b], lhs, rhs, start=(c == 0), stop=last),
                                 psem)
                    P.wait("scalar", s_proj, t + 1)
                    if t > 0:
                        P.wait("scalar", s_chain, 2 * t)
                    if half == 1 and t == 0:
                        P.wait("scalar", s_endA1)
                        P.wait("scalar", half_sems[0]["dwr"][0], 96)
                    P.op("scalar", lambda s: s.copy(kv_bf[:], ps_a[:, 0:512]))
                    P.op("scalar", lambda s: s.activation(mb_bf[:], ps_a[:, 512:1536], AF.Sigmoid), s_ev1)
                    P.wait("scalar", s_q[0], 16 * gt)
                    P.op("scalar", lambda s: s.copy(q_bf[:, 0, :], ps_a[:, 1536:2560]), s_ev1)
                    P.wait("sync", s_ev1, 2 * (t + 1))
                    P.dma("sync", q_hbm[t, :, 1024 * half:1024 * (half + 1)], q_bf[:, 0, :], s_q[0])
                    if debug and gt == 0:
                        P.dma("sync", dbg["kv0"], kv_bf[:], s_out)
                        P.dma("sync", dbg["mb0"], mb_bf[:], s_out)
                        P.dma("sync", dbg["q0"], q_bf[:, 0, :], s_out)
                    if t > 0:
                        P.wait("scalar", s_F, t)
                    P.wait("scalar", s_ev1, 2 * (t + 1))
                    kview = kv_bf.ap().rearrange("p (j d) -> p j d", j=4)
                    P.op("scalar", lambda s, kview=kview: s.activation(
                        xe2[:, :, 3:131], kview[:, 0:2, :], AF.Relu))
                    P.op("scalar", lambda s, kview=kview: s.activation(
                        xe2[:, :, 131:259], kview[:, 0:2, :], AF.Relu, scale=-1.0), s_relu)
                    P.wait("vector", s_relu, t + 1)
                    P.op("vector", lambda v: v.tensor_copy(xe2[:, :, 0:3], xe2[:, :, 256:259]))
                    P.bar("vector")
                    if half == 1 and t == 0:
                        P.wait("vector", s_endA1)
                    for j in (1, 2, 3):
                        P.op("vector", lambda v, j=j, t=t: v.tensor_tensor(
                            F_sb[:, t, :, 256 * (j - 1):256 * j], xe2[:, :, 3:259],
                            xe2[:, :, 3 - j:259 - j], ALU.mult), s_F if j == 3 else None)
                    if debug and gt == 0:
                        P.wait("sync", s_F, 1)
                        P.dma("sync", dbg["F0"], F_sb[:, 0, :, :], s_out)
                    P.wait("scalar", s_F, t + 1)
                    if t > 0:
                        P.wait("scalar", s_chain, 2 * t)
                    for j in range(2):
                        P.op("scalar", lambda s, j=j, t=t: s.activation(
                            sq_scr[:, j, :], F_sb[:, t, j, :], AF.Square, accum_out=sstat[:, j:j + 1]))
                    P.bar("scalar")
                    P.op("scalar", lambda s: s.activation(sstat[:, 2:4], sstat[:, 0:2], AF.Sqrt), s_ssq)
                    P.wait("vector", s_ssq, 2 * t + 1)
                    P.op("vector", lambda v: v.tensor_scalar(
                        sstat[:, 22:24], sstat[:, 2:4], EPS, None, ALU.mult))
                    P.op("vector", lambda v: v.reciprocal(sstat[:, 4:6], sstat[:, 2:4]), s_ssq)
                    P.bar("vector")
                    P.wait("tensor", s_F, t + 1)
                    fswaits = [(s_num, 4 * t)] if t > 0 else []
                    transposes([F_sb.ap()[:, t, j, 128 * c:128 * (c + 1)]
                                for j in range(2) for c in range(6)], fT, swaits=fswaits)
                    P.wait("tensor", s_nA)
                    if t == 0:
                        P.wait("tensor", s_mA)
                    if half == 0 and t == 0:
                        P.wait("tensor", s_ntr, 2)
                    for j in range(2):
                        if t > 0 or j > 0:
                            P.wait("tensor", s_chain, 2 * t + j)
                        for c in range(6):
                            P.op("tensor", lambda t_, j=j, c=c: t_.matmul(
                                ps_n[:], fT[:, 6 * j + c, :], memA[:, j, c, :],
                                start=(c == 0), stop=(c == 5)), s_num if c == 5 else None)
                        for c in range(6):
                            P.op("tensor", lambda t_, j=j, c=c: t_.matmul(
                                ps_dot[:, 4 * j:4 * j + 4], fT[:, 6 * j + c, :], normA[:, j, c, :],
                                start=(c == 0), stop=(c == 5)), s_num if c == 5 else None)
                        P.wait("vector", s_num, 4 * t + 2 * (j + 1))
                        P.wait("vector", s_ssq, 2 * (t + 1))
                        invn = sstat.ap()[:, 4 + j:5 + j]
                        epsn = sstat.ap()[:, 22 + j:23 + j]
                        den = sstat.ap()[:, 6:10]
                        P.op("vector", lambda v, j=j, epsn=epsn, den=den: v.tensor_scalar(
                            den, ps_dot[:, 4 * j:4 * j + 4], 1.0, epsn, ALU.mult, ALU.add))
                        P.bar("vector")
                        P.op("vector", lambda v, den=den: v.reciprocal(sstat[:, 10:14], den))
                        P.bar("vector")
                        P.op("vector", lambda v: v.tensor_scalar(
                            sstat[:, 14:18], sstat[:, 10:14], -1.0, None, ALU.mult))
                        P.bar("vector")
                        def stt_mv(hl, j=j, t=t):
                            P.op("vector", lambda v, hl=hl, j=j: v.scalar_tensor_tensor(
                                mv_scr[:, hl, :], ps_n[:, 128 * hl:128 * (hl + 1)],
                                sstat[:, 14 + hl:15 + hl],
                                kv_bf[:, 256 + 128 * j:384 + 128 * j], ALU.mult, ALU.add))
                        def stt_wmv(hl, j=j, t=t, invn=invn):
                            h8 = 4 * j + hl
                            P.op("vector", lambda v, hl=hl, h8=h8, t=t, invn=invn:
                                 v.scalar_tensor_tensor(
                                wmv[:, t, h8, :], mv_scr[:, hl, :], invn,
                                mb_bf[:, 128 * h8:128 * (h8 + 1)], ALU.mult, ALU.mult))
                        stt_mv(0)
                        stt_mv(1)
                        stt_mv(2)
                        stt_mv(3)
                        P.op("vector", lambda v, den=den, invn=invn: v.tensor_scalar(
                            sstat[:, 18:22], den, invn, None, ALU.mult))
                        P.bar("vector")
                        stt_wmv(0)
                        stt_wmv(1)
                        stt_wmv(2)
                        stt_wmv(3)
                        P.op("vector", lambda v: v.tensor_scalar(
                            sstat[:, 14:18], sstat[:, 18:22], -1.0, 1.0, ALU.mult, ALU.add))
                        P.bar("vector")
                        P.op("vector", lambda v: v.tensor_scalar(
                            sstat[:, 18:22], sstat[:, 14:18], 0.0, 1.0, ALU.max, ALU.min))
                        P.bar("vector")
                        P.op("vector", lambda v, t=t, j=j, invn=invn: v.tensor_scalar(
                            coefn[:, t, 4 * j:4 * j + 4], sstat[:, 18:22], invn,
                            None, ALU.mult), s_chain)
                    if debug and gt == 0:
                        P.wait("sync", s_chain, 2)
                        P.dma("sync", dbg["nst0"], sstat[:, 0:24], s_out)
                        P.dma("sync", dbg["wmv0"], wmv[:, 0, :, :], s_out)
                        P.dma("sync", dbg["coefn0"], coefn[:, 0, :], s_out)
                    if half == 1 and 2 <= t <= 5:
                        for hh in (2 * (t - 2), 2 * (t - 2) + 1):
                            build_head_v(hh)

                P.wait("tensor", s_chain)
                P.wait("tensor", s_ev1)
                s_dmm = P.sem(f"dmm{half}")
                s_dev = P.sem(f"dev{half}")
                s_dwr = [P.sem(f"dwr{half}_0"), P.sem(f"dwr{half}_1")]
                half_sems[half]["dev"] = s_dev
                half_sems[half]["dwr"] = s_dwr
                for c in range(6):
                    reg = (c % 2) * 1024
                    if c >= 1:
                        P.wait("tensor", s_dev, c)
                    for tt in range(NT):
                        for j in range(2):
                            P.op("tensor", lambda t_, c=c, tt=tt, j=j, reg=reg: t_.matmul(
                                ps_a[:, reg + 512 * j:reg + 512 * (j + 1)],
                                F_sb[:, tt, j, 128 * c:128 * (c + 1)],
                                wmv[:, tt, 4 * j:4 * j + 4, :],
                                start=(tt == 0), stop=(tt == NT - 1)), None)
                            dps = ps_dot[:, 0:4] if j == 0 else ps_n.ap()[:, 0:4]
                            P.op("tensor", lambda t_, c=c, tt=tt, j=j, dps=dps: t_.matmul(
                                dps, F_sb[:, tt, j, 128 * c:128 * (c + 1)],
                                coefn[:, tt, 4 * j:4 * j + 4],
                                start=(tt == 0), stop=(tt == NT - 1)),
                                s_dmm if (tt == NT - 1 and j == 1) else None)
                    P.wait("scalar", s_dmm, c + 1)
                    if c >= 1:
                        P.wait("scalar", s_dwr[0], 16 * c)
                    P.op("scalar", lambda s, c=c, reg=reg: s.copy(
                        dstage[:, 0, 0:1024], ps_a[:, reg:reg + 1024]))
                    P.op("scalar", lambda s, c=c: s.copy(dstage[:, 0, 1024:1028], ps_dot[:, 0:4]))
                    P.op("scalar", lambda s, c=c: s.copy(dstage[:, 0, 1028:1032], ps_n.ap()[:, 0:4]), s_dev)
                    P.wait("sync", s_dev, c + 1)
                    P.dma("sync", d_loc[half][c], dstage[:, 0, :], s_dwr[0])
                if half == 1:
                    P.wait("gpsimd", s_proj, 8)
                    s_wg.bump(16)
                    P.emit("gpsimd", lambda e: e.dma_start(
                        out=wg_pre[0][:], in_=wgP).then_inc(s_wg.h, 16))
                    s_wg.bump(16)
                    P.emit("gpsimd", lambda e: e.dma_start(
                        out=wg_pre[1][:], in_=bg).then_inc(s_wg.h, 16))
                P.wait("gpsimd", s_dwr[0])
                P.wait("gpsimd", s_dwr[1])
                P.op("gpsimd", lambda g, half=half: g.collective_compute(
                    "AllReduce", ALU.add, replica_groups=[list(range(NCORE))],
                    ins=[d_loc[half]], outs=[d_red[half]]), s_cc[half])
                P.op("tensor", lambda t: t.nop(), s_endA1 if half == 0 else s_endA2)
                if half == 1:
                    P.wait("sync", s_q[0])
                    P.op("sync", lambda e: e.nop(), s_endA2)

        esB = contextlib.ExitStack()
        with esB:
            def sbB(name, shape, dt):
                return esB.enter_context(nc.sbuf_tensor(name, shape, dt))

            wg_sb = sbB("wg_sb", [128, NCH, HID], BF16)
            bias_g = sbB("bias_g", [1, HID], BF16)
            wg_pre[0] = wg_sb
            wg_pre[1] = bias_g
            memN_b = sbB("memN_b", [128, 2, 6, 512], BF16)
            memNs[1] = memN_b
            q_in = sbB("q_in", [128, 2, HID], BF16)
            xe4 = sbB("xe4", [128, 2, 4, 262], BF16)
            Fq = sbB("Fq", [128, 2, 4, FD], BF16)
            fTq = sbB("fTq", [128, 2, 24, 128], BF16)
            gate_sb = sbB("gate_sb", [128, HID], F32)
            hs_sb = sbB("hs_sb", [128, HID], F32)
            mhg = sbB("mhg", [128, HID], F32)
            ln2_sb = mhg
            y_sb = sbB("y_sb", [128, HID], F32)

            s_hs2 = P.sem("hs2")
            s_den = P.sem("den")
            s_gproj = P.sem("gproj")
            s_gate = P.sem("gate")
            s_qrd = [P.sem("qrd0"), P.sem("qrd1")]
            s_relu2 = P.sem("relu2")
            s_Fq = P.sem("Fq")
            s_ssq2 = P.sem("ssq2")
            s_numa = P.sem("numa")
            s_mhg = P.sem("mhg")
            s_attn = P.sem("attn")
            s_ln2 = P.sem("ln2")
            s_yv = P.sem("yv")
            s_ywr = [P.sem("ywr0"), P.sem("ywr1")]

            for hh in range(8, 16):
                build_head_dma(hh)

            def b1_loads(u):
                us = u % 2
                if u < 2:
                    P.wait("sync", s_hs, 8)
                else:
                    P.wait("sync", s_yv, u - 1)
                P.dma("sync", x_sb[:, us, :], xt[u], s_x[us])
                if u >= 2:
                    P.wait("sync", s_relu2, 4 * u - 4)
                if u == 0:
                    P.wait("sync", s_endA2)
                P.dma("sync", q_in[:, us, :], q_hbm[u], s_qrd[us])
                if u >= 2:
                    P.wait("sync", s_gproj, u - 1)
                P.dma("sync", hsT[:, us], ht_hbm[u], s_ht[us])

            b1_loads(0)
            fq_te = {}
            for t in range(NT):
                slot = t % 2
                if t + 1 < NT:
                    b1_loads(t + 1)
                if debug and t in (1, 2):
                    P.wait("scalar", s_out, s_out.count)
                    P.wait("vector", s_out, s_out.count)
                if t == 0:
                    P.wait("scalar", s_endA2)
                P.wait("tensor", s_wg)
                P.wait("tensor", s_ht[slot], 16 * (4 + t // 2 + 1))
                if t == 0:
                    P.wait("tensor", half_sems[1]["dev"], 6)
                    P.wait("tensor", s_endA2)
                if t > 0:
                    P.wait("tensor", s_gate, t)
                for c in range(NCH + 1):
                    last = c == NCH
                    lhs = ones_r[:] if last else hsT[:, slot, c, :]
                    for wi in range(4):
                        rhs = (bias_g[:, 512 * wi:512 * (wi + 1)] if last
                               else wg_sb[:, c, 512 * wi:512 * (wi + 1)])
                        P.op("tensor", lambda t_, lhs=lhs, rhs=rhs, wi=wi, c=c, last=last:
                             t_.matmul(ps_a[:, 512 * wi:512 * (wi + 1)], lhs, rhs,
                                       start=(c == 0), stop=last),
                             s_gproj if (last and wi == 3) else None)
                P.wait("scalar", s_gproj, t + 1)
                if t > 0:
                    P.wait("scalar", s_mhg, 4 * t)
                    P.wait("scalar", s_ln2, t)
                P.op("scalar", lambda s: s.activation(gate_sb[:], ps_a[:, 0:2048], AF.Sigmoid), s_gate)
                if debug and t == 0:
                    P.wait("sync", s_gate, 1)
                    P.dma("sync", dbg["gate0"], gate_sb[:], s_out)
                for g in range(4):
                    gi = 4 * t + g
                    par = g % 2
                    if t == 0 and g == 2:
                        for hh in range(8, 12):
                            build_head_v(hh)
                    if t == 0 and g == 3:
                        for hh in range(12, 16):
                            build_head_v(hh)
                    P.wait("scalar", s_qrd[slot], 16 * (t // 2 + 1))
                    if gi > 1:
                        P.wait("scalar", s_Fq, gi - 1)
                    qv = q_in.ap()[:, slot, :].rearrange("p (g4 d) -> p g4 d", d=128)
                    P.op("scalar", lambda s, g=g, par=par, qv=qv: s.activation(
                        xe4[:, par, :, 3:131], qv[:, 4 * g:4 * g + 4, :], AF.Relu))
                    P.op("scalar", lambda s, g=g, par=par, qv=qv: s.activation(
                        xe4[:, par, :, 131:259], qv[:, 4 * g:4 * g + 4, :], AF.Relu, scale=-1.0), s_relu2)
                    P.wait("vector", s_relu2, gi + 1)
                    P.op("vector", lambda v, par=par: v.tensor_copy(
                        xe4[:, par, :, 0:3], xe4[:, par, :, 256:259]))
                    P.bar("vector")
                    if gi > 1:
                        P.wait("vector", s_te, fq_te.get(par, 0))
                        P.wait("vector", s_ssq2, gi - 1)
                    for j in (1, 2, 3):
                        P.op("vector", lambda v, j=j, par=par: v.tensor_tensor(
                            Fq[:, par, :, 256 * (j - 1):256 * j], xe4[:, par, :, 3:259],
                            xe4[:, par, :, 3 - j:259 - j], ALU.mult), s_Fq if j == 3 else None)
                    P.wait("scalar", s_Fq, gi + 1)
                    if gi > 1:
                        P.wait("scalar", s_mhg, gi - 1)
                    sreg = 8 * par
                    for hl in range(4):
                        P.op("scalar", lambda s, hl=hl, sreg=sreg, par=par: s.activation(
                            sq_scr[:, hl % 2, :], Fq[:, par, hl, :], AF.Square,
                            accum_out=sstat[:, sreg + hl:sreg + hl + 1]))
                        if hl == 1:
                            P.bar("scalar")
                    P.bar("scalar")
                    P.op("scalar", lambda s, sreg=sreg: s.activation(
                        sstat[:, sreg + 4:sreg + 8], sstat[:, sreg:sreg + 4], AF.Sqrt), s_ssq2)
                    P.wait("tensor", s_Fq, gi + 1)
                    fqswaits = [(s_numa, gi - 1)] if gi > 1 else []
                    transposes([Fq.ap()[:, par, hl, 128 * c:128 * (c + 1)]
                                for hl in range(4) for c in range(6)],
                               fTq.ap()[:, par], swaits=fqswaits)
                    fq_te[par] = s_te.count
                    if gi == 0:
                        P.wait("tensor", half_sems[1]["dev"], 6)
                    if gi > 1:
                        P.wait("tensor", s_mhg, gi - 1)
                    P.wait("tensor", s_memN, 4 * (g + 1))
                    memg = memNs[g // 2]
                    nbank = (ps_n.ap() if par == 0 else ps_a.ap()[:, 2048:2560])
                    for hl in range(4):
                        for c in range(6):
                            P.op("tensor", lambda t_, g=g, c=c, hl=hl, memg=memg, nb=nbank, par=par: t_.matmul(
                                nb[:, 128 * hl:128 * (hl + 1)], fTq[:, par, 6 * hl + c, :],
                                memg[:, g % 2, c, 128 * hl:128 * (hl + 1)],
                                start=(c == 0), stop=(c == 5)), None)
                    if gi > 0:
                        P.wait("tensor", s_den, gi)
                    for hl in range(4):
                        for c in range(6):
                            P.op("tensor", lambda t_, g=g, c=c, hl=hl, par=par: t_.matmul(
                                ps_dot[:, 4 * par + hl:4 * par + hl + 1], fTq[:, par, 6 * hl + c, :],
                                normN[:, g, c, hl:hl + 1],
                                start=(c == 0), stop=(c == 5)),
                                s_numa if (hl == 3 and c == 5) else None)
                    P.wait("vector", s_numa, gi + 1)
                    P.wait("vector", s_ssq2, gi + 1)
                    n4 = sstat.ap()[:, sreg + 4:sreg + 8]
                    den = sstat.ap()[:, 16:20]
                    P.op("vector", lambda v, n4=n4, den=den, par=par: v.scalar_tensor_tensor(
                        den, n4, EPS, ps_dot[:, 4 * par:4 * par + 4], ALU.mult, ALU.add), s_den)
                    P.bar("vector")
                    P.op("vector", lambda v, den=den: v.reciprocal(sstat[:, 20:24], den))
                    P.bar("vector")
                    P.wait("vector", s_gate, t + 1)
                    for hl in range(4):
                        hcol = 128 * (4 * g + hl)
                        P.op("vector", lambda v, hl=hl, hcol=hcol, nb=nbank: v.scalar_tensor_tensor(
                            mhg[:, hcol:hcol + 128], nb[:, 128 * hl:128 * (hl + 1)],
                            sstat[:, 20 + hl:21 + hl], gate_sb[:, hcol:hcol + 128],
                            ALU.mult, ALU.mult), s_mhg if hl == 3 else None)
                P.wait("scalar", s_x[slot], 16 * (4 + t // 2 + 1))
                if t >= 1:
                    P.wait("scalar", s_attn, t)
                P.op("scalar", lambda s, slot=slot, t=t: s.activation(
                    hs_sb[:], x_sb[:, slot, :], AF.Identity,
                    bias=msb[:, 2 * t + 1:2 * t + 2], scale=msb[:, 2 * t:2 * t + 1]), s_hs2)
                if debug and t == 0:
                    P.wait("vector", s_out, s_out.count)
                if t > 0:
                    P.wait("vector", s_ln2, t)
                P.wait("vector", s_hs2, t + 1)
                P.bar("vector")
                P.op("vector", lambda v: v.tensor_tensor(gate_sb[:], mhg[:], hs_sb[:], ALU.add), s_attn)
                P.bar("vector")
                if debug and t == 0:
                    P.wait("sync", s_attn, 1)
                    P.dma("sync", dbg["mhg0"], gate_sb[:], s_out)
                ln_stats(gate_sb.ap(), lns[:, 5:6], lns[:, 6:7])
                P.wait("scalar", s_lnd)
                P.op("scalar", lambda s: s.activation(
                    ln2_sb[:], gate_sb[:], AF.Identity, bias=lns[:, 6:7], scale=lns[:, 5:6]), s_ln2)
                P.wait("vector", s_ln2, t + 1)
                P.wait("vector", s_ywr[0], 16 * t)
                P.op("vector", lambda v, slot=slot: v.tensor_tensor(
                    y_sb[:], ln2_sb[:], x_sb[:, slot, :], ALU.add), s_yv)
                P.wait("sync", s_yv, t + 1)
                P.dma("sync", y_hbm[t], y_sb[:], s_ywr[0])
                if debug and t == 0:
                    P.dma("sync", dbg["y0"], y_sb[:], s_out)
            if debug:
                P.wait("sync", s_memN, 16)
                P.dma("sync", dbg["memN"].rearrange("p g c x -> p (g c x)")
                      .rearrange("p (g2 r) -> p g2 r", g2=2)[:, 0, :], memN_a.ap()
                      .rearrange("p g c x -> p (g c x)"), s_out)
                P.dma("sync", dbg["memN"].rearrange("p g c x -> p (g c x)")
                      .rearrange("p (g2 r) -> p g2 r", g2=2)[:, 1, :], memN_b.ap()
                      .rearrange("p g c x -> p (g c x)"), s_out)
                P.dma("sync", dbg["normN"], normN[:], s_out)
            P.op("tensor", lambda t: t.nop(), s_endB1)
            P.op("vector", lambda v: v.nop(), s_endB1)
            P.op("scalar", lambda s: s.nop(), s_endB1)
            P.wait("sync", s_ywr[0])
            P.wait("sync", s_qrd[0])
            P.wait("sync", s_qrd[1])
            if debug:
                P.wait("sync", s_out, s_out.count)
            P.op("sync", lambda e: e.nop(), s_endB1)

        x_es.close()
        esC = contextlib.ExitStack()
        with esC:
            def sbC(name, shape, dt):
                return esC.enter_context(nc.sbuf_tensor(name, shape, dt))

            sT = sbC("sT", [128, NCH, HID], BF16)
            wf_st = sbC("wf_st", [128, 2, HID], F32)
            absw = sbC("absw", [128, HID], BF16)
            alpha_row = sbC("alpha_row", [1, HID], F32)
            inva = sbC("inva", [1, HID], F32)
            biasf_sb = sbC("biasf_sb", [1, HID], F32)
            bias_pre = sbC("bias_pre", [1, HID], BF16)
            alpha_bc = sbC("alpha_bc", [128, HID], F32)
            y_in = sbC("y_in", [128, 2, HID], F32)
            hs2b = sbC("hs2b", [128, HID], BF16)
            z_sb = sbC("z_sb", [128, HID], F32)
            ln4_sb = sbC("ln4_sb", [128, HID], F32)
            out_sb = sbC("out_sb", [128, HID], F32)

            s_wf = [P.sem("wf0"), P.sem("wf1")]
            s_bfl = P.sem("bfl")
            s_sign = P.sem("sign")
            s_amm = P.sem("amm")
            s_aev = P.sem("aev")
            s_bc = P.sem("bc")
            s_yrd = [P.sem("yrd0"), P.sem("yrd1")]
            s_hs3 = P.sem("hs3")
            s_fmm = P.sem("fmm")
            s_z = P.sem("z")
            s_ln4 = P.sem("ln4")
            s_outv = P.sem("outv")
            s_od = [P.sem("od0"), P.sem("od1")]

            P.wait("gpsimd", s_endB1)
            P.dma("gpsimd", biasf_sb[:], bfh, s_bfl)
            for c in range(NCH):
                slot = c % 2
                if c >= 2:
                    P.wait("gpsimd", s_sign, c - 1)
                P.dma("gpsimd", wf_st[:, slot, :],
                      wfT.rearrange("(c p) j -> p c j", p=128)[:, c, :], s_wf[slot])
                P.wait("scalar", s_wf[slot], 16 * (c // 2 + 1))
                if c == 0:
                    P.wait("scalar", s_endB1, 4)
                if c >= 1:
                    P.wait("scalar", s_amm, c)
                P.op("scalar", lambda s, c=c, slot=slot: s.activation(
                    sT[:, c, :], wf_st[:, slot, :], AF.Sign))
                P.op("scalar", lambda s, c=c, slot=slot: s.activation(
                    absw[:], wf_st[:, slot, :], AF.Abs), s_sign)
                P.wait("tensor", s_sign, c + 1)
                if c == 0:
                    P.wait("tensor", s_endB1, 4)
                    P.wait("tensor", s_z, 0)
                for wi in range(4):
                    P.op("tensor", lambda t_, wi=wi, c=c: t_.matmul(
                        ps_a.ap()[0:1, 512 * wi:512 * (wi + 1)], ones_c[:],
                        absw[:, 512 * wi:512 * (wi + 1)],
                        start=(c == 0), stop=(c == NCH - 1)),
                        s_amm if wi == 3 else None)
            P.wait("scalar", s_amm, NCH)
            P.op("scalar", lambda s: s.activation(
                alpha_row[:], ps_a.ap()[0:1, 0:2048], AF.Identity, scale=1.0 / HID), s_aev)
            P.wait("vector", s_aev)
            P.wait("vector", s_endB1, 4)
            P.op("vector", lambda v: v.reciprocal(inva[:], alpha_row[:]))
            P.wait("vector", s_bfl, 16)
            P.bar("vector")
            P.op("vector", lambda v: v.tensor_tensor(bias_pre[:], biasf_sb[:], inva[:], ALU.mult), s_aev)
            P.wait("tensor", s_aev, 2)
            for wi in range(4):
                P.op("tensor", lambda t_, wi=wi: t_.matmul(
                    ps_a[:, 512 * wi:512 * (wi + 1)], ones_rf[:],
                    alpha_row[:, 512 * wi:512 * (wi + 1)], start=True, stop=True),
                    s_bc if wi == 3 else None)
            P.wait("scalar", s_bc)
            P.op("scalar", lambda s: s.copy(alpha_bc[:], ps_a[:, 0:2048]), s_bc)
            if debug:
                P.wait("sync", s_aev, 2)
                P.dma("sync", dbg["alpha"], alpha_row[:], s_out)

            def b2_yload(t):
                slot = t % 2
                if t == 0:
                    P.wait("sync", s_endB1, 4)
                if t >= 2:
                    P.wait("sync", s_outv, t - 1)
                if t == 0:
                    P.wait("sync", s_ywr[0], 16 * 8)
                P.dma("sync", y_in[:, slot, :], y_hbm[t], s_yrd[slot])

            def b2_ln3(t):
                slot = t % 2
                if debug and t == 1:
                    P.wait("vector", s_out, s_out.count)
                P.wait("vector", s_yrd[slot], 16 * (t // 2 + 1))
                ln_stats(y_in.ap()[:, slot, :], lns[:, 5:6], lns[:, 6:7])
                P.wait("scalar", s_lnd)
                if t >= 1:
                    P.wait("scalar", s_te, s_te.count)
                P.op("scalar", lambda s, slot=slot: s.activation(
                    hs2b[:], y_in[:, slot, :], AF.Identity,
                    bias=lns[:, 6:7], scale=lns[:, 5:6]), s_hs3)

            b2_yload(0)
            b2_ln3(0)
            for t in range(NT):
                slot = t % 2
                P.wait("tensor", s_hs3, t + 1)
                if t == 0:
                    P.wait("tensor", s_bc, 2)
                b2swaits = [(s_fmm, t - 1)] if t >= 2 else []
                transposes([hs2b.ap()[:, 128 * c:128 * (c + 1)] for c in range(NCH)],
                           hsT.ap()[:, slot], swaits=b2swaits)
                if t > 0:
                    P.wait("tensor", s_z, t)
                for c in range(NCH + 1):
                    last = c == NCH
                    lhs = ones_r[:] if last else hsT[:, slot, c, :]
                    for wi in range(4):
                        rhs = (bias_pre[:, 512 * wi:512 * (wi + 1)] if last
                               else sT[:, c, 512 * wi:512 * (wi + 1)])
                        P.op("tensor", lambda t_, lhs=lhs, rhs=rhs, wi=wi, c=c, last=last:
                             t_.matmul(ps_a[:, 512 * wi:512 * (wi + 1)], lhs, rhs,
                                       start=(c == 0), stop=last),
                             s_fmm if (last and wi == 3) else None)
                if t + 1 < NT:
                    b2_yload(t + 1)
                    b2_ln3(t + 1)
                P.wait("vector", s_fmm, t + 1)
                P.wait("vector", s_bc, 2)
                if debug and t == 1:
                    P.wait("vector", s_out, s_out.count)
                if t > 0:
                    P.wait("vector", s_ln4, t)
                P.op("vector", lambda v: v.tensor_tensor(z_sb[:], ps_a[:, 0:2048], alpha_bc[:], ALU.mult), s_z)
                if debug and t == 0:
                    P.wait("sync", s_z, 1)
                    P.dma("sync", dbg["z0"], z_sb[:], s_out)
                P.wait("vector", s_z, t + 1)
                P.bar("vector")
                ln_stats(z_sb.ap(), lns[:, 16:17], lns[:, 17:18])
                P.wait("scalar", s_lnd)
                P.op("scalar", lambda s: s.activation(
                    ln4_sb[:], z_sb[:], AF.Identity, bias=lns[:, 17:18], scale=lns[:, 16:17]), s_ln4)
                P.wait("vector", s_ln4, t + 1)
                if t >= 1:
                    P.wait("vector", s_od[0], 16 * t)
                P.op("vector", lambda v, slot=slot: v.tensor_tensor(
                    out_sb[:], ln4_sb[:], y_in[:, slot, :], ALU.add), s_outv)
                P.wait("sync", s_outv, t + 1)
                P.dma("sync", outh.rearrange("(n p) d -> n p d", p=128)[t], out_sb[:], s_od[0])

            P.wait("gpsimd", s_od[0])
            if debug:
                P.wait("gpsimd", s_out)

        blk = es.enter_context(nc.Block())
        for s in P.sems:
            s.h = es.enter_context(nc.semaphore(s.name))

        @blk.tensor
        def _(e):
            for f in P.q["tensor"]:
                f(e)

        @blk.vector
        def _(e):
            for f in P.q["vector"]:
                f(e)

        @blk.scalar
        def _(e):
            for f in P.q["scalar"]:
                f(e)

        @blk.gpsimd
        def _(e):
            for f in P.q["gpsimd"]:
                f(e)

        @blk.sync
        def _(e):
            for f in P.q["sync"]:
                f(e)

    return nc, dbg


_NC_CACHE = {}


def _get_nc(debug=False):
    key = bool(debug)
    if key not in _NC_CACHE:
        _NC_CACHE[key] = build_nc(debug)
    return _NC_CACHE[key]


def make_in_maps(inputs):
    x = np.ascontiguousarray(np.asarray(inputs["L0_attn_hidden_states"], np.float32).reshape(8192, HID))
    kc = np.asarray(inputs["L0_attn_Wk"], np.float32).T.reshape(NCH, 128, 512)
    vc = np.asarray(inputs["L0_attn_Wv"], np.float32).T.reshape(NCH, 128, 512)
    mc = np.asarray(inputs["L0_attn_Wmb"], np.float32).T.reshape(NCH, 128, HID)
    qc = np.asarray(inputs["L0_attn_Wq"], np.float32).T.reshape(NCH, 128, HID)
    mem = np.asarray(inputs["L0_attn_memory"], np.float32)
    com = {
        "WfT": np.ascontiguousarray(np.asarray(inputs["L0_ffn_weight"], np.float32).T),
        "WgP": np.ascontiguousarray(
            np.asarray(inputs["L0_attn_Wg"], np.float32).T.reshape(NCH, 128, HID)
            .transpose(1, 0, 2)),
        "memP4": np.ascontiguousarray(
            mem.reshape(2, 2, 4, 6, 128, 128).transpose(0, 1, 4, 3, 2, 5)
            .reshape(4, 128, 6, 512)),
        "memPh": np.ascontiguousarray(mem.reshape(16, 6, 128, 128).transpose(0, 2, 1, 3)),
        "bk": np.asarray(inputs["L0_attn_bk"], np.float32).reshape(1, 512),
        "bv": np.asarray(inputs["L0_attn_bv"], np.float32).reshape(1, 512),
        "bm": np.asarray(inputs["L0_attn_bmb"], np.float32).reshape(1, HID),
        "bq": np.asarray(inputs["L0_attn_bq"], np.float32).reshape(1, HID),
        "bg": np.asarray(inputs["L0_attn_bg"], np.float32).reshape(1, HID),
        "bf": np.asarray(inputs["L0_ffn_bias"], np.float32).reshape(1, HID),
        "norm96": np.ascontiguousarray(np.asarray(inputs["L0_attn_norm"], np.float32).reshape(96, 128)),
        "ident": np.eye(128, dtype=np.float32),
    }
    for h in range(2):
        com[f"Wa{h}"] = np.ascontiguousarray(np.concatenate(
            [kc[:, :, 256 * h:256 * (h + 1)], vc[:, :, 256 * h:256 * (h + 1)],
             mc[:, :, 1024 * h:1024 * (h + 1)], qc[:, :, 1024 * h:1024 * (h + 1)]],
            axis=2).transpose(1, 0, 2))
    in_maps = []
    for c in range(NCORE):
        m = dict(com)
        m["x_part"] = np.ascontiguousarray(x[c * TOK:(c + 1) * TOK])
        in_maps.append(m)
    return in_maps


def kernel(**inputs):
    nc, _ = _get_nc(False)
    in_maps = make_in_maps(inputs)
    res = run_bass_kernel_spmd(nc, in_maps, core_ids=list(range(NCORE)))
    out = np.concatenate([r["out_part"] for r in res.results], axis=0)
    return out.reshape(2, 4096, HID)


# version 06
# speedup vs baseline: 1.2078x; 1.0688x over previous
import contextlib
import numpy as np
import concourse.bass as bass
import concourse.mybir as mybir
from concourse.bass_utils import run_bass_kernel_spmd

F32 = mybir.dt.float32
BF16 = mybir.dt.bfloat16
AF = mybir.ActivationFunctionType
ALU = mybir.AluOpType
AX = mybir.AxisListType

NCORE = 8
TOK = 1024
NT = 8
HID = 2048
NCH = 16
D = 128
H = 16
FD = 768
EPS = 1e-8
LNEPS = 1e-5


class Sem:
    __slots__ = ("name", "count", "h")

    def __init__(self, name):
        self.name = name
        self.count = 0
        self.h = None

    def bump(self, n=1):
        self.count += n
        return self.count


class Prog:
    ENGINES = ("tensor", "vector", "scalar", "gpsimd", "sync")

    def __init__(self):
        self.q = {e: [] for e in self.ENGINES}
        self.sems = []
        self.barsems = {}

    def sem(self, name):
        s = Sem(name)
        self.sems.append(s)
        return s

    def emit(self, eng, fn):
        self.q[eng].append(fn)

    def wait(self, eng, sem, val=None):
        v = sem.count if val is None else val
        if v <= 0:
            return
        self.emit(eng, lambda e, s=sem, vv=v: e.wait_ge(s.h, vv))

    def op(self, eng, builder, sem=None, inc=1):
        if sem is not None:
            sem.bump(inc)
        extra = []

        def f(e, b=builder, s=sem, n=inc, x=extra):
            inst = b(e)
            if s is not None:
                inst.then_inc(s.h, n)
            for (xs, xn) in x:
                inst.then_inc(xs.h, xn)

        self.emit(eng, f)
        self.lastop = getattr(self, 'lastop', {})
        self.lastop[eng] = (extra, sem, sem.count if sem is not None else 0)

    def nop(self, eng):
        self.emit(eng, lambda e: e.nop())

    def drain(self, eng):
        self.emit(eng, lambda e: e.drain())

    def bar(self, eng):
        extra, psem, pval = self.lastop[eng]
        if psem is not None:
            self.emit(eng, lambda e, ss=psem, v=pval: e.wait_ge(ss.h, v))
            return
        s = self.barsems.get(eng)
        if s is None:
            s = self.sem(f"bar_{eng}")
            self.barsems[eng] = s
        c = s.bump(1)
        extra.append((s, 1))
        self.emit(eng, lambda e, ss=s, v=c: e.wait_ge(ss.h, v))

    def dma(self, qeng, out_ap, in_ap, sem, slow=False):
        sem.bump(16)
        self.emit(qeng, lambda e, o=out_ap, i=in_ap, s=sem, sl=slow:
                  e.dma_start(out=o, in_=i, allow_slow_non_contiguous=sl).then_inc(s.h, 16))


def build_nc(debug=False):
    nc = bass.Bass("TRN2", target_bir_lowering=False)
    P = Prog()

    xh = nc.dram_tensor("x_part", [TOK, HID], F32, kind="ExternalInput").ap()
    waH = [nc.dram_tensor(f"Wa{h}", [128, NCH, 2560], F32, kind="ExternalInput").ap()
           for h in range(2)]
    wgP = nc.dram_tensor("WgP", [128, NCH, HID], F32, kind="ExternalInput").ap()
    memP4h = nc.dram_tensor("memP4", [4, 128, 6, 512], F32, kind="ExternalInput").ap()
    memPh = nc.dram_tensor("memPh", [16, 128, 6, 128], F32, kind="ExternalInput").ap()
    wfT = nc.dram_tensor("WfT", [HID, HID], F32, kind="ExternalInput").ap()
    bk = nc.dram_tensor("bk", [1, 512], F32, kind="ExternalInput").ap()
    bv = nc.dram_tensor("bv", [1, 512], F32, kind="ExternalInput").ap()
    bm = nc.dram_tensor("bm", [1, HID], F32, kind="ExternalInput").ap()
    bq = nc.dram_tensor("bq", [1, HID], F32, kind="ExternalInput").ap()
    bg = nc.dram_tensor("bg", [1, HID], F32, kind="ExternalInput").ap()
    bfh = nc.dram_tensor("bf", [1, HID], F32, kind="ExternalInput").ap()
    normh = nc.dram_tensor("norm96", [96, 128], F32, kind="ExternalInput").ap()
    identh = nc.dram_tensor("ident", [128, 128], F32, kind="ExternalInput").ap()
    outh = nc.dram_tensor("out_part", [TOK, HID], F32, kind="ExternalOutput").ap()

    q_hbm = nc.dram_tensor("q_scr", [NT, 128, HID], BF16).ap()
    ht_hbm = nc.dram_tensor("ht_scr", [NT, 128, NCH * 128], BF16).ap()
    y_hbm = nc.dram_tensor("y_scr", [NT, 128, HID], F32).ap()
    d_loc = [nc.dram_tensor(f"d_loc{i}", [6, 128, 1032], F32).ap() for i in range(2)]
    d_red = [nc.dram_tensor(f"d_red{i}", [6, 128, 1032], F32, addr_space="Shared").ap() for i in range(2)]

    dbg = {}
    if debug:
        def dbgt(name, shape, dt=F32):
            dbg[name] = nc.dram_tensor("dbg_" + name, shape, dt, kind="ExternalOutput").ap()
        dbgt("hs0", [128, HID], BF16)
        dbgt("msb", [128, 16])
        dbgt("lns0", [128, 28])
        dbgt("bn0", [128, 24])
        dbgt("x0", [128, HID])
        dbgt("lsnap", [128, 28])
        dbgt("kv0", [128, 512], BF16)
        dbgt("mb0", [128, 1024], BF16)
        dbgt("q0", [128, 1024], BF16)
        dbgt("F0", [128, 2, FD], BF16)
        dbgt("nst0", [128, 24])
        dbgt("wmv0", [128, 8, 128], BF16)
        dbgt("coefn0", [128, 8], BF16)
        dbgt("memN", [128, 4, 6, 512], BF16)
        dbgt("normN", [128, 4, 6, 4], BF16)
        dbgt("mhg0", [128, HID])
        dbgt("gate0", [128, HID])
        dbgt("y0", [128, HID])
        dbgt("alpha", [1, HID])
        dbgt("z0", [128, HID])

    xt = xh.rearrange("(n p) d -> n p d", p=128)

    es = contextlib.ExitStack()
    with es:
        def sb(name, shape, dt):
            return es.enter_context(nc.sbuf_tensor(name, shape, dt))

        def pst(name, shape, dt):
            return es.enter_context(nc.psum_tensor(name, shape, dt))

        ps_a = pst("ps_a", [128, 2560], F32)
        ps_n = pst("ps_n", [128, 512], F32)
        ps_dot = pst("ps_dot", [128, 8], F32)
        ps_tr = pst("ps_tr", [128, 8, 128], BF16)

        ident_bf = sb("ident_bf", [128, 128], BF16)
        ident_f = sb("ident_f", [128, 128], F32)
        ones_r = sb("ones_r", [1, 128], BF16)
        ones_rf = sb("ones_rf", [1, 128], F32)
        ones_c = sb("ones_c", [128, 1], BF16)
        normF = sb("normF", [128, 96], F32)
        msb = sb("msb", [128, 16], F32)
        bn24 = sb("bn24", [128, 24], F32)
        lns = sb("lns", [128, 28], F32)
        sq_scr = sb("sq_scr", [128, 2, FD], F32)
        sstat = sb("sstat", [128, 24], F32)
        mv_scr = sb("mv_scr", [128, 4, 128], F32)
        lsnap = sb("lsnap", [128, 28], F32)
        hsb_sb = sb("hsb_sb", [128, HID], BF16)
        hsT = sb("hsT", [128, 2, NCH, 128], BF16)
        memN_a = sb("memN_a", [128, 2, 6, 512], BF16)
        memNs = [memN_a, None]
        normN = sb("normN", [128, 4, 6, 4], BF16)
        mstage = sb("mstage", [128, 1, 6, 128], F32)
        dstage2 = sb("dstage2", [128, 1, 6, 129], F32)
        norm96 = sb("norm96s", [96, 128], F32)
        x_es = contextlib.ExitStack()
        x_sb = x_es.enter_context(nc.sbuf_tensor("x_sb", [128, 2, HID], F32))

        s_const = P.sem("const")
        s_konst = P.sem("konst")
        s_ntr = P.sem("ntr")
        s_endA1 = P.sem("endA1")
        s_endA2 = P.sem("endA2")
        s_endB1 = P.sem("endB1")
        s_cc = [P.sem("cc0"), P.sem("cc1")]
        s_out = P.sem("out")
        s_x = [P.sem("x_s0"), P.sem("x_s1")]
        s_ht = [P.sem("ht_s0"), P.sem("ht_s1")]
        s_htw = [P.sem("htw_s0"), P.sem("htw_s1")]
        s_hs = P.sem("hs")
        s_hsb = P.sem("hsb")
        s_tr = P.sem("tr")
        s_te = P.sem("te")
        s_lnv = P.sem("lnv")
        s_lsn = P.sem("lsn")
        s_wg = P.sem("wg")
        s_mst = P.sem("mst")
        s_memN = P.sem("memN")
        s_lnq = P.sem("lnq")
        s_lnd = P.sem("lnd")

        P.dma("gpsimd", ident_bf[:], identh, s_const)
        P.dma("gpsimd", ident_f[:], identh, s_const)
        P.op("vector", lambda v: v.memset(ones_r[:], 1.0), s_konst)
        P.op("vector", lambda v: v.memset(ones_rf[:], 1.0), s_konst)
        P.op("vector", lambda v: v.memset(ones_c[:], 1.0), s_konst)
        P.op("vector", lambda v: v.memset(lns[:], 0.0), s_konst)
        P.op("vector", lambda v: v.memset(bn24[:], 0.0), s_konst)
        P.op("vector", lambda v: v.memset(sstat[:], 0.0), s_konst)
        P.op("vector", lambda v: v.memset(msb[:], 0.0), s_konst)

        def ln_stats(src_ap, rstd_col, negms_col):
            for i in range(4):
                P.op("vector", lambda v, i=i, a=src_ap: v.bn_stats(
                    bn24[:, 6 * i:6 * i + 6], a[:, 512 * i:512 * (i + 1)]))
            m8 = bn24.ap()[:, 1:24:3]
            M28 = bn24.ap()[:, 2:24:3]
            P.bar("vector")
            P.op("vector", lambda v, m8=m8: v.tensor_tensor(lns[:, 8:16], m8, m8, ALU.mult))
            P.op("vector", lambda v, m8=m8: v.reduce_sum(lns[:, 1:2], m8, axis=AX.X))
            P.bar("vector")
            P.op("vector", lambda v, M28=M28: v.scalar_tensor_tensor(
                lns[:, 20:28], lns[:, 8:16], 256.0, M28, ALU.mult, ALU.add))
            P.op("vector", lambda v: v.tensor_scalar(
                lns[:, 2:3], lns[:, 1:2], 0.125, None, ALU.mult))
            P.bar("vector")
            P.op("vector", lambda v: v.reduce_sum(lns[:, 0:1], lns[:, 20:28], axis=AX.X))
            P.op("vector", lambda v: v.tensor_tensor(
                lns[:, 3:4], lns[:, 2:3], lns[:, 2:3], ALU.mult))
            P.bar("vector")
            P.op("vector", lambda v: v.tensor_scalar(
                lns[:, 4:5], lns[:, 0:1], 1.0 / HID, LNEPS, ALU.mult, ALU.add))
            P.bar("vector")
            P.op("vector", lambda v: v.tensor_tensor(
                lns[:, 18:19], lns[:, 4:5], lns[:, 3:4], ALU.subtract))
            P.bar("vector")
            P.op("vector", lambda v: v.reciprocal(lns[:, 19:20], lns[:, 18:19]), s_lnv)
            P.wait("scalar", s_lnv)
            P.op("scalar", lambda s, rc=rstd_col: s.activation(rc, lns[:, 19:20], AF.Sqrt), s_lnq)
            P.wait("vector", s_lnq)
            P.op("vector", lambda v, rc=rstd_col, nm=negms_col: v.scalar_tensor_tensor(
                nm, lns[:, 2:3], -1.0, rc, ALU.mult, ALU.mult), s_lnd)

        def transposes(srcs, dst, swaits=()):
            n = len(srcs)
            r0 = 0
            while r0 < n:
                rn = min(8, n - r0)
                P.wait("tensor", s_te)
                for i in range(rn):
                    P.op("tensor", lambda t, i=i, a=srcs[r0 + i]: t.transpose(
                        ps_tr[:, i, :], a, ident_bf[:]), s_tr if i == rn - 1 else None)
                P.wait("scalar", s_tr)
                for sw in swaits:
                    P.wait("scalar", sw[0], sw[1])
                swaits = ()
                P.op("scalar", lambda s, r0=r0, rn=rn, d=dst: s.copy(
                    d[:, r0:r0 + rn, :], ps_tr[:, 0:rn, :]), s_te)
                r0 += rn
            P.wait("tensor", s_te)
            return s_te.count

        def build_head_dma(hh):
            P.wait("gpsimd", s_cc[hh // 8], 1)
            if hh >= 1:
                P.wait("gpsimd", s_memN, hh)
            hl = hh % 8
            P.dma("gpsimd", mstage[:, 0, :, :], memPh[hh], s_mst)
            P.dma("gpsimd", dstage2[:, 0, :, 0:128],
                  d_red[hh // 8][:, :, 128 * hl:128 * (hl + 1)].rearrange("c p d -> p c d"),
                  s_mst)
            P.dma("gpsimd", dstage2[:, 0, :, 128:129],
                  d_red[hh // 8][:, :, 1024 + hl:1025 + hl].rearrange("c p d -> p c d"),
                  s_mst, slow=True)

        def build_head_v(hh):
            P.wait("vector", s_mst, 48 * (hh + 1))
            P.op("vector", lambda v, hh=hh: v.tensor_tensor(
                memNs[hh // 8][:, (hh // 4) % 2, :, 128 * (hh % 4):128 * (hh % 4 + 1)],
                mstage[:, 0, :, :], dstage2[:, 0, :, 0:128], ALU.add))
            P.op("vector", lambda v, hh=hh: v.tensor_tensor(
                normN[:, hh // 4, :, (hh % 4):(hh % 4) + 1],
                normF.ap()[:, 6 * hh:6 * (hh + 1)].rearrange("p (c o) -> p c o", o=1),
                dstage2[:, 0, :, 128:129], ALU.add), s_memN)

        P.dma("gpsimd", norm96[:], normh, s_const)
        P.wait("tensor", s_const)
        P.wait("tensor", s_konst)
        P.op("tensor", lambda t: t.transpose(
            ps_n.ap()[:, 0:96], norm96[0:96, :], ident_f[0:96, 0:96]), s_ntr)
        P.wait("scalar", s_ntr)
        P.op("scalar", lambda s: s.copy(normF[:], ps_n.ap()[:, 0:96]), s_ntr)

        wg_pre = [None, None]
        esA = contextlib.ExitStack()
        with esA:
            def sbA(name, shape, dt):
                return esA.enter_context(nc.sbuf_tensor(name, shape, dt))

            wA = sbA("wA", [128, NCH, 2560], BF16)
            bias_kvmb = sbA("bias_kvmb", [1, 1536], BF16)
            bias_q = sbA("bias_q", [1, 1024], BF16)
            memA = sbA("memA", [128, 2, 6, 512], BF16)
            normA = sbA("normA", [128, 2, 6, 4], BF16)
            kv_bf = sbA("kv_bf", [128, 512], BF16)
            mb_bf = sbA("mb_bf", [128, 1024], BF16)
            q_bf = sbA("q_bf", [128, 1, 1024], BF16)
            xe2 = sbA("xe2", [128, 2, 262], BF16)
            F_sb = sbA("F_sb", [128, NT, 2, FD], BF16)
            fT = sbA("fT", [128, 12, 128], BF16)
            wmv = sbA("wmv", [128, NT, 8, 128], BF16)
            coefn = sbA("coefn", [128, NT, 8], BF16)
            dstage = sbA("dstage", [128, 1, 1032], F32)

            s_q = [P.sem("q_s0")]
            half_sems = {}

            s_w4 = [P.sem(f"w4_{i}") for i in range(4)]
            for half in range(2):
                s_wA = P.sem(f"wA{half}")
                half_sems[half] = {}
                s_nA = P.sem(f"nA{half}")
                s_proj = P.sem(f"proj{half}")
                s_ev1 = P.sem(f"ev1_{half}")
                s_F = P.sem(f"F{half}")
                s_relu = P.sem(f"relu{half}")
                s_ssq = P.sem(f"ssq{half}")
                s_num = P.sem(f"num{half}")
                s_chain = P.sem(f"chain{half}")
                s_stats = P.sem(f"stats{half}")
                half_sems[half]["proj"] = s_proj
                half_sems[half]["num"] = s_num
                half_sems[half]["chain"] = s_chain

                s_p7 = P.sem(f"p7_{half}")
                s_wB = P.sem(f"wB{half}")
                s_mA = P.sem(f"mA{half}")
                half_sems[half]["p7"] = s_p7
                if half == 0:
                    P.dma("gpsimd", bias_kvmb[:, 0:256], bk[:, 0:256], s_wB)
                    P.dma("gpsimd", bias_kvmb[:, 256:512], bv[:, 0:256], s_wB)
                    P.dma("gpsimd", bias_kvmb[:, 512:1536], bm[:, 0:1024], s_wB)
                    P.dma("gpsimd", bias_q[:], bq[:, 0:1024], s_wB)
                for c in range(NCH):
                    if half == 1:
                        P.wait("gpsimd", half_sems[0]["p7"], c + 1)
                    P.dma("gpsimd", wA[:, c, :], waH[half][:, c, :], s_w4[c // 4])
                if half == 1:
                    P.wait("gpsimd", half_sems[0]["proj"], 8)
                    P.dma("gpsimd", bias_kvmb[:, 0:256], bk[:, 256:512], s_wB)
                    P.dma("gpsimd", bias_kvmb[:, 256:512], bv[:, 256:512], s_wB)
                    P.dma("gpsimd", bias_kvmb[:, 512:1536], bm[:, 1024:2048], s_wB)
                    P.dma("gpsimd", bias_q[:], bq[:, 1024:2048], s_wB)
                for i in range(4):
                    P.wait("gpsimd", s_w4[i], 64 * (half + 1))
                P.op("gpsimd", lambda g: g.nop(), s_wA)
                if half == 1:
                    P.wait("gpsimd", half_sems[0]["num"], 32)
                for j in range(2):
                    for c6 in range(6):
                        P.dma("gpsimd", memA[:, j, c6, :], memP4h[2 * half + j][:, c6, :], s_mA)
                if half == 1:
                    for hh in range(0, 8):
                        build_head_dma(hh)
                P.wait("vector", s_ntr, 2)
                if half == 1:
                    P.wait("vector", half_sems[0]["num"], 32)
                P.op("vector", lambda v, half=half: v.tensor_copy(
                    normA[:],
                    normF.ap()[:, 48 * half:48 * (half + 1)]
                    .rearrange("p (j hl c) -> p j c hl", j=2, hl=4, c=6)), s_nA)

                def a1_front(u):
                    us = u % 2
                    if debug and u == 1:
                        P.wait("vector", s_out, s_out.count)
                        P.wait("scalar", s_out, s_out.count)
                    P.wait("vector", s_x[us], 16 * (u // 2 + 1))
                    ln_stats(x_sb.ap()[:, us, :], msb[:, 2 * u:2 * u + 1],
                             msb[:, 2 * u + 1:2 * u + 2])
                    if debug and u == 0:
                        P.bar("vector")
                        P.op("vector", lambda v: v.tensor_copy(lsnap[:], lns[:, 0:28]), s_lsn)
                    P.op("vector", lambda v: v.nop(), s_stats)
                    P.wait("scalar", s_x[us], 16 * (u // 2 + 1))
                    P.wait("scalar", s_lnd, s_lnd.count)
                    if u >= 1:
                        P.wait("scalar", s_te, s_te.count)
                    P.op("scalar", lambda s, us=us, u=u: s.activation(
                        hsb_sb[:], x_sb[:, us, :], AF.Identity,
                        bias=msb[:, 2 * u + 1:2 * u + 2], scale=msb[:, 2 * u:2 * u + 1]), s_hs)
                    if debug and u == 0:
                        P.wait("sync", s_hs, 1)
                        P.dma("sync", dbg["hs0"], hsb_sb[:], s_out)
                        P.dma("sync", dbg["msb"], msb[:, 0:16], s_out)
                        P.dma("sync", dbg["lns0"], lns[:, 0:28], s_out)
                        P.dma("sync", dbg["bn0"], bn24[:, 0:24], s_out)
                        P.dma("sync", dbg["x0"], x_sb[:, 0, :], s_out)
                        P.wait("sync", s_lsn, 1)
                        P.dma("sync", dbg["lsnap"], lsnap[:], s_out)

                if half == 0:
                    P.dma("sync", x_sb[:, 0, :], xt[0], s_x[0])
                    a1_front(0)
                else:
                    P.wait("sync", s_htw[0])
                    P.dma("sync", hsT[:, 0], ht_hbm[0], s_ht[0])
                    P.wait("sync", s_htw[1])
                    P.dma("sync", hsT[:, 1], ht_hbm[1], s_ht[1])
                for t in range(NT):
                    gt = 8 * half + t
                    slot = t % 2
                    if half == 0:
                        if t + 1 < NT:
                            if debug and t + 1 == 2:
                                P.wait("sync", s_out, s_out.count)
                            if t + 1 >= 2:
                                P.wait("sync", s_hs, t)
                            P.dma("sync", x_sb[:, (t + 1) % 2, :], xt[t + 1], s_x[(t + 1) % 2])
                        P.wait("tensor", s_hs, gt + 1)
                        htswaits = []
                        if t >= 2:
                            htswaits = [(s_proj, t - 1), (s_htw[slot], 16 * (t // 2))]
                        transposes([hsb_sb.ap()[:, 128 * c:128 * (c + 1)] for c in range(NCH)],
                                   hsT.ap()[:, slot], swaits=htswaits)
                        P.wait("sync", s_te, s_te.count)
                        P.dma("sync", ht_hbm[t], hsT[:, slot], s_htw[slot])
                    else:
                        if 2 <= t + 1 < NT:
                            P.wait("sync", s_proj, t)
                            P.dma("sync", hsT[:, (t + 1) % 2], ht_hbm[t + 1], s_ht[(t + 1) % 2])
                        if t == 0:
                            P.wait("tensor", half_sems[0]["dev"], 6)
                        P.wait("tensor", s_ht[slot], 16 * (t // 2 + 1))
                    if t > 0:
                        P.wait("tensor", s_wA, 1)
                        P.wait("tensor", s_ev1, 2 * t)
                    wins = [(0, 512, 0), (512, 1024, 0), (1024, 1536, 0),
                            (1536, 2048, 1), (2048, 2560, 1)]
                    for c in range(NCH + 1):
                        last = c == NCH
                        lhs = ones_r[:] if last else hsT[:, slot, c, :]
                        if t == 0:
                            if last:
                                P.wait("tensor", s_wB)
                            else:
                                P.wait("tensor", s_w4[c // 4], 64 * (half + 1))
                        for wi, (a, b, isq) in enumerate(wins):
                            if last:
                                rhs = bias_q[:, a - 1536:b - 1536] if isq else bias_kvmb[:, a:b]
                            else:
                                rhs = wA[:, c, a:b]
                            psem = None
                            if last and wi == 4:
                                psem = s_proj
                            elif t == NT - 1 and wi == 4:
                                psem = s_p7
                            P.op("tensor", lambda t_, lhs=lhs, rhs=rhs, a=a, b=b, c=c, last=last:
                                 t_.matmul(ps_a[:, a:b], lhs, rhs, start=(c == 0), stop=last),
                                 psem)
                    P.wait("scalar", s_proj, t + 1)
                    if t > 0:
                        P.wait("scalar", s_chain, 2 * t)
                    if half == 1 and t == 0:
                        P.wait("scalar", s_endA1)
                        P.wait("scalar", half_sems[0]["dwr"][0], 96)
                    P.op("scalar", lambda s: s.copy(kv_bf[:], ps_a[:, 0:512]))
                    P.op("scalar", lambda s: s.activation(mb_bf[:], ps_a[:, 512:1536], AF.Sigmoid), s_ev1)
                    P.wait("scalar", s_q[0], 16 * gt)
                    P.op("scalar", lambda s: s.copy(q_bf[:, 0, :], ps_a[:, 1536:2560]), s_ev1)
                    P.wait("sync", s_ev1, 2 * (t + 1))
                    P.dma("sync", q_hbm[t, :, 1024 * half:1024 * (half + 1)], q_bf[:, 0, :], s_q[0])
                    if debug and gt == 0:
                        P.dma("sync", dbg["kv0"], kv_bf[:], s_out)
                        P.dma("sync", dbg["mb0"], mb_bf[:], s_out)
                        P.dma("sync", dbg["q0"], q_bf[:, 0, :], s_out)
                    if t > 0:
                        P.wait("scalar", s_F, t)
                    P.wait("scalar", s_ev1, 2 * (t + 1))
                    kview = kv_bf.ap().rearrange("p (j d) -> p j d", j=4)
                    P.op("scalar", lambda s, kview=kview: s.activation(
                        xe2[:, :, 3:131], kview[:, 0:2, :], AF.Relu))
                    P.op("scalar", lambda s, kview=kview: s.activation(
                        xe2[:, :, 131:259], kview[:, 0:2, :], AF.Relu, scale=-1.0), s_relu)
                    P.wait("vector", s_relu, t + 1)
                    P.op("vector", lambda v: v.tensor_copy(xe2[:, :, 0:3], xe2[:, :, 256:259]))
                    P.bar("vector")
                    if half == 1 and t == 0:
                        P.wait("vector", s_endA1)
                    for j in (1, 2, 3):
                        P.op("vector", lambda v, j=j, t=t: v.tensor_tensor(
                            F_sb[:, t, :, 256 * (j - 1):256 * j], xe2[:, :, 3:259],
                            xe2[:, :, 3 - j:259 - j], ALU.mult), s_F if j == 3 else None)
                    if debug and gt == 0:
                        P.wait("sync", s_F, 1)
                        P.dma("sync", dbg["F0"], F_sb[:, 0, :, :], s_out)
                    P.wait("scalar", s_F, t + 1)
                    if debug and gt == 1:
                        P.wait("scalar", s_out, s_out.count)
                    if t > 0:
                        P.wait("scalar", s_chain, 2 * t)
                    for j in range(2):
                        P.op("scalar", lambda s, j=j, t=t: s.activation(
                            sq_scr[:, j, :], F_sb[:, t, j, :], AF.Square, accum_out=sstat[:, j:j + 1]))
                    P.bar("scalar")
                    P.op("scalar", lambda s: s.activation(sstat[:, 2:4], sstat[:, 0:2], AF.Sqrt), s_ssq)
                    P.wait("vector", s_ssq, 2 * t + 1)
                    P.op("vector", lambda v: v.tensor_scalar(
                        sstat[:, 22:24], sstat[:, 2:4], EPS, None, ALU.mult))
                    P.op("vector", lambda v: v.reciprocal(sstat[:, 4:6], sstat[:, 2:4]), s_ssq)
                    P.bar("vector")
                    P.wait("tensor", s_F, t + 1)
                    fswaits = [(s_num, 4 * t)] if t > 0 else []
                    transposes([F_sb.ap()[:, t, j, 128 * c:128 * (c + 1)]
                                for j in range(2) for c in range(6)], fT, swaits=fswaits)
                    if half == 0 and t + 1 < NT:
                        a1_front(t + 1)
                    P.wait("tensor", s_nA)
                    if t == 0:
                        P.wait("tensor", s_mA)
                    if half == 0 and t == 0:
                        P.wait("tensor", s_ntr, 2)
                    for j in range(2):
                        if t > 0 or j > 0:
                            P.wait("tensor", s_chain, 2 * t + j)
                        for c in range(6):
                            P.op("tensor", lambda t_, j=j, c=c: t_.matmul(
                                ps_n[:], fT[:, 6 * j + c, :], memA[:, j, c, :],
                                start=(c == 0), stop=(c == 5)), s_num if c == 5 else None)
                        for c in range(6):
                            P.op("tensor", lambda t_, j=j, c=c: t_.matmul(
                                ps_dot[:, 4 * j:4 * j + 4], fT[:, 6 * j + c, :], normA[:, j, c, :],
                                start=(c == 0), stop=(c == 5)), s_num if c == 5 else None)
                        P.wait("vector", s_num, 4 * t + 2 * (j + 1))
                        P.wait("vector", s_ssq, 2 * (t + 1))
                        invn = sstat.ap()[:, 4 + j:5 + j]
                        epsn = sstat.ap()[:, 22 + j:23 + j]
                        den = sstat.ap()[:, 6:10]
                        P.op("vector", lambda v, j=j, epsn=epsn, den=den: v.tensor_scalar(
                            den, ps_dot[:, 4 * j:4 * j + 4], 1.0, epsn, ALU.mult, ALU.add))
                        P.bar("vector")
                        P.op("vector", lambda v, den=den: v.reciprocal(sstat[:, 10:14], den))
                        P.bar("vector")
                        P.op("vector", lambda v: v.tensor_scalar(
                            sstat[:, 14:18], sstat[:, 10:14], -1.0, None, ALU.mult))
                        P.bar("vector")
                        def stt_mv(hl, j=j, t=t):
                            P.op("vector", lambda v, hl=hl, j=j: v.scalar_tensor_tensor(
                                mv_scr[:, hl, :], ps_n[:, 128 * hl:128 * (hl + 1)],
                                sstat[:, 14 + hl:15 + hl],
                                kv_bf[:, 256 + 128 * j:384 + 128 * j], ALU.mult, ALU.add))
                        def stt_wmv(hl, j=j, t=t, invn=invn):
                            h8 = 4 * j + hl
                            P.op("vector", lambda v, hl=hl, h8=h8, t=t, invn=invn:
                                 v.scalar_tensor_tensor(
                                wmv[:, t, h8, :], mv_scr[:, hl, :], invn,
                                mb_bf[:, 128 * h8:128 * (h8 + 1)], ALU.mult, ALU.mult))
                        stt_mv(0)
                        stt_mv(1)
                        stt_mv(2)
                        stt_mv(3)
                        P.op("vector", lambda v, den=den, invn=invn: v.tensor_scalar(
                            sstat[:, 18:22], den, invn, None, ALU.mult))
                        P.bar("vector")
                        stt_wmv(0)
                        stt_wmv(1)
                        stt_wmv(2)
                        stt_wmv(3)
                        P.op("vector", lambda v: v.tensor_scalar(
                            sstat[:, 14:18], sstat[:, 18:22], -1.0, 1.0, ALU.mult, ALU.add))
                        P.bar("vector")
                        P.op("vector", lambda v: v.tensor_scalar(
                            sstat[:, 18:22], sstat[:, 14:18], 0.0, 1.0, ALU.max, ALU.min))
                        P.bar("vector")
                        P.op("vector", lambda v, t=t, j=j, invn=invn: v.tensor_scalar(
                            coefn[:, t, 4 * j:4 * j + 4], sstat[:, 18:22], invn,
                            None, ALU.mult), s_chain)
                    if debug and gt == 0:
                        P.wait("sync", s_chain, 2)
                        P.dma("sync", dbg["nst0"], sstat[:, 0:24], s_out)
                        P.dma("sync", dbg["wmv0"], wmv[:, 0, :, :], s_out)
                        P.dma("sync", dbg["coefn0"], coefn[:, 0, :], s_out)
                    if half == 1 and 2 <= t <= 5:
                        for hh in (2 * (t - 2), 2 * (t - 2) + 1):
                            build_head_v(hh)

                P.wait("tensor", s_chain)
                P.wait("tensor", s_ev1)
                s_dmm = P.sem(f"dmm{half}")
                s_dev = P.sem(f"dev{half}")
                s_dwr = [P.sem(f"dwr{half}_0"), P.sem(f"dwr{half}_1")]
                half_sems[half]["dev"] = s_dev
                half_sems[half]["dwr"] = s_dwr
                for c in range(6):
                    reg = (c % 2) * 1024
                    if c >= 1:
                        P.wait("tensor", s_dev, c)
                    for tt in range(NT):
                        for j in range(2):
                            P.op("tensor", lambda t_, c=c, tt=tt, j=j, reg=reg: t_.matmul(
                                ps_a[:, reg + 512 * j:reg + 512 * (j + 1)],
                                F_sb[:, tt, j, 128 * c:128 * (c + 1)],
                                wmv[:, tt, 4 * j:4 * j + 4, :],
                                start=(tt == 0), stop=(tt == NT - 1)), None)
                            dps = ps_dot[:, 0:4] if j == 0 else ps_n.ap()[:, 0:4]
                            P.op("tensor", lambda t_, c=c, tt=tt, j=j, dps=dps: t_.matmul(
                                dps, F_sb[:, tt, j, 128 * c:128 * (c + 1)],
                                coefn[:, tt, 4 * j:4 * j + 4],
                                start=(tt == 0), stop=(tt == NT - 1)),
                                s_dmm if (tt == NT - 1 and j == 1) else None)
                    P.wait("scalar", s_dmm, c + 1)
                    if c >= 1:
                        P.wait("scalar", s_dwr[0], 16 * c)
                    P.op("scalar", lambda s, c=c, reg=reg: s.copy(
                        dstage[:, 0, 0:1024], ps_a[:, reg:reg + 1024]))
                    P.op("scalar", lambda s, c=c: s.copy(dstage[:, 0, 1024:1028], ps_dot[:, 0:4]))
                    P.op("scalar", lambda s, c=c: s.copy(dstage[:, 0, 1028:1032], ps_n.ap()[:, 0:4]), s_dev)
                    P.wait("sync", s_dev, c + 1)
                    P.dma("sync", d_loc[half][c], dstage[:, 0, :], s_dwr[0])
                if half == 1:
                    P.wait("gpsimd", s_proj, 8)
                    s_wg.bump(16)
                    P.emit("gpsimd", lambda e: e.dma_start(
                        out=wg_pre[0][:], in_=wgP).then_inc(s_wg.h, 16))
                    s_wg.bump(16)
                    P.emit("gpsimd", lambda e: e.dma_start(
                        out=wg_pre[1][:], in_=bg).then_inc(s_wg.h, 16))
                P.wait("gpsimd", s_dwr[0])
                P.wait("gpsimd", s_dwr[1])
                P.op("gpsimd", lambda g, half=half: g.collective_compute(
                    "AllReduce", ALU.add, replica_groups=[list(range(NCORE))],
                    ins=[d_loc[half]], outs=[d_red[half]]), s_cc[half])
                P.op("tensor", lambda t: t.nop(), s_endA1 if half == 0 else s_endA2)
                if half == 1:
                    P.wait("sync", s_q[0])
                    P.op("sync", lambda e: e.nop(), s_endA2)

        esB = contextlib.ExitStack()
        with esB:
            def sbB(name, shape, dt):
                return esB.enter_context(nc.sbuf_tensor(name, shape, dt))

            wg_sb = sbB("wg_sb", [128, NCH, HID], BF16)
            bias_g = sbB("bias_g", [1, HID], BF16)
            wg_pre[0] = wg_sb
            wg_pre[1] = bias_g
            memN_b = sbB("memN_b", [128, 2, 6, 512], BF16)
            memNs[1] = memN_b
            q_in = sbB("q_in", [128, 2, HID], BF16)
            xe4 = sbB("xe4", [128, 2, 4, 262], BF16)
            Fq = sbB("Fq", [128, 2, 4, FD], BF16)
            fTq = sbB("fTq", [128, 2, 24, 128], BF16)
            gate_sb = sbB("gate_sb", [128, HID], F32)
            hs_sb = sbB("hs_sb", [128, HID], F32)
            mhg = sbB("mhg", [128, HID], F32)
            ln2_sb = mhg
            y_sb = sbB("y_sb", [128, HID], F32)

            s_hs2 = P.sem("hs2")
            s_den = P.sem("den")
            s_gproj = P.sem("gproj")
            s_gate = P.sem("gate")
            s_qrd = [P.sem("qrd0"), P.sem("qrd1")]
            s_relu2 = P.sem("relu2")
            s_Fq = P.sem("Fq")
            s_ssq2 = P.sem("ssq2")
            s_numa = P.sem("numa")
            s_mhg = P.sem("mhg")
            s_attn = P.sem("attn")
            s_ln2 = P.sem("ln2")
            s_yv = P.sem("yv")
            s_ywr = [P.sem("ywr0"), P.sem("ywr1")]

            for hh in range(8, 16):
                build_head_dma(hh)

            def b1_loads(u):
                us = u % 2
                if u < 2:
                    P.wait("sync", s_hs, 8)
                else:
                    P.wait("sync", s_yv, u - 1)
                P.dma("sync", x_sb[:, us, :], xt[u], s_x[us])
                if u >= 2:
                    P.wait("sync", s_relu2, 4 * u - 4)
                if u == 0:
                    P.wait("sync", s_endA2)
                P.dma("sync", q_in[:, us, :], q_hbm[u], s_qrd[us])
                if u >= 2:
                    P.wait("sync", s_gproj, u - 1)
                P.dma("sync", hsT[:, us], ht_hbm[u], s_ht[us])

            b1_loads(0)
            fq_te = {}
            for t in range(NT):
                slot = t % 2
                if t + 1 < NT:
                    b1_loads(t + 1)
                if debug and t in (1, 2):
                    P.wait("scalar", s_out, s_out.count)
                    P.wait("vector", s_out, s_out.count)
                if t == 0:
                    P.wait("scalar", s_endA2)
                P.wait("tensor", s_wg)
                P.wait("tensor", s_ht[slot], 16 * (4 + t // 2 + 1))
                if t == 0:
                    P.wait("tensor", half_sems[1]["dev"], 6)
                    P.wait("tensor", s_endA2)
                if t > 0:
                    P.wait("tensor", s_gate, t)
                for c in range(NCH + 1):
                    last = c == NCH
                    lhs = ones_r[:] if last else hsT[:, slot, c, :]
                    for wi in range(4):
                        rhs = (bias_g[:, 512 * wi:512 * (wi + 1)] if last
                               else wg_sb[:, c, 512 * wi:512 * (wi + 1)])
                        P.op("tensor", lambda t_, lhs=lhs, rhs=rhs, wi=wi, c=c, last=last:
                             t_.matmul(ps_a[:, 512 * wi:512 * (wi + 1)], lhs, rhs,
                                       start=(c == 0), stop=last),
                             s_gproj if (last and wi == 3) else None)
                def emit_gate(t=t):
                    P.wait("scalar", s_gproj, t + 1)
                    if t > 0:
                        P.wait("scalar", s_mhg, 4 * t)
                        P.wait("scalar", s_ln2, t)
                    P.op("scalar", lambda s: s.activation(
                        gate_sb[:], ps_a[:, 0:2048], AF.Sigmoid), s_gate)
                    if debug and t == 0:
                        P.wait("sync", s_gate, 1)
                        P.dma("sync", dbg["gate0"], gate_sb[:], s_out)
                for g in range(4):
                    gi = 4 * t + g
                    par = g % 2
                    if t == 0 and g == 2:
                        for hh in range(8, 12):
                            build_head_v(hh)
                    if t == 0 and g == 3:
                        for hh in range(12, 16):
                            build_head_v(hh)
                    P.wait("scalar", s_qrd[slot], 16 * (t // 2 + 1))
                    if gi > 1:
                        P.wait("scalar", s_Fq, gi - 1)
                    qv = q_in.ap()[:, slot, :].rearrange("p (g4 d) -> p g4 d", d=128)
                    P.op("scalar", lambda s, g=g, par=par, qv=qv: s.activation(
                        xe4[:, par, :, 3:131], qv[:, 4 * g:4 * g + 4, :], AF.Relu))
                    P.op("scalar", lambda s, g=g, par=par, qv=qv: s.activation(
                        xe4[:, par, :, 131:259], qv[:, 4 * g:4 * g + 4, :], AF.Relu, scale=-1.0), s_relu2)
                    P.wait("vector", s_relu2, gi + 1)
                    P.op("vector", lambda v, par=par: v.tensor_copy(
                        xe4[:, par, :, 0:3], xe4[:, par, :, 256:259]))
                    P.bar("vector")
                    if gi > 1:
                        P.wait("vector", s_te, fq_te.get(par, 0))
                        P.wait("vector", s_ssq2, gi - 1)
                    for j in (1, 2, 3):
                        P.op("vector", lambda v, j=j, par=par: v.tensor_tensor(
                            Fq[:, par, :, 256 * (j - 1):256 * j], xe4[:, par, :, 3:259],
                            xe4[:, par, :, 3 - j:259 - j], ALU.mult), s_Fq if j == 3 else None)
                    P.wait("scalar", s_Fq, gi + 1)
                    if gi > 1:
                        P.wait("scalar", s_mhg, gi - 1)
                    sreg = 8 * par
                    for hl in range(4):
                        P.op("scalar", lambda s, hl=hl, sreg=sreg, par=par: s.activation(
                            sq_scr[:, hl % 2, :], Fq[:, par, hl, :], AF.Square,
                            accum_out=sstat[:, sreg + hl:sreg + hl + 1]))
                        if hl == 1:
                            P.bar("scalar")
                    P.bar("scalar")
                    P.op("scalar", lambda s, sreg=sreg: s.activation(
                        sstat[:, sreg + 4:sreg + 8], sstat[:, sreg:sreg + 4], AF.Sqrt), s_ssq2)
                    if g == 0:
                        emit_gate()
                    P.wait("tensor", s_Fq, gi + 1)
                    fqswaits = [(s_numa, gi - 1)] if gi > 1 else []
                    transposes([Fq.ap()[:, par, hl, 128 * c:128 * (c + 1)]
                                for hl in range(4) for c in range(6)],
                               fTq.ap()[:, par], swaits=fqswaits)
                    fq_te[par] = s_te.count
                    if gi == 0:
                        P.wait("tensor", half_sems[1]["dev"], 6)
                    if gi > 1:
                        P.wait("tensor", s_mhg, gi - 1)
                    P.wait("tensor", s_memN, 4 * (g + 1))
                    memg = memNs[g // 2]
                    nbank = (ps_n.ap() if par == 0 else ps_a.ap()[:, 2048:2560])
                    for hl in range(4):
                        for c in range(6):
                            P.op("tensor", lambda t_, g=g, c=c, hl=hl, memg=memg, nb=nbank, par=par: t_.matmul(
                                nb[:, 128 * hl:128 * (hl + 1)], fTq[:, par, 6 * hl + c, :],
                                memg[:, g % 2, c, 128 * hl:128 * (hl + 1)],
                                start=(c == 0), stop=(c == 5)), None)
                    if gi > 0:
                        P.wait("tensor", s_den, gi)
                    for hl in range(4):
                        for c in range(6):
                            P.op("tensor", lambda t_, g=g, c=c, hl=hl, par=par: t_.matmul(
                                ps_dot[:, 4 * par + hl:4 * par + hl + 1], fTq[:, par, 6 * hl + c, :],
                                normN[:, g, c, hl:hl + 1],
                                start=(c == 0), stop=(c == 5)),
                                s_numa if (hl == 3 and c == 5) else None)
                    P.wait("vector", s_numa, gi + 1)
                    P.wait("vector", s_ssq2, gi + 1)
                    n4 = sstat.ap()[:, sreg + 4:sreg + 8]
                    den = sstat.ap()[:, 16:20]
                    P.op("vector", lambda v, n4=n4, den=den, par=par: v.scalar_tensor_tensor(
                        den, n4, EPS, ps_dot[:, 4 * par:4 * par + 4], ALU.mult, ALU.add), s_den)
                    P.bar("vector")
                    P.op("vector", lambda v, den=den: v.reciprocal(sstat[:, 20:24], den))
                    P.bar("vector")
                    P.wait("vector", s_gate, t + 1)
                    for hl in range(4):
                        hcol = 128 * (4 * g + hl)
                        P.op("vector", lambda v, hl=hl, hcol=hcol, nb=nbank: v.scalar_tensor_tensor(
                            mhg[:, hcol:hcol + 128], nb[:, 128 * hl:128 * (hl + 1)],
                            sstat[:, 20 + hl:21 + hl], gate_sb[:, hcol:hcol + 128],
                            ALU.mult, ALU.mult), s_mhg if hl == 3 else None)
                P.wait("scalar", s_x[slot], 16 * (4 + t // 2 + 1))
                if t >= 1:
                    P.wait("scalar", s_attn, t)
                P.op("scalar", lambda s, slot=slot, t=t: s.activation(
                    hs_sb[:], x_sb[:, slot, :], AF.Identity,
                    bias=msb[:, 2 * t + 1:2 * t + 2], scale=msb[:, 2 * t:2 * t + 1]), s_hs2)
                if debug and t == 0:
                    P.wait("vector", s_out, s_out.count)
                if t > 0:
                    P.wait("vector", s_ln2, t)
                P.wait("vector", s_hs2, t + 1)
                P.bar("vector")
                P.op("vector", lambda v: v.tensor_tensor(gate_sb[:], mhg[:], hs_sb[:], ALU.add), s_attn)
                P.bar("vector")
                if debug and t == 0:
                    P.wait("sync", s_attn, 1)
                    P.dma("sync", dbg["mhg0"], gate_sb[:], s_out)
                ln_stats(gate_sb.ap(), lns[:, 5:6], lns[:, 6:7])
                P.wait("scalar", s_lnd)
                P.op("scalar", lambda s: s.activation(
                    ln2_sb[:], gate_sb[:], AF.Identity, bias=lns[:, 6:7], scale=lns[:, 5:6]), s_ln2)
                P.wait("vector", s_ln2, t + 1)
                P.wait("vector", s_ywr[0], 16 * t)
                P.op("vector", lambda v, slot=slot: v.tensor_tensor(
                    y_sb[:], ln2_sb[:], x_sb[:, slot, :], ALU.add), s_yv)
                P.wait("sync", s_yv, t + 1)
                P.dma("sync", y_hbm[t], y_sb[:], s_ywr[0])
                if debug and t == 0:
                    P.dma("sync", dbg["y0"], y_sb[:], s_out)
            if debug:
                P.wait("sync", s_memN, 16)
                P.dma("sync", dbg["memN"].rearrange("p g c x -> p (g c x)")
                      .rearrange("p (g2 r) -> p g2 r", g2=2)[:, 0, :], memN_a.ap()
                      .rearrange("p g c x -> p (g c x)"), s_out)
                P.dma("sync", dbg["memN"].rearrange("p g c x -> p (g c x)")
                      .rearrange("p (g2 r) -> p g2 r", g2=2)[:, 1, :], memN_b.ap()
                      .rearrange("p g c x -> p (g c x)"), s_out)
                P.dma("sync", dbg["normN"], normN[:], s_out)
            P.op("tensor", lambda t: t.nop(), s_endB1)
            P.op("vector", lambda v: v.nop(), s_endB1)
            P.op("scalar", lambda s: s.nop(), s_endB1)
            P.wait("sync", s_ywr[0])
            P.wait("sync", s_qrd[0])
            P.wait("sync", s_qrd[1])
            if debug:
                P.wait("sync", s_out, s_out.count)
            P.op("sync", lambda e: e.nop(), s_endB1)

        x_es.close()
        esC = contextlib.ExitStack()
        with esC:
            def sbC(name, shape, dt):
                return esC.enter_context(nc.sbuf_tensor(name, shape, dt))

            sT = sbC("sT", [128, NCH, HID], BF16)
            wf_st = sbC("wf_st", [128, 2, HID], F32)
            absw = sbC("absw", [128, HID], BF16)
            alpha_row = sbC("alpha_row", [1, HID], F32)
            inva = sbC("inva", [1, HID], F32)
            biasf_sb = sbC("biasf_sb", [1, HID], F32)
            bias_pre = sbC("bias_pre", [1, HID], BF16)
            alpha_bc = sbC("alpha_bc", [128, HID], F32)
            y_in = sbC("y_in", [128, 2, HID], F32)
            hs2b = sbC("hs2b", [128, HID], BF16)
            z_sb = sbC("z_sb", [128, HID], F32)
            ln4_sb = sbC("ln4_sb", [128, HID], F32)
            out_sb = sbC("out_sb", [128, HID], F32)

            s_wf = [P.sem("wf0"), P.sem("wf1")]
            s_bfl = P.sem("bfl")
            s_sign = P.sem("sign")
            s_amm = P.sem("amm")
            s_aev = P.sem("aev")
            s_bc = P.sem("bc")
            s_yrd = [P.sem("yrd0"), P.sem("yrd1")]
            s_hs3 = P.sem("hs3")
            s_fmm = P.sem("fmm")
            s_z = P.sem("z")
            s_ln4 = P.sem("ln4")
            s_outv = P.sem("outv")
            s_od = [P.sem("od0"), P.sem("od1")]

            P.wait("gpsimd", s_endB1)
            P.dma("gpsimd", biasf_sb[:], bfh, s_bfl)
            for c in range(NCH):
                slot = c % 2
                if c >= 2:
                    P.wait("gpsimd", s_sign, c - 1)
                P.dma("gpsimd", wf_st[:, slot, :],
                      wfT.rearrange("(c p) j -> p c j", p=128)[:, c, :], s_wf[slot])
                P.wait("scalar", s_wf[slot], 16 * (c // 2 + 1))
                if c == 0:
                    P.wait("scalar", s_endB1, 4)
                if c >= 1:
                    P.wait("scalar", s_amm, c)
                P.op("scalar", lambda s, c=c, slot=slot: s.activation(
                    sT[:, c, :], wf_st[:, slot, :], AF.Sign))
                P.op("scalar", lambda s, c=c, slot=slot: s.activation(
                    absw[:], wf_st[:, slot, :], AF.Abs), s_sign)
                P.wait("tensor", s_sign, c + 1)
                if c == 0:
                    P.wait("tensor", s_endB1, 4)
                    P.wait("tensor", s_z, 0)
                for wi in range(4):
                    P.op("tensor", lambda t_, wi=wi, c=c: t_.matmul(
                        ps_a.ap()[0:1, 512 * wi:512 * (wi + 1)], ones_c[:],
                        absw[:, 512 * wi:512 * (wi + 1)],
                        start=(c == 0), stop=(c == NCH - 1)),
                        s_amm if wi == 3 else None)
            P.wait("scalar", s_amm, NCH)
            P.op("scalar", lambda s: s.activation(
                alpha_row[:], ps_a.ap()[0:1, 0:2048], AF.Identity, scale=1.0 / HID), s_aev)
            P.wait("vector", s_aev)
            P.wait("vector", s_endB1, 4)
            P.op("vector", lambda v: v.reciprocal(inva[:], alpha_row[:]))
            P.wait("vector", s_bfl, 16)
            P.bar("vector")
            P.op("vector", lambda v: v.tensor_tensor(bias_pre[:], biasf_sb[:], inva[:], ALU.mult), s_aev)
            P.wait("tensor", s_aev, 2)
            for wi in range(4):
                P.op("tensor", lambda t_, wi=wi: t_.matmul(
                    ps_a[:, 512 * wi:512 * (wi + 1)], ones_rf[:],
                    alpha_row[:, 512 * wi:512 * (wi + 1)], start=True, stop=True),
                    s_bc if wi == 3 else None)
            P.wait("scalar", s_bc)
            P.op("scalar", lambda s: s.copy(alpha_bc[:], ps_a[:, 0:2048]), s_bc)
            if debug:
                P.wait("sync", s_aev, 2)
                P.dma("sync", dbg["alpha"], alpha_row[:], s_out)

            def b2_yload(t):
                slot = t % 2
                if t == 0:
                    P.wait("sync", s_endB1, 4)
                if t >= 2:
                    P.wait("sync", s_outv, t - 1)
                if t == 0:
                    P.wait("sync", s_ywr[0], 16 * 8)
                P.dma("sync", y_in[:, slot, :], y_hbm[t], s_yrd[slot])

            def b2_ln3(t):
                slot = t % 2
                if debug and t == 1:
                    P.wait("vector", s_out, s_out.count)
                P.wait("vector", s_yrd[slot], 16 * (t // 2 + 1))
                ln_stats(y_in.ap()[:, slot, :], lns[:, 5:6], lns[:, 6:7])
                P.wait("scalar", s_lnd)
                if t >= 1:
                    P.wait("scalar", s_te, s_te.count)
                P.op("scalar", lambda s, slot=slot: s.activation(
                    hs2b[:], y_in[:, slot, :], AF.Identity,
                    bias=lns[:, 6:7], scale=lns[:, 5:6]), s_hs3)

            def b2_ln4out(v):
                vslot = v % 2
                P.wait("vector", s_z, v + 1)
                P.bar("vector")
                ln_stats(z_sb.ap(), lns[:, 16:17], lns[:, 17:18])
                P.wait("scalar", s_lnd)
                P.op("scalar", lambda s: s.activation(
                    ln4_sb[:], z_sb[:], AF.Identity, bias=lns[:, 17:18], scale=lns[:, 16:17]), s_ln4)
                P.wait("vector", s_ln4, v + 1)
                if v >= 1:
                    P.wait("vector", s_od[0], 16 * v)
                P.op("vector", lambda v_, vslot=vslot: v_.tensor_tensor(
                    out_sb[:], ln4_sb[:], y_in[:, vslot, :], ALU.add), s_outv)
                P.wait("sync", s_outv, v + 1)
                P.dma("sync", outh.rearrange("(n p) d -> n p d", p=128)[v], out_sb[:], s_od[0])

            b2_yload(0)
            b2_ln3(0)
            for t in range(NT):
                slot = t % 2
                P.wait("tensor", s_hs3, t + 1)
                if t == 0:
                    P.wait("tensor", s_bc, 2)
                b2swaits = [(s_fmm, t - 1)] if t >= 2 else []
                transposes([hs2b.ap()[:, 128 * c:128 * (c + 1)] for c in range(NCH)],
                           hsT.ap()[:, slot], swaits=b2swaits)
                if t > 0:
                    P.wait("tensor", s_z, t)
                for c in range(NCH + 1):
                    last = c == NCH
                    lhs = ones_r[:] if last else hsT[:, slot, c, :]
                    for wi in range(4):
                        rhs = (bias_pre[:, 512 * wi:512 * (wi + 1)] if last
                               else sT[:, c, 512 * wi:512 * (wi + 1)])
                        P.op("tensor", lambda t_, lhs=lhs, rhs=rhs, wi=wi, c=c, last=last:
                             t_.matmul(ps_a[:, 512 * wi:512 * (wi + 1)], lhs, rhs,
                                       start=(c == 0), stop=last),
                             s_fmm if (last and wi == 3) else None)
                if t > 0:
                    b2_ln4out(t - 1)
                if t + 1 < NT:
                    b2_yload(t + 1)
                    b2_ln3(t + 1)
                P.wait("vector", s_fmm, t + 1)
                P.wait("vector", s_bc, 2)
                if debug and t == 1:
                    P.wait("vector", s_out, s_out.count)
                if t > 0:
                    P.wait("vector", s_ln4, t)
                P.op("vector", lambda v: v.tensor_tensor(z_sb[:], ps_a[:, 0:2048], alpha_bc[:], ALU.mult), s_z)
                if debug and t == 0:
                    P.wait("sync", s_z, 1)
                    P.dma("sync", dbg["z0"], z_sb[:], s_out)
            b2_ln4out(NT - 1)

            P.wait("gpsimd", s_od[0])
            if debug:
                P.wait("gpsimd", s_out)

        blk = es.enter_context(nc.Block())
        for s in P.sems:
            s.h = es.enter_context(nc.semaphore(s.name))

        @blk.tensor
        def _(e):
            for f in P.q["tensor"]:
                f(e)

        @blk.vector
        def _(e):
            for f in P.q["vector"]:
                f(e)

        @blk.scalar
        def _(e):
            for f in P.q["scalar"]:
                f(e)

        @blk.gpsimd
        def _(e):
            for f in P.q["gpsimd"]:
                f(e)

        @blk.sync
        def _(e):
            for f in P.q["sync"]:
                f(e)

    return nc, dbg


_NC_CACHE = {}


def _get_nc(debug=False):
    key = bool(debug)
    if key not in _NC_CACHE:
        _NC_CACHE[key] = build_nc(debug)
    return _NC_CACHE[key]


def make_in_maps(inputs):
    x = np.ascontiguousarray(np.asarray(inputs["L0_attn_hidden_states"], np.float32).reshape(8192, HID))
    kc = np.asarray(inputs["L0_attn_Wk"], np.float32).T.reshape(NCH, 128, 512)
    vc = np.asarray(inputs["L0_attn_Wv"], np.float32).T.reshape(NCH, 128, 512)
    mc = np.asarray(inputs["L0_attn_Wmb"], np.float32).T.reshape(NCH, 128, HID)
    qc = np.asarray(inputs["L0_attn_Wq"], np.float32).T.reshape(NCH, 128, HID)
    mem = np.asarray(inputs["L0_attn_memory"], np.float32)
    com = {
        "WfT": np.ascontiguousarray(np.asarray(inputs["L0_ffn_weight"], np.float32).T),
        "WgP": np.ascontiguousarray(
            np.asarray(inputs["L0_attn_Wg"], np.float32).T.reshape(NCH, 128, HID)
            .transpose(1, 0, 2)),
        "memP4": np.ascontiguousarray(
            mem.reshape(2, 2, 4, 6, 128, 128).transpose(0, 1, 4, 3, 2, 5)
            .reshape(4, 128, 6, 512)),
        "memPh": np.ascontiguousarray(mem.reshape(16, 6, 128, 128).transpose(0, 2, 1, 3)),
        "bk": np.asarray(inputs["L0_attn_bk"], np.float32).reshape(1, 512),
        "bv": np.asarray(inputs["L0_attn_bv"], np.float32).reshape(1, 512),
        "bm": np.asarray(inputs["L0_attn_bmb"], np.float32).reshape(1, HID),
        "bq": np.asarray(inputs["L0_attn_bq"], np.float32).reshape(1, HID),
        "bg": np.asarray(inputs["L0_attn_bg"], np.float32).reshape(1, HID),
        "bf": np.asarray(inputs["L0_ffn_bias"], np.float32).reshape(1, HID),
        "norm96": np.ascontiguousarray(np.asarray(inputs["L0_attn_norm"], np.float32).reshape(96, 128)),
        "ident": np.eye(128, dtype=np.float32),
    }
    for h in range(2):
        com[f"Wa{h}"] = np.ascontiguousarray(np.concatenate(
            [kc[:, :, 256 * h:256 * (h + 1)], vc[:, :, 256 * h:256 * (h + 1)],
             mc[:, :, 1024 * h:1024 * (h + 1)], qc[:, :, 1024 * h:1024 * (h + 1)]],
            axis=2).transpose(1, 0, 2))
    in_maps = []
    for c in range(NCORE):
        m = dict(com)
        m["x_part"] = np.ascontiguousarray(x[c * TOK:(c + 1) * TOK])
        in_maps.append(m)
    return in_maps


def kernel(**inputs):
    nc, _ = _get_nc(False)
    in_maps = make_in_maps(inputs)
    res = run_bass_kernel_spmd(nc, in_maps, core_ids=list(range(NCORE)))
    out = np.concatenate([r["out_part"] for r in res.results], axis=0)
    return out.reshape(2, 4096, HID)


# version 07
# speedup vs baseline: 1.2895x; 1.0676x over previous
import contextlib
import numpy as np
import concourse.bass as bass
import concourse.mybir as mybir
from concourse.bass_utils import run_bass_kernel_spmd

F32 = mybir.dt.float32
BF16 = mybir.dt.bfloat16
AF = mybir.ActivationFunctionType
ALU = mybir.AluOpType
AX = mybir.AxisListType

NCORE = 8
TOK = 1024
NT = 8
HID = 2048
NCH = 16
D = 128
H = 16
FD = 768
EPS = 1e-8
LNEPS = 1e-5


class Sem:
    __slots__ = ("name", "count", "h")

    def __init__(self, name):
        self.name = name
        self.count = 0
        self.h = None

    def bump(self, n=1):
        self.count += n
        return self.count


class Prog:
    ENGINES = ("tensor", "vector", "scalar", "gpsimd", "sync")

    def __init__(self):
        self.q = {e: [] for e in self.ENGINES}
        self.sems = []
        self.barsems = {}

    def sem(self, name):
        s = Sem(name)
        self.sems.append(s)
        return s

    def emit(self, eng, fn):
        self.q[eng].append(fn)

    def wait(self, eng, sem, val=None):
        v = sem.count if val is None else val
        if v <= 0:
            return
        self.emit(eng, lambda e, s=sem, vv=v: e.wait_ge(s.h, vv))

    def op(self, eng, builder, sem=None, inc=1):
        if sem is not None:
            sem.bump(inc)
        extra = []

        def f(e, b=builder, s=sem, n=inc, x=extra):
            inst = b(e)
            if s is not None:
                inst.then_inc(s.h, n)
            for (xs, xn) in x:
                inst.then_inc(xs.h, xn)

        self.emit(eng, f)
        self.lastop = getattr(self, 'lastop', {})
        self.lastop[eng] = (extra, sem, sem.count if sem is not None else 0)

    def nop(self, eng):
        self.emit(eng, lambda e: e.nop())

    def drain(self, eng):
        self.emit(eng, lambda e: e.drain())

    def bar(self, eng):
        extra, psem, pval = self.lastop[eng]
        if psem is not None:
            self.emit(eng, lambda e, ss=psem, v=pval: e.wait_ge(ss.h, v))
            return
        s = self.barsems.get(eng)
        if s is None:
            s = self.sem(f"bar_{eng}")
            self.barsems[eng] = s
        c = s.bump(1)
        extra.append((s, 1))
        self.emit(eng, lambda e, ss=s, v=c: e.wait_ge(ss.h, v))

    def dma(self, qeng, out_ap, in_ap, sem, slow=False):
        sem.bump(16)
        self.emit(qeng, lambda e, o=out_ap, i=in_ap, s=sem, sl=slow:
                  e.dma_start(out=o, in_=i, allow_slow_non_contiguous=sl).then_inc(s.h, 16))


def build_nc(debug=False):
    nc = bass.Bass("TRN2", target_bir_lowering=False)
    P = Prog()

    xh = nc.dram_tensor("x_part", [TOK, HID], F32, kind="ExternalInput").ap()
    waH = [nc.dram_tensor(f"Wa{h}", [128, NCH, 2560], F32, kind="ExternalInput").ap()
           for h in range(2)]
    wgP = nc.dram_tensor("WgP", [128, NCH, HID], F32, kind="ExternalInput").ap()
    memP4h = nc.dram_tensor("memP4", [4, 128, 6, 512], F32, kind="ExternalInput").ap()
    memPh = nc.dram_tensor("memPh", [16, 128, 6, 128], F32, kind="ExternalInput").ap()
    wfT = nc.dram_tensor("WfT", [HID, HID], F32, kind="ExternalInput").ap()
    bk = nc.dram_tensor("bk", [1, 512], F32, kind="ExternalInput").ap()
    bv = nc.dram_tensor("bv", [1, 512], F32, kind="ExternalInput").ap()
    bm = nc.dram_tensor("bm", [1, HID], F32, kind="ExternalInput").ap()
    bq = nc.dram_tensor("bq", [1, HID], F32, kind="ExternalInput").ap()
    bg = nc.dram_tensor("bg", [1, HID], F32, kind="ExternalInput").ap()
    bfh = nc.dram_tensor("bf", [1, HID], F32, kind="ExternalInput").ap()
    normh = nc.dram_tensor("norm96", [96, 128], F32, kind="ExternalInput").ap()
    identh = nc.dram_tensor("ident", [128, 128], F32, kind="ExternalInput").ap()
    outh = nc.dram_tensor("out_part", [TOK, HID], F32, kind="ExternalOutput").ap()

    q_hbm = nc.dram_tensor("q_scr", [NT, 128, HID], BF16).ap()
    ht_hbm = nc.dram_tensor("ht_scr", [NT, 128, NCH * 128], BF16).ap()
    y_hbm = nc.dram_tensor("y_scr", [NT, 128, HID], F32).ap()
    d_loc = [nc.dram_tensor(f"d_loc{i}", [6, 128, 1032], F32).ap() for i in range(2)]
    d_red = [nc.dram_tensor(f"d_red{i}", [6, 128, 1032], F32, addr_space="Shared").ap() for i in range(2)]

    dbg = {}
    if debug:
        def dbgt(name, shape, dt=F32):
            dbg[name] = nc.dram_tensor("dbg_" + name, shape, dt, kind="ExternalOutput").ap()
        dbgt("hs0", [128, HID], BF16)
        dbgt("msb", [128, 16])
        dbgt("lns0", [128, 28])
        dbgt("bn0", [128, 24])
        dbgt("x0", [128, HID])
        dbgt("lsnap", [128, 28])
        dbgt("kv0", [128, 512], BF16)
        dbgt("mb0", [128, 1024], BF16)
        dbgt("q0", [128, 1024], BF16)
        dbgt("F0", [128, 2, FD], BF16)
        dbgt("nst0", [128, 24])
        dbgt("wmv0", [128, 8, 128], BF16)
        dbgt("coefn0", [128, 8], BF16)
        dbgt("memN", [128, 4, 6, 512], BF16)
        dbgt("normN", [128, 4, 6, 4], BF16)
        dbgt("mhg0", [128, HID])
        dbgt("gate0", [128, HID])
        dbgt("y0", [128, HID])
        dbgt("alpha", [1, HID])
        dbgt("z0", [128, HID])

    xt = xh.rearrange("(n p) d -> n p d", p=128)

    es = contextlib.ExitStack()
    with es:
        def sb(name, shape, dt):
            return es.enter_context(nc.sbuf_tensor(name, shape, dt))

        def pst(name, shape, dt):
            return es.enter_context(nc.psum_tensor(name, shape, dt))

        ps_a = pst("ps_a", [128, 2560], F32)
        ps_n = pst("ps_n", [128, 512], F32)
        ps_dot = pst("ps_dot", [128, 8], F32)
        ps_tr = pst("ps_tr", [128, 8, 128], BF16)

        ident_bf = sb("ident_bf", [128, 128], BF16)
        ident_f = sb("ident_f", [128, 128], F32)
        ones_r = sb("ones_r", [1, 128], BF16)
        ones_rf = sb("ones_rf", [1, 128], F32)
        ones_c = sb("ones_c", [128, 1], BF16)
        normF = sb("normF", [128, 96], F32)
        msb = sb("msb", [128, 16], F32)
        bn24 = sb("bn24", [128, 24], F32)
        lns = sb("lns", [128, 28], F32)
        sq_scr = sb("sq_scr", [128, 2, FD], F32)
        sstat = sb("sstat", [128, 24], F32)
        mv_scr = sb("mv_scr", [128, 4, 128], F32)
        lsnap = sb("lsnap", [128, 28], F32)
        hsb_sb = sb("hsb_sb", [128, HID], BF16)
        hsT = sb("hsT", [128, 2, NCH, 128], BF16)
        memN_a = sb("memN_a", [128, 2, 6, 512], BF16)
        memNs = [memN_a, None]
        normN = sb("normN", [128, 4, 6, 4], BF16)
        mstage = sb("mstage", [128, 2, 6, 128], BF16)
        dstage2 = sb("dstage2", [128, 2, 6, 129], BF16)
        norm96 = sb("norm96s", [96, 128], F32)
        x_es = contextlib.ExitStack()
        x_sb = x_es.enter_context(nc.sbuf_tensor("x_sb", [128, 2, HID], F32))

        s_const = P.sem("const")
        s_konst = P.sem("konst")
        s_ntr = P.sem("ntr")
        s_endA1 = P.sem("endA1")
        s_endA2 = P.sem("endA2")
        s_endB1 = P.sem("endB1")
        s_cc = [P.sem("cc0"), P.sem("cc1")]
        s_out = P.sem("out")
        s_x = [P.sem("x_s0"), P.sem("x_s1")]
        s_ht = [P.sem("ht_s0"), P.sem("ht_s1")]
        s_htw = [P.sem("htw_s0"), P.sem("htw_s1")]
        s_hs = P.sem("hs")
        s_hsb = P.sem("hsb")
        s_tr = P.sem("tr")
        s_te = P.sem("te")
        s_lnv = P.sem("lnv")
        s_lsn = P.sem("lsn")
        s_wg = P.sem("wg")
        s_mst = P.sem("mst")
        s_mstB = P.sem("mstB")
        s_mst2 = [s_mst, s_mstB]
        s_memN = P.sem("memN")
        s_lnq = P.sem("lnq")
        s_lnd = P.sem("lnd")

        P.dma("gpsimd", ident_bf[:], identh, s_const)
        P.dma("gpsimd", ident_f[:], identh, s_const)
        P.op("vector", lambda v: v.memset(ones_r[:], 1.0), s_konst)
        P.op("vector", lambda v: v.memset(ones_rf[:], 1.0), s_konst)
        P.op("vector", lambda v: v.memset(ones_c[:], 1.0), s_konst)
        P.op("vector", lambda v: v.memset(lns[:], 0.0), s_konst)
        P.op("vector", lambda v: v.memset(bn24[:], 0.0), s_konst)
        P.op("vector", lambda v: v.memset(sstat[:], 0.0), s_konst)
        P.op("vector", lambda v: v.memset(msb[:], 0.0), s_konst)

        def ln_stats(src_ap, rstd_col, negms_col):
            for i in range(4):
                P.op("vector", lambda v, i=i, a=src_ap: v.bn_stats(
                    bn24[:, 6 * i:6 * i + 6], a[:, 512 * i:512 * (i + 1)]))
            m8 = bn24.ap()[:, 1:24:3]
            M28 = bn24.ap()[:, 2:24:3]
            P.bar("vector")
            P.op("vector", lambda v, m8=m8: v.tensor_tensor(lns[:, 8:16], m8, m8, ALU.mult))
            P.op("vector", lambda v, m8=m8: v.reduce_sum(lns[:, 1:2], m8, axis=AX.X))
            P.bar("vector")
            P.op("vector", lambda v, M28=M28: v.scalar_tensor_tensor(
                lns[:, 20:28], lns[:, 8:16], 256.0, M28, ALU.mult, ALU.add))
            P.op("vector", lambda v: v.tensor_scalar(
                lns[:, 2:3], lns[:, 1:2], 0.125, None, ALU.mult))
            P.bar("vector")
            P.op("vector", lambda v: v.reduce_sum(lns[:, 0:1], lns[:, 20:28], axis=AX.X))
            P.op("vector", lambda v: v.tensor_tensor(
                lns[:, 3:4], lns[:, 2:3], lns[:, 2:3], ALU.mult))
            P.bar("vector")
            P.op("vector", lambda v: v.tensor_scalar(
                lns[:, 4:5], lns[:, 0:1], 1.0 / HID, LNEPS, ALU.mult, ALU.add))
            P.bar("vector")
            P.op("vector", lambda v: v.tensor_tensor(
                lns[:, 18:19], lns[:, 4:5], lns[:, 3:4], ALU.subtract))
            P.bar("vector")
            P.op("vector", lambda v: v.reciprocal(lns[:, 19:20], lns[:, 18:19]), s_lnv)
            P.wait("scalar", s_lnv)
            P.op("scalar", lambda s, rc=rstd_col: s.activation(rc, lns[:, 19:20], AF.Sqrt), s_lnq)
            P.wait("vector", s_lnq)
            P.op("vector", lambda v, rc=rstd_col, nm=negms_col: v.scalar_tensor_tensor(
                nm, lns[:, 2:3], -1.0, rc, ALU.mult, ALU.mult), s_lnd)

        def transposes(srcs, dst, swaits=()):
            n = len(srcs)
            r0 = 0
            while r0 < n:
                rn = min(8, n - r0)
                P.wait("tensor", s_te)
                for i in range(rn):
                    P.op("tensor", lambda t, i=i, a=srcs[r0 + i]: t.transpose(
                        ps_tr[:, i, :], a, ident_bf[:]), s_tr if i == rn - 1 else None)
                P.wait("scalar", s_tr)
                for sw in swaits:
                    P.wait("scalar", sw[0], sw[1])
                swaits = ()
                P.op("scalar", lambda s, r0=r0, rn=rn, d=dst: s.copy(
                    d[:, r0:r0 + rn, :], ps_tr[:, 0:rn, :]), s_te)
                r0 += rn
            P.wait("tensor", s_te)
            return s_te.count

        def build_head_dma(hh):
            P.wait("gpsimd", s_cc[hh // 8], 1)
            if hh >= 2:
                P.wait("gpsimd", s_memN, hh - 1)
            hl = hh % 8
            sl = hh % 2
            sm = s_mst2[sl]
            P.dma("gpsimd", mstage[:, sl, :, :], memPh[hh], sm)
            P.dma("gpsimd", dstage2[:, sl, :, 0:128],
                  d_red[hh // 8][:, :, 128 * hl:128 * (hl + 1)].rearrange("c p d -> p c d"),
                  sm)
            P.dma("gpsimd", dstage2[:, sl, :, 128:129],
                  d_red[hh // 8][:, :, 1024 + hl:1025 + hl].rearrange("c p d -> p c d"),
                  sm, slow=True)

        def build_head_v(hh):
            sl = hh % 2
            P.wait("vector", s_mst2[sl], 48 * (hh // 2 + 1))
            P.op("vector", lambda v, hh=hh, sl=sl: v.tensor_tensor(
                memNs[hh // 8][:, (hh // 4) % 2, :, 128 * (hh % 4):128 * (hh % 4 + 1)],
                mstage[:, sl, :, :], dstage2[:, sl, :, 0:128], ALU.add))
            P.op("vector", lambda v, hh=hh, sl=sl: v.tensor_tensor(
                normN[:, hh // 4, :, (hh % 4):(hh % 4) + 1],
                normF.ap()[:, 6 * hh:6 * (hh + 1)].rearrange("p (c o) -> p c o", o=1),
                dstage2[:, sl, :, 128:129], ALU.add), s_memN)

        P.dma("gpsimd", norm96[:], normh, s_const)
        P.wait("tensor", s_const)
        P.wait("tensor", s_konst)
        P.op("tensor", lambda t: t.transpose(
            ps_n.ap()[:, 0:96], norm96[0:96, :], ident_f[0:96, 0:96]), s_ntr)
        P.wait("scalar", s_ntr)
        P.op("scalar", lambda s: s.copy(normF[:], ps_n.ap()[:, 0:96]), s_ntr)

        wg_pre = [None, None]
        esA = contextlib.ExitStack()
        with esA:
            def sbA(name, shape, dt):
                return esA.enter_context(nc.sbuf_tensor(name, shape, dt))

            wA = sbA("wA", [128, NCH, 2560], BF16)
            bias_kvmb = sbA("bias_kvmb", [1, 1536], BF16)
            bias_q = sbA("bias_q", [1, 1024], BF16)
            memA = sbA("memA", [128, 2, 6, 512], BF16)
            normA = sbA("normA", [128, 2, 6, 4], BF16)
            kv_bf = sbA("kv_bf", [128, 512], BF16)
            mb_bf = sbA("mb_bf", [128, 1024], BF16)
            q_bf = sbA("q_bf", [128, 1, 1024], BF16)
            xe2 = sbA("xe2", [128, 2, 262], BF16)
            F_sb = sbA("F_sb", [128, NT, 2, FD], BF16)
            fT = sbA("fT", [128, 12, 128], BF16)
            wmv = sbA("wmv", [128, NT, 8, 128], BF16)
            coefn = sbA("coefn", [128, NT, 8], BF16)
            dstage = sbA("dstage", [128, 1, 1032], F32)

            s_q = [P.sem("q_s0")]
            half_sems = {}

            s_w4 = [P.sem(f"w4_{i}") for i in range(4)]
            s_pw = P.sem("pwA")
            s_mvd = P.sem("mvdA")
            s_denA = P.sem("denA")
            s_wB = P.sem("wBA")
            s_mA = P.sem("mAA")
            for half in range(2):
                s_wA = P.sem(f"wA{half}")
                half_sems[half] = {}
                s_nA = P.sem(f"nA{half}")
                s_proj = P.sem(f"proj{half}")
                s_ev1 = P.sem(f"ev1_{half}")
                s_F = P.sem(f"F{half}")
                s_relu = P.sem(f"relu{half}")
                s_ssq = P.sem(f"ssq{half}")
                s_num = P.sem(f"num{half}")
                s_chain = P.sem(f"chain{half}")
                s_stats = P.sem(f"stats{half}")
                half_sems[half]["proj"] = s_proj
                half_sems[half]["num"] = s_num
                half_sems[half]["chain"] = s_chain

                s_p7 = P.sem(f"p7_{half}")
                half_sems[half]["p7"] = s_p7
                if half == 0:
                    P.dma("gpsimd", bias_kvmb[:, 0:256], bk[:, 0:256], s_wB)
                    P.dma("gpsimd", bias_kvmb[:, 256:512], bv[:, 0:256], s_wB)
                    P.dma("gpsimd", bias_kvmb[:, 512:1536], bm[:, 0:1024], s_wB)
                    P.dma("gpsimd", bias_q[:], bq[:, 0:1024], s_wB)
                for c in range(NCH):
                    if half == 1:
                        P.wait("gpsimd", half_sems[0]["p7"], c + 1)
                    P.dma("gpsimd", wA[:, c, :], waH[half][:, c, :], s_w4[c // 4])
                if half == 1:
                    P.wait("gpsimd", half_sems[0]["proj"], 8)
                    P.dma("gpsimd", bias_kvmb[:, 0:256], bk[:, 256:512], s_wB)
                    P.dma("gpsimd", bias_kvmb[:, 256:512], bv[:, 256:512], s_wB)
                    P.dma("gpsimd", bias_kvmb[:, 512:1536], bm[:, 1024:2048], s_wB)
                    P.dma("gpsimd", bias_q[:], bq[:, 1024:2048], s_wB)
                for i in range(4):
                    P.wait("gpsimd", s_w4[i], 64 * (half + 1))
                P.op("gpsimd", lambda g: g.nop(), s_wA)
                if half == 1:
                    P.wait("gpsimd", half_sems[0]["num"], 32)
                for j in range(2):
                    for c6 in range(6):
                        P.dma("gpsimd", memA[:, j, c6, :], memP4h[2 * half + j][:, c6, :], s_mA)
                if half == 1:
                    for hh in range(0, 8):
                        build_head_dma(hh)
                P.wait("vector", s_ntr, 2)
                if half == 1:
                    P.wait("vector", half_sems[0]["num"], 32)
                P.op("vector", lambda v, half=half: v.tensor_copy(
                    normA[:],
                    normF.ap()[:, 48 * half:48 * (half + 1)]
                    .rearrange("p (j hl c) -> p j c hl", j=2, hl=4, c=6)), s_nA)

                def a1_front(u):
                    us = u % 2
                    if debug and u == 1:
                        P.wait("vector", s_out, s_out.count)
                        P.wait("scalar", s_out, s_out.count)
                    P.wait("vector", s_x[us], 16 * (u // 2 + 1))
                    ln_stats(x_sb.ap()[:, us, :], msb[:, 2 * u:2 * u + 1],
                             msb[:, 2 * u + 1:2 * u + 2])
                    if debug and u == 0:
                        P.bar("vector")
                        P.op("vector", lambda v: v.tensor_copy(lsnap[:], lns[:, 0:28]), s_lsn)
                    P.op("vector", lambda v: v.nop(), s_stats)
                    P.wait("scalar", s_x[us], 16 * (u // 2 + 1))
                    P.wait("scalar", s_lnd, s_lnd.count)
                    if u >= 1:
                        P.wait("scalar", s_te, s_te.count)
                    P.op("scalar", lambda s, us=us, u=u: s.activation(
                        hsb_sb[:], x_sb[:, us, :], AF.Identity,
                        bias=msb[:, 2 * u + 1:2 * u + 2], scale=msb[:, 2 * u:2 * u + 1]), s_hs)
                    if debug and u == 0:
                        P.wait("sync", s_hs, 1)
                        P.dma("sync", dbg["hs0"], hsb_sb[:], s_out)
                        P.dma("sync", dbg["msb"], msb[:, 0:16], s_out)
                        P.dma("sync", dbg["lns0"], lns[:, 0:28], s_out)
                        P.dma("sync", dbg["bn0"], bn24[:, 0:24], s_out)
                        P.dma("sync", dbg["x0"], x_sb[:, 0, :], s_out)
                        P.wait("sync", s_lsn, 1)
                        P.dma("sync", dbg["lsnap"], lsnap[:], s_out)

                if half == 0:
                    P.dma("sync", x_sb[:, 0, :], xt[0], s_x[0])
                    a1_front(0)
                else:
                    P.wait("sync", s_htw[0])
                    P.dma("sync", hsT[:, 0], ht_hbm[0], s_ht[0])
                    P.wait("sync", s_htw[1])
                    P.dma("sync", hsT[:, 1], ht_hbm[1], s_ht[1])
                for t in range(NT):
                    gt = 8 * half + t
                    slot = t % 2
                    if half == 0:
                        if t + 1 < NT:
                            if debug and t + 1 == 2:
                                P.wait("sync", s_out, s_out.count)
                            if t + 1 >= 2:
                                P.wait("sync", s_hs, t)
                            P.dma("sync", x_sb[:, (t + 1) % 2, :], xt[t + 1], s_x[(t + 1) % 2])
                        P.wait("tensor", s_hs, gt + 1)
                        htswaits = []
                        if t >= 2:
                            htswaits = [(s_proj, t - 1), (s_htw[slot], 16 * (t // 2))]
                        transposes([hsb_sb.ap()[:, 128 * c:128 * (c + 1)] for c in range(NCH)],
                                   hsT.ap()[:, slot], swaits=htswaits)
                        P.wait("sync", s_te, s_te.count)
                        P.dma("sync", ht_hbm[t], hsT[:, slot], s_htw[slot])
                    else:
                        if 2 <= t + 1 < NT:
                            P.wait("sync", s_proj, t)
                            P.dma("sync", hsT[:, (t + 1) % 2], ht_hbm[t + 1], s_ht[(t + 1) % 2])
                        if t == 0:
                            P.wait("tensor", half_sems[0]["dev"], 6)
                        P.wait("tensor", s_ht[slot], 16 * (t // 2 + 1))
                    if t > 0:
                        P.wait("tensor", s_wA, 1)
                        P.wait("tensor", s_ev1, 3 * t)
                    wins = [(0, 512, 0), (512, 1024, 0), (1024, 1536, 0),
                            (1536, 2048, 1), (2048, 2560, 1)]
                    for wi, (a, b, isq) in enumerate(wins):
                        if wi == 4 and 16 * half + 2 * t > 0:
                            P.wait("tensor", s_mvd, 16 * half + 2 * t)
                        for c in range(NCH + 1):
                            last = c == NCH
                            lhs = ones_r[:] if last else hsT[:, slot, c, :]
                            if t == 0 and wi == 0:
                                if last:
                                    P.wait("tensor", s_wB)
                                else:
                                    P.wait("tensor", s_w4[c // 4], 64 * (half + 1))
                            if last:
                                rhs = bias_q[:, a - 1536:b - 1536] if isq else bias_kvmb[:, a:b]
                            else:
                                rhs = wA[:, c, a:b]
                            psem = None
                            if last:
                                psem = s_proj if wi == 4 else s_pw
                            elif t == NT - 1 and wi == 4:
                                psem = s_p7
                            P.op("tensor", lambda t_, lhs=lhs, rhs=rhs, a=a, b=b, c=c, last=last:
                                 t_.matmul(ps_a[:, a:b], lhs, rhs, start=(c == 0), stop=last),
                                 psem)
                    P.wait("scalar", s_pw, 32 * half + 4 * t + 1)
                    if t > 0:
                        P.wait("scalar", s_chain, 2 * t)
                    if half == 1 and t == 0:
                        P.wait("scalar", s_endA1)
                        P.wait("scalar", half_sems[0]["dwr"][0], 96)
                    P.op("scalar", lambda s: s.copy(kv_bf[:], ps_a[:, 0:512]), s_ev1)
                    P.bar("scalar")
                    if t > 0:
                        P.wait("scalar", s_F, t)
                    kview = kv_bf.ap().rearrange("p (j d) -> p j d", j=4)
                    P.op("scalar", lambda s, kview=kview: s.activation(
                        xe2[:, :, 3:131], kview[:, 0:2, :], AF.Relu))
                    P.op("scalar", lambda s, kview=kview: s.activation(
                        xe2[:, :, 131:259], kview[:, 0:2, :], AF.Relu, scale=-1.0), s_relu)
                    P.wait("scalar", s_pw, 32 * half + 4 * t + 3)
                    P.op("scalar", lambda s: s.activation(mb_bf[:], ps_a[:, 512:1536], AF.Sigmoid), s_ev1)
                    P.wait("scalar", s_proj, t + 1)
                    P.wait("scalar", s_q[0], 16 * gt)
                    P.op("scalar", lambda s: s.copy(q_bf[:, 0, :], ps_a[:, 1536:2560]), s_ev1)
                    P.wait("sync", s_ev1, 3 * (t + 1))
                    P.dma("sync", q_hbm[t, :, 1024 * half:1024 * (half + 1)], q_bf[:, 0, :], s_q[0])
                    if debug and gt == 0:
                        P.dma("sync", dbg["kv0"], kv_bf[:], s_out)
                        P.dma("sync", dbg["mb0"], mb_bf[:], s_out)
                        P.dma("sync", dbg["q0"], q_bf[:, 0, :], s_out)
                    P.wait("vector", s_relu, t + 1)
                    P.op("vector", lambda v: v.tensor_copy(xe2[:, :, 0:3], xe2[:, :, 256:259]))
                    P.bar("vector")
                    if half == 1 and t == 0:
                        P.wait("vector", s_endA1)
                    for j in (1, 2, 3):
                        P.op("vector", lambda v, j=j, t=t: v.tensor_tensor(
                            F_sb[:, t, :, 256 * (j - 1):256 * j], xe2[:, :, 3:259],
                            xe2[:, :, 3 - j:259 - j], ALU.mult), s_F if j == 3 else None)
                    if debug and gt == 0:
                        P.wait("sync", s_F, 1)
                        P.dma("sync", dbg["F0"], F_sb[:, 0, :, :], s_out)
                    P.wait("scalar", s_F, t + 1)
                    if debug and gt == 1:
                        P.wait("scalar", s_out, s_out.count)
                    if t > 0:
                        P.wait("scalar", s_chain, 2 * t)
                    for j in range(2):
                        P.op("scalar", lambda s, j=j, t=t: s.activation(
                            sq_scr[:, j, :], F_sb[:, t, j, :], AF.Square, accum_out=sstat[:, j:j + 1]))
                    P.bar("scalar")
                    P.op("scalar", lambda s: s.activation(sstat[:, 2:4], sstat[:, 0:2], AF.Sqrt), s_ssq)
                    P.wait("vector", s_ssq, 2 * t + 1)
                    P.op("vector", lambda v: v.tensor_scalar(
                        sstat[:, 22:24], sstat[:, 2:4], EPS, None, ALU.mult))
                    P.op("vector", lambda v: v.reciprocal(sstat[:, 4:6], sstat[:, 2:4]), s_ssq)
                    P.bar("vector")
                    P.wait("tensor", s_F, t + 1)
                    fswaits = [(s_num, 4 * t)] if t > 0 else []
                    transposes([F_sb.ap()[:, t, j, 128 * c:128 * (c + 1)]
                                for j in range(2) for c in range(6)], fT, swaits=fswaits)
                    if half == 0 and t + 1 < NT:
                        a1_front(t + 1)
                    P.wait("tensor", s_nA)
                    if t == 0:
                        P.wait("tensor", s_mA)
                    if half == 0 and t == 0:
                        P.wait("tensor", s_ntr, 2)
                    for j in range(2):
                        nbA = ps_n.ap() if j == 0 else ps_a.ap()[:, 2048:2560]
                        if j == 1:
                            P.wait("tensor", s_ev1, 3 * (t + 1))
                        for c in range(6):
                            P.op("tensor", lambda t_, j=j, c=c, nbA=nbA: t_.matmul(
                                nbA[:, 0:512], fT[:, 6 * j + c, :], memA[:, j, c, :],
                                start=(c == 0), stop=(c == 5)), s_num if c == 5 else None)
                        if 16 * half + 2 * t + j > 0:
                            P.wait("tensor", s_denA, 16 * half + 2 * t + j)
                        for c in range(6):
                            P.op("tensor", lambda t_, j=j, c=c: t_.matmul(
                                ps_dot[:, 4 * j:4 * j + 4], fT[:, 6 * j + c, :], normA[:, j, c, :],
                                start=(c == 0), stop=(c == 5)), s_num if c == 5 else None)
                        P.wait("vector", s_num, 4 * t + 2 * (j + 1))
                        P.wait("vector", s_ssq, 2 * (t + 1))
                        invn = sstat.ap()[:, 4 + j:5 + j]
                        epsn = sstat.ap()[:, 22 + j:23 + j]
                        den = sstat.ap()[:, 6:10]
                        P.op("vector", lambda v, j=j, epsn=epsn, den=den: v.tensor_scalar(
                            den, ps_dot[:, 4 * j:4 * j + 4], 1.0, epsn, ALU.mult, ALU.add), s_denA)
                        P.bar("vector")
                        P.op("vector", lambda v, den=den: v.reciprocal(sstat[:, 10:14], den))
                        P.bar("vector")
                        P.op("vector", lambda v: v.tensor_scalar(
                            sstat[:, 14:18], sstat[:, 10:14], -1.0, None, ALU.mult))
                        P.bar("vector")
                        def stt_mv(hl, j=j, t=t, nbA=nbA):
                            P.op("vector", lambda v, hl=hl, j=j, nbA=nbA: v.scalar_tensor_tensor(
                                mv_scr[:, hl, :], nbA[:, 128 * hl:128 * (hl + 1)],
                                sstat[:, 14 + hl:15 + hl],
                                kv_bf[:, 256 + 128 * j:384 + 128 * j], ALU.mult, ALU.add))
                        def stt_wmv(hl, j=j, t=t, invn=invn):
                            h8 = 4 * j + hl
                            P.op("vector", lambda v, hl=hl, h8=h8, t=t, invn=invn:
                                 v.scalar_tensor_tensor(
                                wmv[:, t, h8, :], mv_scr[:, hl, :], invn,
                                mb_bf[:, 128 * h8:128 * (h8 + 1)], ALU.mult, ALU.mult))
                        stt_mv(0)
                        stt_mv(1)
                        stt_mv(2)
                        stt_mv(3)
                        P.bar("vector")
                        P.op("vector", lambda v: v.nop(), s_mvd)
                        P.op("vector", lambda v, den=den, invn=invn: v.tensor_scalar(
                            sstat[:, 18:22], den, invn, None, ALU.mult))
                        P.bar("vector")
                        stt_wmv(0)
                        stt_wmv(1)
                        stt_wmv(2)
                        stt_wmv(3)
                        P.op("vector", lambda v: v.tensor_scalar(
                            sstat[:, 14:18], sstat[:, 18:22], -1.0, 1.0, ALU.mult, ALU.add))
                        P.bar("vector")
                        P.op("vector", lambda v: v.tensor_scalar(
                            sstat[:, 18:22], sstat[:, 14:18], 0.0, 1.0, ALU.max, ALU.min))
                        P.bar("vector")
                        P.op("vector", lambda v, t=t, j=j, invn=invn: v.tensor_scalar(
                            coefn[:, t, 4 * j:4 * j + 4], sstat[:, 18:22], invn,
                            None, ALU.mult), s_chain)
                    if debug and gt == 0:
                        P.wait("sync", s_chain, 2)
                        P.dma("sync", dbg["nst0"], sstat[:, 0:24], s_out)
                        P.dma("sync", dbg["wmv0"], wmv[:, 0, :, :], s_out)
                        P.dma("sync", dbg["coefn0"], coefn[:, 0, :], s_out)
                    if half == 1 and 2 <= t <= 5:
                        for hh in (2 * (t - 2), 2 * (t - 2) + 1):
                            build_head_v(hh)

                P.wait("tensor", s_chain)
                P.wait("tensor", s_ev1)
                s_dmm = P.sem(f"dmm{half}")
                s_dev = P.sem(f"dev{half}")
                s_dwr = [P.sem(f"dwr{half}_0"), P.sem(f"dwr{half}_1")]
                half_sems[half]["dev"] = s_dev
                half_sems[half]["dwr"] = s_dwr
                for c in range(6):
                    reg = (c % 2) * 1024
                    if c >= 1:
                        P.wait("tensor", s_dev, c)
                    for tt in range(NT):
                        for j in range(2):
                            P.op("tensor", lambda t_, c=c, tt=tt, j=j, reg=reg: t_.matmul(
                                ps_a[:, reg + 512 * j:reg + 512 * (j + 1)],
                                F_sb[:, tt, j, 128 * c:128 * (c + 1)],
                                wmv[:, tt, 4 * j:4 * j + 4, :],
                                start=(tt == 0), stop=(tt == NT - 1)), None)
                            dps = ps_dot[:, 0:4] if j == 0 else ps_n.ap()[:, 0:4]
                            P.op("tensor", lambda t_, c=c, tt=tt, j=j, dps=dps: t_.matmul(
                                dps, F_sb[:, tt, j, 128 * c:128 * (c + 1)],
                                coefn[:, tt, 4 * j:4 * j + 4],
                                start=(tt == 0), stop=(tt == NT - 1)),
                                s_dmm if (tt == NT - 1 and j == 1) else None)
                    P.wait("scalar", s_dmm, c + 1)
                    if c >= 1:
                        P.wait("scalar", s_dwr[0], 16 * c)
                    P.op("scalar", lambda s, c=c, reg=reg: s.copy(
                        dstage[:, 0, 0:1024], ps_a[:, reg:reg + 1024]))
                    P.op("scalar", lambda s, c=c: s.copy(dstage[:, 0, 1024:1028], ps_dot[:, 0:4]))
                    P.op("scalar", lambda s, c=c: s.copy(dstage[:, 0, 1028:1032], ps_n.ap()[:, 0:4]), s_dev)
                    P.wait("sync", s_dev, c + 1)
                    P.dma("sync", d_loc[half][c], dstage[:, 0, :], s_dwr[0])
                if half == 1:
                    P.wait("gpsimd", s_proj, 8)
                    s_wg.bump(16)
                    P.emit("gpsimd", lambda e: e.dma_start(
                        out=wg_pre[0][:], in_=wgP).then_inc(s_wg.h, 16))
                    s_wg.bump(16)
                    P.emit("gpsimd", lambda e: e.dma_start(
                        out=wg_pre[1][:], in_=bg).then_inc(s_wg.h, 16))
                P.wait("gpsimd", s_dwr[0])
                P.wait("gpsimd", s_dwr[1])
                P.op("gpsimd", lambda g, half=half: g.collective_compute(
                    "AllReduce", ALU.add, replica_groups=[list(range(NCORE))],
                    ins=[d_loc[half]], outs=[d_red[half]]), s_cc[half])
                P.op("tensor", lambda t: t.nop(), s_endA1 if half == 0 else s_endA2)
                if half == 1:
                    P.wait("sync", s_q[0])
                    P.op("sync", lambda e: e.nop(), s_endA2)

        esB = contextlib.ExitStack()
        with esB:
            def sbB(name, shape, dt):
                return esB.enter_context(nc.sbuf_tensor(name, shape, dt))

            wg_sb = sbB("wg_sb", [128, NCH, HID], BF16)
            bias_g = sbB("bias_g", [1, HID], BF16)
            wg_pre[0] = wg_sb
            wg_pre[1] = bias_g
            memN_b = sbB("memN_b", [128, 2, 6, 512], BF16)
            memNs[1] = memN_b
            q_in = sbB("q_in", [128, 2, HID], BF16)
            xe4 = sbB("xe4", [128, 2, 4, 262], BF16)
            Fq = sbB("Fq", [128, 2, 4, FD], BF16)
            fTq = sbB("fTq", [128, 2, 24, 128], BF16)
            gate_sb = sbB("gate_sb", [128, HID], F32)
            hs_sb = sbB("hs_sb", [128, HID], F32)
            mhg = sbB("mhg", [128, HID], F32)
            ln2_sb = mhg
            y_sb = sbB("y_sb", [128, HID], F32)

            s_hs2 = P.sem("hs2")
            s_den = P.sem("den")
            s_gproj = P.sem("gproj")
            s_gate = P.sem("gate")
            s_qrd = [P.sem("qrd0"), P.sem("qrd1")]
            s_relu2 = P.sem("relu2")
            s_Fq = P.sem("Fq")
            s_ssq2 = P.sem("ssq2")
            s_numa = P.sem("numa")
            s_mhg = P.sem("mhg")
            s_attn = P.sem("attn")
            s_ln2 = P.sem("ln2")
            s_yv = P.sem("yv")
            s_ywr = [P.sem("ywr0"), P.sem("ywr1")]

            for hh in range(8, 16):
                build_head_dma(hh)

            def b1_loads(u):
                us = u % 2
                if u < 2:
                    P.wait("sync", s_hs, 8)
                else:
                    P.wait("sync", s_yv, u - 1)
                P.dma("sync", x_sb[:, us, :], xt[u], s_x[us])
                if u >= 2:
                    P.wait("sync", s_relu2, 4 * u - 4)
                if u == 0:
                    P.wait("sync", s_endA2)
                P.dma("sync", q_in[:, us, :], q_hbm[u], s_qrd[us])
                if u >= 2:
                    P.wait("sync", s_gproj, u - 1)
                P.dma("sync", hsT[:, us], ht_hbm[u], s_ht[us])

            b1_loads(0)
            fq_te = {}
            for t in range(NT):
                slot = t % 2
                if t + 1 < NT:
                    b1_loads(t + 1)
                if debug and t in (1, 2):
                    P.wait("scalar", s_out, s_out.count)
                    P.wait("vector", s_out, s_out.count)
                if t == 0:
                    P.wait("scalar", s_endA2)
                P.wait("tensor", s_wg)
                P.wait("tensor", s_ht[slot], 16 * (4 + t // 2 + 1))
                if t == 0:
                    P.wait("tensor", half_sems[1]["dev"], 6)
                    P.wait("tensor", s_endA2)
                if t > 0:
                    P.wait("tensor", s_gate, t)
                for c in range(NCH + 1):
                    last = c == NCH
                    lhs = ones_r[:] if last else hsT[:, slot, c, :]
                    for wi in range(4):
                        rhs = (bias_g[:, 512 * wi:512 * (wi + 1)] if last
                               else wg_sb[:, c, 512 * wi:512 * (wi + 1)])
                        P.op("tensor", lambda t_, lhs=lhs, rhs=rhs, wi=wi, c=c, last=last:
                             t_.matmul(ps_a[:, 512 * wi:512 * (wi + 1)], lhs, rhs,
                                       start=(c == 0), stop=last),
                             s_gproj if (last and wi == 3) else None)
                def emit_gate(t=t):
                    P.wait("scalar", s_gproj, t + 1)
                    if t > 0:
                        P.wait("scalar", s_mhg, 4 * t)
                        P.wait("scalar", s_ln2, t)
                    P.op("scalar", lambda s: s.activation(
                        gate_sb[:], ps_a[:, 0:2048], AF.Sigmoid), s_gate)
                    if debug and t == 0:
                        P.wait("sync", s_gate, 1)
                        P.dma("sync", dbg["gate0"], gate_sb[:], s_out)
                for g in range(4):
                    gi = 4 * t + g
                    par = g % 2
                    if t == 0 and g == 2:
                        for hh in range(8, 12):
                            build_head_v(hh)
                    if t == 0 and g == 3:
                        for hh in range(12, 16):
                            build_head_v(hh)
                    P.wait("scalar", s_qrd[slot], 16 * (t // 2 + 1))
                    if gi > 1:
                        P.wait("scalar", s_Fq, gi - 1)
                    qv = q_in.ap()[:, slot, :].rearrange("p (g4 d) -> p g4 d", d=128)
                    P.op("scalar", lambda s, g=g, par=par, qv=qv: s.activation(
                        xe4[:, par, :, 3:131], qv[:, 4 * g:4 * g + 4, :], AF.Relu))
                    P.op("scalar", lambda s, g=g, par=par, qv=qv: s.activation(
                        xe4[:, par, :, 131:259], qv[:, 4 * g:4 * g + 4, :], AF.Relu, scale=-1.0), s_relu2)
                    P.wait("vector", s_relu2, gi + 1)
                    P.op("vector", lambda v, par=par: v.tensor_copy(
                        xe4[:, par, :, 0:3], xe4[:, par, :, 256:259]))
                    P.bar("vector")
                    if gi > 1:
                        P.wait("vector", s_te, fq_te.get(par, 0))
                        P.wait("vector", s_ssq2, gi - 1)
                    for j in (1, 2, 3):
                        P.op("vector", lambda v, j=j, par=par: v.tensor_tensor(
                            Fq[:, par, :, 256 * (j - 1):256 * j], xe4[:, par, :, 3:259],
                            xe4[:, par, :, 3 - j:259 - j], ALU.mult), s_Fq if j == 3 else None)
                    P.wait("scalar", s_Fq, gi + 1)
                    if gi > 1:
                        P.wait("scalar", s_mhg, gi - 1)
                    sreg = 8 * par
                    for hl in range(4):
                        P.op("scalar", lambda s, hl=hl, sreg=sreg, par=par: s.activation(
                            sq_scr[:, hl % 2, :], Fq[:, par, hl, :], AF.Square,
                            accum_out=sstat[:, sreg + hl:sreg + hl + 1]))
                        if hl == 1:
                            P.bar("scalar")
                    P.bar("scalar")
                    P.op("scalar", lambda s, sreg=sreg: s.activation(
                        sstat[:, sreg + 4:sreg + 8], sstat[:, sreg:sreg + 4], AF.Sqrt), s_ssq2)
                    if g == 0:
                        emit_gate()
                    P.wait("tensor", s_Fq, gi + 1)
                    fqswaits = [(s_numa, gi - 1)] if gi > 1 else []
                    transposes([Fq.ap()[:, par, hl, 128 * c:128 * (c + 1)]
                                for hl in range(4) for c in range(6)],
                               fTq.ap()[:, par], swaits=fqswaits)
                    fq_te[par] = s_te.count
                    if gi == 0:
                        P.wait("tensor", half_sems[1]["dev"], 6)
                    if gi > 1:
                        P.wait("tensor", s_mhg, gi - 1)
                    P.wait("tensor", s_memN, 4 * (g + 1))
                    memg = memNs[g // 2]
                    nbank = (ps_n.ap() if par == 0 else ps_a.ap()[:, 2048:2560])
                    for hl in range(4):
                        for c in range(6):
                            P.op("tensor", lambda t_, g=g, c=c, hl=hl, memg=memg, nb=nbank, par=par: t_.matmul(
                                nb[:, 128 * hl:128 * (hl + 1)], fTq[:, par, 6 * hl + c, :],
                                memg[:, g % 2, c, 128 * hl:128 * (hl + 1)],
                                start=(c == 0), stop=(c == 5)), None)
                    if gi > 0:
                        P.wait("tensor", s_den, gi)
                    for hl in range(4):
                        for c in range(6):
                            P.op("tensor", lambda t_, g=g, c=c, hl=hl, par=par: t_.matmul(
                                ps_dot[:, 4 * par + hl:4 * par + hl + 1], fTq[:, par, 6 * hl + c, :],
                                normN[:, g, c, hl:hl + 1],
                                start=(c == 0), stop=(c == 5)),
                                s_numa if (hl == 3 and c == 5) else None)
                    P.wait("vector", s_numa, gi + 1)
                    P.wait("vector", s_ssq2, gi + 1)
                    n4 = sstat.ap()[:, sreg + 4:sreg + 8]
                    den = sstat.ap()[:, 16:20]
                    P.op("vector", lambda v, n4=n4, den=den, par=par: v.scalar_tensor_tensor(
                        den, n4, EPS, ps_dot[:, 4 * par:4 * par + 4], ALU.mult, ALU.add), s_den)
                    P.bar("vector")
                    P.op("vector", lambda v, den=den: v.reciprocal(sstat[:, 20:24], den))
                    P.bar("vector")
                    P.wait("vector", s_gate, t + 1)
                    for hl in range(4):
                        hcol = 128 * (4 * g + hl)
                        P.op("vector", lambda v, hl=hl, hcol=hcol, nb=nbank: v.scalar_tensor_tensor(
                            mhg[:, hcol:hcol + 128], nb[:, 128 * hl:128 * (hl + 1)],
                            sstat[:, 20 + hl:21 + hl], gate_sb[:, hcol:hcol + 128],
                            ALU.mult, ALU.mult), s_mhg if hl == 3 else None)
                P.wait("scalar", s_x[slot], 16 * (4 + t // 2 + 1))
                if t >= 1:
                    P.wait("scalar", s_attn, t)
                P.op("scalar", lambda s, slot=slot, t=t: s.activation(
                    hs_sb[:], x_sb[:, slot, :], AF.Identity,
                    bias=msb[:, 2 * t + 1:2 * t + 2], scale=msb[:, 2 * t:2 * t + 1]), s_hs2)
                if debug and t == 0:
                    P.wait("vector", s_out, s_out.count)
                if t > 0:
                    P.wait("vector", s_ln2, t)
                P.wait("vector", s_hs2, t + 1)
                P.bar("vector")
                P.op("vector", lambda v: v.tensor_tensor(gate_sb[:], mhg[:], hs_sb[:], ALU.add), s_attn)
                P.bar("vector")
                if debug and t == 0:
                    P.wait("sync", s_attn, 1)
                    P.dma("sync", dbg["mhg0"], gate_sb[:], s_out)
                ln_stats(gate_sb.ap(), lns[:, 5:6], lns[:, 6:7])
                P.wait("scalar", s_lnd)
                P.op("scalar", lambda s: s.activation(
                    ln2_sb[:], gate_sb[:], AF.Identity, bias=lns[:, 6:7], scale=lns[:, 5:6]), s_ln2)
                P.wait("vector", s_ln2, t + 1)
                P.wait("vector", s_ywr[0], 16 * t)
                P.op("vector", lambda v, slot=slot: v.tensor_tensor(
                    y_sb[:], ln2_sb[:], x_sb[:, slot, :], ALU.add), s_yv)
                P.wait("sync", s_yv, t + 1)
                P.dma("sync", y_hbm[t], y_sb[:], s_ywr[0])
                if debug and t == 0:
                    P.dma("sync", dbg["y0"], y_sb[:], s_out)
            if debug:
                P.wait("sync", s_memN, 16)
                P.dma("sync", dbg["memN"].rearrange("p g c x -> p (g c x)")
                      .rearrange("p (g2 r) -> p g2 r", g2=2)[:, 0, :], memN_a.ap()
                      .rearrange("p g c x -> p (g c x)"), s_out)
                P.dma("sync", dbg["memN"].rearrange("p g c x -> p (g c x)")
                      .rearrange("p (g2 r) -> p g2 r", g2=2)[:, 1, :], memN_b.ap()
                      .rearrange("p g c x -> p (g c x)"), s_out)
                P.dma("sync", dbg["normN"], normN[:], s_out)
            P.op("tensor", lambda t: t.nop(), s_endB1)
            P.op("vector", lambda v: v.nop(), s_endB1)
            P.op("scalar", lambda s: s.nop(), s_endB1)
            P.wait("sync", s_ywr[0])
            P.wait("sync", s_qrd[0])
            P.wait("sync", s_qrd[1])
            if debug:
                P.wait("sync", s_out, s_out.count)
            P.op("sync", lambda e: e.nop(), s_endB1)

        x_es.close()
        esC = contextlib.ExitStack()
        with esC:
            def sbC(name, shape, dt):
                return esC.enter_context(nc.sbuf_tensor(name, shape, dt))

            sT = sbC("sT", [128, NCH, HID], BF16)
            wf_st = sbC("wf_st", [128, 2, HID], F32)
            absw = sbC("absw", [128, HID], BF16)
            alpha_row = sbC("alpha_row", [1, HID], F32)
            inva = sbC("inva", [1, HID], F32)
            biasf_sb = sbC("biasf_sb", [1, HID], F32)
            bias_pre = sbC("bias_pre", [1, HID], BF16)
            alpha_bc = sbC("alpha_bc", [128, HID], F32)
            y_in = sbC("y_in", [128, 2, HID], F32)
            hs2b = sbC("hs2b", [128, HID], BF16)
            z_sb = sbC("z_sb", [128, HID], F32)
            ln4_sb = sbC("ln4_sb", [128, HID], F32)
            out_sb = sbC("out_sb", [128, HID], F32)

            s_wf = [P.sem("wf0"), P.sem("wf1")]
            s_bfl = P.sem("bfl")
            s_sign = P.sem("sign")
            s_amm = P.sem("amm")
            s_aev = P.sem("aev")
            s_bc = P.sem("bc")
            s_yrd = [P.sem("yrd0"), P.sem("yrd1")]
            s_hs3 = P.sem("hs3")
            s_fmm = P.sem("fmm")
            s_z = P.sem("z")
            s_ln4 = P.sem("ln4")
            s_outv = P.sem("outv")
            s_od = [P.sem("od0"), P.sem("od1")]

            P.wait("gpsimd", s_endB1)
            P.dma("gpsimd", biasf_sb[:], bfh, s_bfl)
            for c in range(NCH):
                slot = c % 2
                if c >= 2:
                    P.wait("gpsimd", s_sign, c - 1)
                P.dma("gpsimd", wf_st[:, slot, :],
                      wfT.rearrange("(c p) j -> p c j", p=128)[:, c, :], s_wf[slot])
                P.wait("scalar", s_wf[slot], 16 * (c // 2 + 1))
                if c == 0:
                    P.wait("scalar", s_endB1, 4)
                if c >= 1:
                    P.wait("scalar", s_amm, c)
                P.op("scalar", lambda s, c=c, slot=slot: s.activation(
                    sT[:, c, :], wf_st[:, slot, :], AF.Sign))
                P.op("scalar", lambda s, c=c, slot=slot: s.activation(
                    absw[:], wf_st[:, slot, :], AF.Abs), s_sign)
                P.wait("tensor", s_sign, c + 1)
                if c == 0:
                    P.wait("tensor", s_endB1, 4)
                    P.wait("tensor", s_z, 0)
                for wi in range(4):
                    P.op("tensor", lambda t_, wi=wi, c=c: t_.matmul(
                        ps_a.ap()[0:1, 512 * wi:512 * (wi + 1)], ones_c[:],
                        absw[:, 512 * wi:512 * (wi + 1)],
                        start=(c == 0), stop=(c == NCH - 1)),
                        s_amm if wi == 3 else None)
            P.wait("scalar", s_amm, NCH)
            P.op("scalar", lambda s: s.activation(
                alpha_row[:], ps_a.ap()[0:1, 0:2048], AF.Identity, scale=1.0 / HID), s_aev)
            P.wait("vector", s_aev)
            P.wait("vector", s_endB1, 4)
            P.op("vector", lambda v: v.reciprocal(inva[:], alpha_row[:]))
            P.wait("vector", s_bfl, 16)
            P.bar("vector")
            P.op("vector", lambda v: v.tensor_tensor(bias_pre[:], biasf_sb[:], inva[:], ALU.mult), s_aev)
            P.wait("tensor", s_aev, 2)
            for wi in range(4):
                P.op("tensor", lambda t_, wi=wi: t_.matmul(
                    ps_a[:, 512 * wi:512 * (wi + 1)], ones_rf[:],
                    alpha_row[:, 512 * wi:512 * (wi + 1)], start=True, stop=True),
                    s_bc if wi == 3 else None)
            P.wait("scalar", s_bc)
            P.op("scalar", lambda s: s.copy(alpha_bc[:], ps_a[:, 0:2048]), s_bc)
            if debug:
                P.wait("sync", s_aev, 2)
                P.dma("sync", dbg["alpha"], alpha_row[:], s_out)

            def b2_yload(t):
                slot = t % 2
                if t == 0:
                    P.wait("sync", s_endB1, 4)
                if t >= 2:
                    P.wait("sync", s_outv, t - 1)
                if t == 0:
                    P.wait("sync", s_ywr[0], 16 * 8)
                P.dma("sync", y_in[:, slot, :], y_hbm[t], s_yrd[slot])

            def b2_ln3(t):
                slot = t % 2
                if debug and t == 1:
                    P.wait("vector", s_out, s_out.count)
                P.wait("vector", s_yrd[slot], 16 * (t // 2 + 1))
                ln_stats(y_in.ap()[:, slot, :], lns[:, 5:6], lns[:, 6:7])
                P.wait("scalar", s_lnd)
                if t >= 1:
                    P.wait("scalar", s_te, s_te.count)
                P.op("scalar", lambda s, slot=slot: s.activation(
                    hs2b[:], y_in[:, slot, :], AF.Identity,
                    bias=lns[:, 6:7], scale=lns[:, 5:6]), s_hs3)

            def b2_ln4out(v):
                vslot = v % 2
                P.wait("vector", s_z, v + 1)
                P.bar("vector")
                ln_stats(z_sb.ap(), lns[:, 16:17], lns[:, 17:18])
                P.wait("scalar", s_lnd)
                P.op("scalar", lambda s: s.activation(
                    ln4_sb[:], z_sb[:], AF.Identity, bias=lns[:, 17:18], scale=lns[:, 16:17]), s_ln4)
                P.wait("vector", s_ln4, v + 1)
                if v >= 1:
                    P.wait("vector", s_od[0], 16 * v)
                P.op("vector", lambda v_, vslot=vslot: v_.tensor_tensor(
                    out_sb[:], ln4_sb[:], y_in[:, vslot, :], ALU.add), s_outv)
                P.wait("sync", s_outv, v + 1)
                P.dma("sync", outh.rearrange("(n p) d -> n p d", p=128)[v], out_sb[:], s_od[0])

            b2_yload(0)
            b2_ln3(0)
            for t in range(NT):
                slot = t % 2
                P.wait("tensor", s_hs3, t + 1)
                if t == 0:
                    P.wait("tensor", s_bc, 2)
                b2swaits = [(s_fmm, t - 1)] if t >= 2 else []
                transposes([hs2b.ap()[:, 128 * c:128 * (c + 1)] for c in range(NCH)],
                           hsT.ap()[:, slot], swaits=b2swaits)
                if t > 0:
                    P.wait("tensor", s_z, t)
                for c in range(NCH + 1):
                    last = c == NCH
                    lhs = ones_r[:] if last else hsT[:, slot, c, :]
                    for wi in range(4):
                        rhs = (bias_pre[:, 512 * wi:512 * (wi + 1)] if last
                               else sT[:, c, 512 * wi:512 * (wi + 1)])
                        P.op("tensor", lambda t_, lhs=lhs, rhs=rhs, wi=wi, c=c, last=last:
                             t_.matmul(ps_a[:, 512 * wi:512 * (wi + 1)], lhs, rhs,
                                       start=(c == 0), stop=last),
                             s_fmm if (last and wi == 3) else None)
                if t > 0:
                    b2_ln4out(t - 1)
                if t + 1 < NT:
                    b2_yload(t + 1)
                    b2_ln3(t + 1)
                P.wait("vector", s_fmm, t + 1)
                P.wait("vector", s_bc, 2)
                if debug and t == 1:
                    P.wait("vector", s_out, s_out.count)
                if t > 0:
                    P.wait("vector", s_ln4, t)
                P.op("vector", lambda v: v.tensor_tensor(z_sb[:], ps_a[:, 0:2048], alpha_bc[:], ALU.mult), s_z)
                if debug and t == 0:
                    P.wait("sync", s_z, 1)
                    P.dma("sync", dbg["z0"], z_sb[:], s_out)
            b2_ln4out(NT - 1)

            P.wait("gpsimd", s_od[0])
            if debug:
                P.wait("gpsimd", s_out)

        blk = es.enter_context(nc.Block())
        for s in P.sems:
            s.h = es.enter_context(nc.semaphore(s.name))

        @blk.tensor
        def _(e):
            for f in P.q["tensor"]:
                f(e)

        @blk.vector
        def _(e):
            for f in P.q["vector"]:
                f(e)

        @blk.scalar
        def _(e):
            for f in P.q["scalar"]:
                f(e)

        @blk.gpsimd
        def _(e):
            for f in P.q["gpsimd"]:
                f(e)

        @blk.sync
        def _(e):
            for f in P.q["sync"]:
                f(e)

    return nc, dbg


_NC_CACHE = {}


def _get_nc(debug=False):
    key = bool(debug)
    if key not in _NC_CACHE:
        _NC_CACHE[key] = build_nc(debug)
    return _NC_CACHE[key]


def make_in_maps(inputs):
    x = np.ascontiguousarray(np.asarray(inputs["L0_attn_hidden_states"], np.float32).reshape(8192, HID))
    kc = np.asarray(inputs["L0_attn_Wk"], np.float32).T.reshape(NCH, 128, 512)
    vc = np.asarray(inputs["L0_attn_Wv"], np.float32).T.reshape(NCH, 128, 512)
    mc = np.asarray(inputs["L0_attn_Wmb"], np.float32).T.reshape(NCH, 128, HID)
    qc = np.asarray(inputs["L0_attn_Wq"], np.float32).T.reshape(NCH, 128, HID)
    mem = np.asarray(inputs["L0_attn_memory"], np.float32)
    com = {
        "WfT": np.ascontiguousarray(np.asarray(inputs["L0_ffn_weight"], np.float32).T),
        "WgP": np.ascontiguousarray(
            np.asarray(inputs["L0_attn_Wg"], np.float32).T.reshape(NCH, 128, HID)
            .transpose(1, 0, 2)),
        "memP4": np.ascontiguousarray(
            mem.reshape(2, 2, 4, 6, 128, 128).transpose(0, 1, 4, 3, 2, 5)
            .reshape(4, 128, 6, 512)),
        "memPh": np.ascontiguousarray(mem.reshape(16, 6, 128, 128).transpose(0, 2, 1, 3)),
        "bk": np.asarray(inputs["L0_attn_bk"], np.float32).reshape(1, 512),
        "bv": np.asarray(inputs["L0_attn_bv"], np.float32).reshape(1, 512),
        "bm": np.asarray(inputs["L0_attn_bmb"], np.float32).reshape(1, HID),
        "bq": np.asarray(inputs["L0_attn_bq"], np.float32).reshape(1, HID),
        "bg": np.asarray(inputs["L0_attn_bg"], np.float32).reshape(1, HID),
        "bf": np.asarray(inputs["L0_ffn_bias"], np.float32).reshape(1, HID),
        "norm96": np.ascontiguousarray(np.asarray(inputs["L0_attn_norm"], np.float32).reshape(96, 128)),
        "ident": np.eye(128, dtype=np.float32),
    }
    for h in range(2):
        com[f"Wa{h}"] = np.ascontiguousarray(np.concatenate(
            [kc[:, :, 256 * h:256 * (h + 1)], vc[:, :, 256 * h:256 * (h + 1)],
             mc[:, :, 1024 * h:1024 * (h + 1)], qc[:, :, 1024 * h:1024 * (h + 1)]],
            axis=2).transpose(1, 0, 2))
    in_maps = []
    for c in range(NCORE):
        m = dict(com)
        m["x_part"] = np.ascontiguousarray(x[c * TOK:(c + 1) * TOK])
        in_maps.append(m)
    return in_maps


def kernel(**inputs):
    nc, _ = _get_nc(False)
    in_maps = make_in_maps(inputs)
    res = run_bass_kernel_spmd(nc, in_maps, core_ids=list(range(NCORE)))
    out = np.concatenate([r["out_part"] for r in res.results], axis=0)
    return out.reshape(2, 4096, HID)


# version 08
# speedup vs baseline: 1.3216x; 1.0249x over previous
import contextlib
import numpy as np
import concourse.bass as bass
import concourse.mybir as mybir
from concourse.bass_utils import run_bass_kernel_spmd

F32 = mybir.dt.float32
BF16 = mybir.dt.bfloat16
AF = mybir.ActivationFunctionType
ALU = mybir.AluOpType
AX = mybir.AxisListType

NCORE = 8
TOK = 1024
NT = 8
HID = 2048
NCH = 16
D = 128
H = 16
FD = 768
EPS = 1e-8
LNEPS = 1e-5


class Sem:
    __slots__ = ("name", "count", "h")

    def __init__(self, name):
        self.name = name
        self.count = 0
        self.h = None

    def bump(self, n=1):
        self.count += n
        return self.count


class Prog:
    ENGINES = ("tensor", "vector", "scalar", "gpsimd", "sync")

    def __init__(self):
        self.q = {e: [] for e in self.ENGINES}
        self.sems = []
        self.barsems = {}

    def sem(self, name):
        s = Sem(name)
        self.sems.append(s)
        return s

    def emit(self, eng, fn):
        self.q[eng].append(fn)

    def wait(self, eng, sem, val=None):
        v = sem.count if val is None else val
        if v <= 0:
            return
        self.emit(eng, lambda e, s=sem, vv=v: e.wait_ge(s.h, vv))

    def op(self, eng, builder, sem=None, inc=1):
        if sem is not None:
            sem.bump(inc)
        extra = []

        def f(e, b=builder, s=sem, n=inc, x=extra):
            inst = b(e)
            if s is not None:
                inst.then_inc(s.h, n)
            for (xs, xn) in x:
                inst.then_inc(xs.h, xn)

        self.emit(eng, f)
        self.lastop = getattr(self, 'lastop', {})
        self.lastop[eng] = (extra, sem, sem.count if sem is not None else 0)

    def nop(self, eng):
        self.emit(eng, lambda e: e.nop())

    def drain(self, eng):
        self.emit(eng, lambda e: e.drain())

    def bar(self, eng):
        extra, psem, pval = self.lastop[eng]
        if psem is not None:
            self.emit(eng, lambda e, ss=psem, v=pval: e.wait_ge(ss.h, v))
            return
        s = self.barsems.get(eng)
        if s is None:
            s = self.sem(f"bar_{eng}")
            self.barsems[eng] = s
        c = s.bump(1)
        extra.append((s, 1))
        self.emit(eng, lambda e, ss=s, v=c: e.wait_ge(ss.h, v))

    def dma(self, qeng, out_ap, in_ap, sem, slow=False):
        sem.bump(16)
        self.emit(qeng, lambda e, o=out_ap, i=in_ap, s=sem, sl=slow:
                  e.dma_start(out=o, in_=i, allow_slow_non_contiguous=sl).then_inc(s.h, 16))


def build_nc(debug=False):
    nc = bass.Bass("TRN2", target_bir_lowering=False)
    P = Prog()

    xh = nc.dram_tensor("x_part", [TOK, HID], F32, kind="ExternalInput").ap()
    waH = [nc.dram_tensor(f"Wa{h}", [128, NCH, 2560], F32, kind="ExternalInput").ap()
           for h in range(2)]
    wgP = nc.dram_tensor("WgP", [128, NCH, HID], F32, kind="ExternalInput").ap()
    memP4h = nc.dram_tensor("memP4", [4, 128, 6, 512], F32, kind="ExternalInput").ap()
    memPh = nc.dram_tensor("memPh", [16, 128, 6, 128], F32, kind="ExternalInput").ap()
    wfT = nc.dram_tensor("WfT", [HID, HID], F32, kind="ExternalInput").ap()
    bk = nc.dram_tensor("bk", [1, 512], F32, kind="ExternalInput").ap()
    bv = nc.dram_tensor("bv", [1, 512], F32, kind="ExternalInput").ap()
    bm = nc.dram_tensor("bm", [1, HID], F32, kind="ExternalInput").ap()
    bq = nc.dram_tensor("bq", [1, HID], F32, kind="ExternalInput").ap()
    bg = nc.dram_tensor("bg", [1, HID], F32, kind="ExternalInput").ap()
    bfh = nc.dram_tensor("bf", [1, HID], F32, kind="ExternalInput").ap()
    normh = nc.dram_tensor("norm96", [96, 128], F32, kind="ExternalInput").ap()
    identh = nc.dram_tensor("ident", [128, 128], F32, kind="ExternalInput").ap()
    outh = nc.dram_tensor("out_part", [TOK, HID], F32, kind="ExternalOutput").ap()

    q_hbm = nc.dram_tensor("q_scr", [NT, 128, HID], BF16).ap()
    ht_hbm = nc.dram_tensor("ht_scr", [NT, 128, NCH * 128], BF16).ap()
    y_hbm = nc.dram_tensor("y_scr", [NT, 128, HID], F32).ap()
    d_loc = [nc.dram_tensor(f"d_loc{i}", [6, 128, 1032], BF16).ap() for i in range(2)]
    d_red = [nc.dram_tensor(f"d_red{i}", [6, 128, 1032], BF16, addr_space="Shared").ap() for i in range(2)]

    dbg = {}
    if debug:
        def dbgt(name, shape, dt=F32):
            dbg[name] = nc.dram_tensor("dbg_" + name, shape, dt, kind="ExternalOutput").ap()
        dbgt("hs0", [128, HID], BF16)
        dbgt("msb", [128, 16])
        dbgt("lns0", [128, 28])
        dbgt("bn0", [128, 24])
        dbgt("x0", [128, HID])
        dbgt("lsnap", [128, 28])
        dbgt("kv0", [128, 512], BF16)
        dbgt("mb0", [128, 1024], BF16)
        dbgt("q0", [128, 1024], BF16)
        dbgt("F0", [128, 2, FD], BF16)
        dbgt("nst0", [128, 24])
        dbgt("wmv0", [128, 8, 128], BF16)
        dbgt("coefn0", [128, 8], BF16)
        dbgt("memN", [128, 4, 6, 512], BF16)
        dbgt("normN", [128, 4, 6, 4], BF16)
        dbgt("mhg0", [128, HID])
        dbgt("gate0", [128, HID])
        dbgt("y0", [128, HID])
        dbgt("alpha", [1, HID])
        dbgt("z0", [128, HID])

    xt = xh.rearrange("(n p) d -> n p d", p=128)

    es = contextlib.ExitStack()
    with es:
        def sb(name, shape, dt):
            return es.enter_context(nc.sbuf_tensor(name, shape, dt))

        def pst(name, shape, dt):
            return es.enter_context(nc.psum_tensor(name, shape, dt))

        ps_a = pst("ps_a", [128, 2560], F32)
        ps_n = pst("ps_n", [128, 512], F32)
        ps_dot = pst("ps_dot", [128, 8], F32)
        ps_tr = pst("ps_tr", [128, 8, 128], BF16)

        ident_bf = sb("ident_bf", [128, 128], BF16)
        ident_f = sb("ident_f", [128, 128], F32)
        ones_r = sb("ones_r", [1, 128], BF16)
        ones_rf = sb("ones_rf", [1, 128], F32)
        ones_c = sb("ones_c", [128, 1], BF16)
        normF = sb("normF", [128, 96], F32)
        msb = sb("msb", [128, 16], F32)
        bn24 = sb("bn24", [128, 24], F32)
        lns = sb("lns", [128, 28], F32)
        sq_scr = sb("sq_scr", [128, 2, FD], F32)
        sstat = sb("sstat", [128, 24], F32)
        mv_scr = sb("mv_scr", [128, 4, 128], F32)
        lsnap = sb("lsnap", [128, 28], F32)
        hsb_sb = sb("hsb_sb", [128, HID], BF16)
        hsT = sb("hsT", [128, 2, NCH, 128], BF16)
        memN_a = sb("memN_a", [128, 2, 6, 512], BF16)
        memNs = [memN_a, None]
        normN = sb("normN", [128, 4, 6, 4], BF16)
        mstage = sb("mstage", [128, 2, 6, 128], BF16)
        dstage2 = sb("dstage2", [128, 2, 6, 129], BF16)
        norm96 = sb("norm96s", [96, 128], F32)
        x_es = contextlib.ExitStack()
        x_sb = x_es.enter_context(nc.sbuf_tensor("x_sb", [128, 2, HID], F32))

        s_const = P.sem("const")
        s_konst = P.sem("konst")
        s_ntr = P.sem("ntr")
        s_endA1 = P.sem("endA1")
        s_endA2 = P.sem("endA2")
        s_endB1 = P.sem("endB1")
        s_cc = [P.sem("cc0"), P.sem("cc1")]
        s_out = P.sem("out")
        s_x = [P.sem("x_s0"), P.sem("x_s1")]
        s_ht = [P.sem("ht_s0"), P.sem("ht_s1")]
        s_htw = [P.sem("htw_s0"), P.sem("htw_s1")]
        s_hs = P.sem("hs")
        s_tr = P.sem("tr")
        s_te = P.sem("te")
        s_lnv = P.sem("lnv")
        s_lsn = P.sem("lsn")
        s_wg = P.sem("wg")
        s_mst = P.sem("mst")
        s_mstB = P.sem("mstB")
        s_mst2 = [s_mst, s_mstB]
        s_memN = P.sem("memN")
        s_lnq = P.sem("lnq")
        s_lnd = P.sem("lnd")

        P.dma("gpsimd", ident_bf[:], identh, s_const)
        P.dma("gpsimd", ident_f[:], identh, s_const)
        P.op("vector", lambda v: v.memset(ones_r[:], 1.0), s_konst)
        P.op("vector", lambda v: v.memset(ones_rf[:], 1.0), s_konst)
        P.op("vector", lambda v: v.memset(ones_c[:], 1.0), s_konst)
        P.op("vector", lambda v: v.memset(lns[:], 0.0), s_konst)
        P.op("vector", lambda v: v.memset(bn24[:], 0.0), s_konst)
        P.op("vector", lambda v: v.memset(sstat[:], 0.0), s_konst)
        P.op("vector", lambda v: v.memset(msb[:], 0.0), s_konst)

        def ln_stats(src_ap, rstd_col, negms_col):
            for i in range(4):
                P.op("vector", lambda v, i=i, a=src_ap: v.bn_stats(
                    bn24[:, 6 * i:6 * i + 6], a[:, 512 * i:512 * (i + 1)]))
            m8 = bn24.ap()[:, 1:24:3]
            M28 = bn24.ap()[:, 2:24:3]
            P.bar("vector")
            P.op("vector", lambda v, m8=m8: v.tensor_tensor(lns[:, 8:16], m8, m8, ALU.mult))
            P.op("vector", lambda v, m8=m8: v.reduce_sum(lns[:, 1:2], m8, axis=AX.X))
            P.bar("vector")
            P.op("vector", lambda v, M28=M28: v.scalar_tensor_tensor(
                lns[:, 20:28], lns[:, 8:16], 256.0, M28, ALU.mult, ALU.add))
            P.op("vector", lambda v: v.tensor_scalar(
                lns[:, 2:3], lns[:, 1:2], 0.125, None, ALU.mult))
            P.bar("vector")
            P.op("vector", lambda v: v.reduce_sum(lns[:, 0:1], lns[:, 20:28], axis=AX.X))
            P.op("vector", lambda v: v.tensor_tensor(
                lns[:, 3:4], lns[:, 2:3], lns[:, 2:3], ALU.mult))
            P.bar("vector")
            P.op("vector", lambda v: v.tensor_scalar(
                lns[:, 4:5], lns[:, 0:1], 1.0 / HID, LNEPS, ALU.mult, ALU.add))
            P.bar("vector")
            P.op("vector", lambda v: v.tensor_tensor(
                lns[:, 18:19], lns[:, 4:5], lns[:, 3:4], ALU.subtract))
            P.bar("vector")
            P.op("vector", lambda v: v.reciprocal(lns[:, 19:20], lns[:, 18:19]), s_lnv)
            P.wait("scalar", s_lnv)
            P.op("scalar", lambda s, rc=rstd_col: s.activation(rc, lns[:, 19:20], AF.Sqrt), s_lnq)
            P.wait("vector", s_lnq)
            P.op("vector", lambda v, rc=rstd_col, nm=negms_col: v.scalar_tensor_tensor(
                nm, lns[:, 2:3], -1.0, rc, ALU.mult, ALU.mult), s_lnd)

        def transposes(srcs, dst, swaits=()):
            n = len(srcs)
            r0 = 0
            while r0 < n:
                rn = min(8, n - r0)
                P.wait("tensor", s_te)
                for i in range(rn):
                    P.op("tensor", lambda t, i=i, a=srcs[r0 + i]: t.transpose(
                        ps_tr[:, i, :], a, ident_bf[:]), s_tr if i == rn - 1 else None)
                P.wait("scalar", s_tr)
                for sw in swaits:
                    P.wait("scalar", sw[0], sw[1])
                swaits = ()
                P.op("scalar", lambda s, r0=r0, rn=rn, d=dst: s.copy(
                    d[:, r0:r0 + rn, :], ps_tr[:, 0:rn, :]), s_te)
                r0 += rn
            P.wait("tensor", s_te)
            return s_te.count

        def build_head_dma(hh):
            P.wait("gpsimd", s_cc[hh // 8], 1)
            if hh >= 2:
                P.wait("gpsimd", s_memN, hh - 1)
            hl = hh % 8
            sl = hh % 2
            sm = s_mst2[sl]
            P.dma("gpsimd", mstage[:, sl, :, :], memPh[hh], sm)
            P.dma("gpsimd", dstage2[:, sl, :, 0:128],
                  d_red[hh // 8][:, :, 128 * hl:128 * (hl + 1)].rearrange("c p d -> p c d"),
                  sm)
            P.dma("gpsimd", dstage2[:, sl, :, 128:129],
                  d_red[hh // 8][:, :, 1024 + hl:1025 + hl].rearrange("c p d -> p c d"),
                  sm, slow=True)

        def build_head_v(hh):
            sl = hh % 2
            P.wait("vector", s_mst2[sl], 48 * (hh // 2 + 1))
            P.op("vector", lambda v, hh=hh, sl=sl: v.tensor_tensor(
                memNs[hh // 8][:, (hh // 4) % 2, :, 128 * (hh % 4):128 * (hh % 4 + 1)],
                mstage[:, sl, :, :], dstage2[:, sl, :, 0:128], ALU.add))
            P.op("vector", lambda v, hh=hh, sl=sl: v.tensor_tensor(
                normN[:, hh // 4, :, (hh % 4):(hh % 4) + 1],
                normF.ap()[:, 6 * hh:6 * (hh + 1)].rearrange("p (c o) -> p c o", o=1),
                dstage2[:, sl, :, 128:129], ALU.add), s_memN)

        P.dma("gpsimd", norm96[:], normh, s_const)
        P.wait("tensor", s_const)
        P.wait("tensor", s_konst)
        P.op("tensor", lambda t: t.transpose(
            ps_n.ap()[:, 0:96], norm96[0:96, :], ident_f[0:96, 0:96]), s_ntr)
        P.wait("scalar", s_ntr)
        P.op("scalar", lambda s: s.copy(normF[:], ps_n.ap()[:, 0:96]), s_ntr)

        wg_pre = [None, None]
        esA = contextlib.ExitStack()
        with esA:
            def sbA(name, shape, dt):
                return esA.enter_context(nc.sbuf_tensor(name, shape, dt))

            wA = sbA("wA", [128, NCH, 2560], BF16)
            bias_kvmb = sbA("bias_kvmb", [1, 1536], BF16)
            bias_q = sbA("bias_q", [1, 1024], BF16)
            memA = sbA("memA", [128, 2, 6, 512], BF16)
            normA = sbA("normA", [128, 2, 6, 4], BF16)
            kv_bf = sbA("kv_bf", [128, 512], BF16)
            mb_bf = sbA("mb_bf", [128, 1024], BF16)
            q_bf = sbA("q_bf", [128, 1, 1024], BF16)
            xe2 = sbA("xe2", [128, 2, 262], BF16)
            F_sb = sbA("F_sb", [128, NT, 2, FD], BF16)
            fT = sbA("fT", [128, 12, 128], BF16)
            wmv = sbA("wmv", [128, NT, 8, 128], BF16)
            coefn = sbA("coefn", [128, NT, 8], BF16)
            dstage = sbA("dstage", [128, 1, 1032], BF16)

            s_q = [P.sem("q_s0")]
            half_sems = {}

            s_w4 = [P.sem(f"w4_{i}") for i in range(4)]
            s_pw = P.sem("pwA")
            s_mvd = P.sem("mvdA")
            s_denA = P.sem("denA")
            s_wB = P.sem("wBA")
            s_mA = P.sem("mAA")
            for half in range(2):
                s_wA = P.sem(f"wA{half}")
                half_sems[half] = {}
                s_nA = P.sem(f"nA{half}")
                s_proj = P.sem(f"proj{half}")
                s_ev1 = P.sem(f"ev1_{half}")
                s_F = P.sem(f"F{half}")
                s_relu = P.sem(f"relu{half}")
                s_ssq = P.sem(f"ssq{half}")
                s_num = P.sem(f"num{half}")
                s_chain = P.sem(f"chain{half}")
                s_stats = P.sem(f"stats{half}")
                half_sems[half]["proj"] = s_proj
                half_sems[half]["num"] = s_num
                half_sems[half]["chain"] = s_chain

                s_p7 = P.sem(f"p7_{half}")
                half_sems[half]["p7"] = s_p7
                if half == 0:
                    P.dma("gpsimd", bias_kvmb[:, 0:256], bk[:, 0:256], s_wB)
                    P.dma("gpsimd", bias_kvmb[:, 256:512], bv[:, 0:256], s_wB)
                    P.dma("gpsimd", bias_kvmb[:, 512:1536], bm[:, 0:1024], s_wB)
                    P.dma("gpsimd", bias_q[:], bq[:, 0:1024], s_wB)
                for c in range(NCH):
                    if half == 1:
                        P.wait("gpsimd", half_sems[0]["p7"], c + 1)
                    P.dma("gpsimd", wA[:, c, 0:1280], waH[half][:, c, 0:1280], s_w4[c // 4])
                    P.dma("gpsimd", wA[:, c, 1280:2560], waH[half][:, c, 1280:2560], s_w4[c // 4])
                if half == 1:
                    P.wait("gpsimd", half_sems[0]["proj"], 8)
                    P.dma("gpsimd", bias_kvmb[:, 0:256], bk[:, 256:512], s_wB)
                    P.dma("gpsimd", bias_kvmb[:, 256:512], bv[:, 256:512], s_wB)
                    P.dma("gpsimd", bias_kvmb[:, 512:1536], bm[:, 1024:2048], s_wB)
                    P.dma("gpsimd", bias_q[:], bq[:, 1024:2048], s_wB)
                for i in range(4):
                    P.wait("gpsimd", s_w4[i], 128 * (half + 1))
                P.op("gpsimd", lambda g: g.nop(), s_wA)
                if half == 1:
                    P.wait("gpsimd", half_sems[0]["num"], 32)
                for j in range(2):
                    for c6 in range(6):
                        P.dma("gpsimd", memA[:, j, c6, :], memP4h[2 * half + j][:, c6, :], s_mA)
                if half == 1:
                    for hh in range(0, 8):
                        build_head_dma(hh)
                P.wait("vector", s_ntr, 2)
                if half == 1:
                    P.wait("vector", half_sems[0]["num"], 32)
                P.op("vector", lambda v, half=half: v.tensor_copy(
                    normA[:],
                    normF.ap()[:, 48 * half:48 * (half + 1)]
                    .rearrange("p (j hl c) -> p j c hl", j=2, hl=4, c=6)), s_nA)

                def a1_front(u):
                    us = u % 2
                    if debug and u == 1:
                        P.wait("vector", s_out, s_out.count)
                        P.wait("scalar", s_out, s_out.count)
                    P.wait("vector", s_x[us], 16 * (u // 2 + 1))
                    ln_stats(x_sb.ap()[:, us, :], msb[:, 2 * u:2 * u + 1],
                             msb[:, 2 * u + 1:2 * u + 2])
                    if debug and u == 0:
                        P.bar("vector")
                        P.op("vector", lambda v: v.tensor_copy(lsnap[:], lns[:, 0:28]), s_lsn)
                    P.op("vector", lambda v: v.nop(), s_stats)
                    P.wait("scalar", s_x[us], 16 * (u // 2 + 1))
                    P.wait("scalar", s_lnd, s_lnd.count)
                    if u >= 1:
                        P.wait("scalar", s_te, s_te.count)
                    P.op("scalar", lambda s, us=us, u=u: s.activation(
                        hsb_sb[:], x_sb[:, us, :], AF.Identity,
                        bias=msb[:, 2 * u + 1:2 * u + 2], scale=msb[:, 2 * u:2 * u + 1]), s_hs)
                    if debug and u == 0:
                        P.wait("sync", s_hs, 1)
                        P.dma("sync", dbg["hs0"], hsb_sb[:], s_out)
                        P.dma("sync", dbg["msb"], msb[:, 0:16], s_out)
                        P.dma("sync", dbg["lns0"], lns[:, 0:28], s_out)
                        P.dma("sync", dbg["bn0"], bn24[:, 0:24], s_out)
                        P.dma("sync", dbg["x0"], x_sb[:, 0, :], s_out)
                        P.wait("sync", s_lsn, 1)
                        P.dma("sync", dbg["lsnap"], lsnap[:], s_out)

                if half == 0:
                    P.dma("sync", x_sb[:, 0, :], xt[0], s_x[0])
                    a1_front(0)
                else:
                    P.wait("sync", s_htw[0])
                    P.dma("sync", hsT[:, 0], ht_hbm[0], s_ht[0])
                    P.wait("sync", s_htw[1])
                    P.dma("sync", hsT[:, 1], ht_hbm[1], s_ht[1])
                for t in range(NT):
                    gt = 8 * half + t
                    slot = t % 2
                    if half == 0:
                        if t + 1 < NT:
                            if debug and t + 1 == 2:
                                P.wait("sync", s_out, s_out.count)
                            if t + 1 >= 2:
                                P.wait("sync", s_hs, t)
                            P.dma("sync", x_sb[:, (t + 1) % 2, :], xt[t + 1], s_x[(t + 1) % 2])
                        P.wait("tensor", s_hs, gt + 1)
                        htswaits = []
                        if t >= 2:
                            htswaits = [(s_proj, t - 1), (s_htw[slot], 16 * (t // 2))]
                        transposes([hsb_sb.ap()[:, 128 * c:128 * (c + 1)] for c in range(NCH)],
                                   hsT.ap()[:, slot], swaits=htswaits)
                        P.wait("sync", s_te, s_te.count)
                        P.dma("sync", ht_hbm[t], hsT[:, slot], s_htw[slot])
                    else:
                        if 2 <= t + 1 < NT:
                            P.wait("sync", s_proj, t)
                            P.dma("sync", hsT[:, (t + 1) % 2], ht_hbm[t + 1], s_ht[(t + 1) % 2])
                        if t == 0:
                            P.wait("tensor", half_sems[0]["dev"], 6)
                        P.wait("tensor", s_ht[slot], 16 * (t // 2 + 1))
                    if t > 0:
                        P.wait("tensor", s_wA, 1)
                        P.wait("tensor", s_ev1, 3 * t)
                    wins = [(0, 512, 0), (512, 1024, 0), (1024, 1536, 0),
                            (1536, 2048, 1), (2048, 2560, 1)]
                    for wi, (a, b, isq) in enumerate(wins):
                        if wi == 4 and 16 * half + 2 * t > 0:
                            P.wait("tensor", s_mvd, 16 * half + 2 * t)
                        for c in range(NCH + 1):
                            last = c == NCH
                            lhs = ones_r[:] if last else hsT[:, slot, c, :]
                            if t == 0 and wi == 0:
                                if last:
                                    P.wait("tensor", s_wB)
                                else:
                                    P.wait("tensor", s_w4[c // 4], 128 * (half + 1))
                            if last:
                                rhs = bias_q[:, a - 1536:b - 1536] if isq else bias_kvmb[:, a:b]
                            else:
                                rhs = wA[:, c, a:b]
                            psem = None
                            if last:
                                psem = s_proj if wi == 4 else s_pw
                            elif t == NT - 1 and wi == 4:
                                psem = s_p7
                            P.op("tensor", lambda t_, lhs=lhs, rhs=rhs, a=a, b=b, c=c, last=last:
                                 t_.matmul(ps_a[:, a:b], lhs, rhs, start=(c == 0), stop=last),
                                 psem)
                    P.wait("scalar", s_pw, 32 * half + 4 * t + 1)
                    if t > 0:
                        P.wait("scalar", s_chain, 2 * t)
                    if half == 1 and t == 0:
                        P.wait("scalar", s_endA1)
                        P.wait("scalar", half_sems[0]["dwr"][0], 96)
                    P.op("scalar", lambda s: s.copy(kv_bf[:], ps_a[:, 0:512]), s_ev1)
                    P.bar("scalar")
                    if t > 0:
                        P.wait("scalar", s_F, t)
                    kview = kv_bf.ap().rearrange("p (j d) -> p j d", j=4)
                    P.op("scalar", lambda s, kview=kview: s.activation(
                        xe2[:, :, 3:131], kview[:, 0:2, :], AF.Relu))
                    P.op("scalar", lambda s, kview=kview: s.activation(
                        xe2[:, :, 131:259], kview[:, 0:2, :], AF.Relu, scale=-1.0), s_relu)
                    P.wait("scalar", s_pw, 32 * half + 4 * t + 3)
                    P.op("scalar", lambda s: s.activation(mb_bf[:], ps_a[:, 512:1536], AF.Sigmoid), s_ev1)
                    P.wait("scalar", s_proj, t + 1)
                    P.wait("scalar", s_q[0], 16 * gt)
                    P.op("scalar", lambda s: s.copy(q_bf[:, 0, :], ps_a[:, 1536:2560]), s_ev1)
                    P.wait("sync", s_ev1, 3 * (t + 1))
                    P.dma("sync", q_hbm[t, :, 1024 * half:1024 * (half + 1)], q_bf[:, 0, :], s_q[0])
                    if debug and gt == 0:
                        P.dma("sync", dbg["kv0"], kv_bf[:], s_out)
                        P.dma("sync", dbg["mb0"], mb_bf[:], s_out)
                        P.dma("sync", dbg["q0"], q_bf[:, 0, :], s_out)
                    P.wait("vector", s_relu, t + 1)
                    P.op("vector", lambda v: v.tensor_copy(xe2[:, :, 0:3], xe2[:, :, 256:259]))
                    P.bar("vector")
                    if half == 1 and t == 0:
                        P.wait("vector", s_endA1)
                    for j in (1, 2, 3):
                        P.op("vector", lambda v, j=j, t=t: v.tensor_tensor(
                            F_sb[:, t, :, 256 * (j - 1):256 * j], xe2[:, :, 3:259],
                            xe2[:, :, 3 - j:259 - j], ALU.mult), s_F if j == 3 else None)
                    if debug and gt == 0:
                        P.wait("sync", s_F, 1)
                        P.dma("sync", dbg["F0"], F_sb[:, 0, :, :], s_out)
                    P.wait("scalar", s_F, t + 1)
                    if debug and gt == 1:
                        P.wait("scalar", s_out, s_out.count)
                    if t > 0:
                        P.wait("scalar", s_chain, 2 * t)
                    for j in range(2):
                        P.op("scalar", lambda s, j=j, t=t: s.activation(
                            sq_scr[:, j, :], F_sb[:, t, j, :], AF.Square, accum_out=sstat[:, j:j + 1]))
                    P.bar("scalar")
                    P.op("scalar", lambda s: s.activation(sstat[:, 2:4], sstat[:, 0:2], AF.Sqrt), s_ssq)
                    P.wait("vector", s_ssq, 2 * t + 1)
                    P.op("vector", lambda v: v.tensor_scalar(
                        sstat[:, 22:24], sstat[:, 2:4], EPS, None, ALU.mult))
                    P.op("vector", lambda v: v.reciprocal(sstat[:, 4:6], sstat[:, 2:4]), s_ssq)
                    P.bar("vector")
                    P.wait("tensor", s_F, t + 1)
                    fswaits = [(s_num, 4 * t)] if t > 0 else []
                    transposes([F_sb.ap()[:, t, j, 128 * c:128 * (c + 1)]
                                for j in range(2) for c in range(6)], fT, swaits=fswaits)
                    if half == 0 and t + 1 < NT:
                        a1_front(t + 1)
                    P.wait("tensor", s_nA)
                    if t == 0:
                        P.wait("tensor", s_mA)
                    if half == 0 and t == 0:
                        P.wait("tensor", s_ntr, 2)
                    for j in range(2):
                        nbA = ps_n.ap() if j == 0 else ps_a.ap()[:, 2048:2560]
                        if j == 1:
                            P.wait("tensor", s_ev1, 3 * (t + 1))
                        for c in range(6):
                            P.op("tensor", lambda t_, j=j, c=c, nbA=nbA: t_.matmul(
                                nbA[:, 0:512], fT[:, 6 * j + c, :], memA[:, j, c, :],
                                start=(c == 0), stop=(c == 5)), s_num if c == 5 else None)
                        if 16 * half + 2 * t + j > 0:
                            P.wait("tensor", s_denA, 16 * half + 2 * t + j)
                        for c in range(6):
                            P.op("tensor", lambda t_, j=j, c=c: t_.matmul(
                                ps_dot[:, 4 * j:4 * j + 4], fT[:, 6 * j + c, :], normA[:, j, c, :],
                                start=(c == 0), stop=(c == 5)), s_num if c == 5 else None)
                        P.wait("vector", s_num, 4 * t + 2 * (j + 1))
                        P.wait("vector", s_ssq, 2 * (t + 1))
                        invn = sstat.ap()[:, 4 + j:5 + j]
                        epsn = sstat.ap()[:, 22 + j:23 + j]
                        den = sstat.ap()[:, 6:10]
                        P.op("vector", lambda v, j=j, epsn=epsn, den=den: v.tensor_scalar(
                            den, ps_dot[:, 4 * j:4 * j + 4], 1.0, epsn, ALU.mult, ALU.add), s_denA)
                        P.bar("vector")
                        P.op("vector", lambda v, den=den: v.reciprocal(sstat[:, 10:14], den))
                        P.bar("vector")
                        P.op("vector", lambda v: v.tensor_scalar(
                            sstat[:, 14:18], sstat[:, 10:14], -1.0, None, ALU.mult))
                        P.bar("vector")
                        def stt_mv(hl, j=j, t=t, nbA=nbA):
                            P.op("vector", lambda v, hl=hl, j=j, nbA=nbA: v.scalar_tensor_tensor(
                                mv_scr[:, hl, :], nbA[:, 128 * hl:128 * (hl + 1)],
                                sstat[:, 14 + hl:15 + hl],
                                kv_bf[:, 256 + 128 * j:384 + 128 * j], ALU.mult, ALU.add))
                        def stt_wmv(hl, j=j, t=t, invn=invn):
                            h8 = 4 * j + hl
                            P.op("vector", lambda v, hl=hl, h8=h8, t=t, invn=invn:
                                 v.scalar_tensor_tensor(
                                wmv[:, t, h8, :], mv_scr[:, hl, :], invn,
                                mb_bf[:, 128 * h8:128 * (h8 + 1)], ALU.mult, ALU.mult))
                        stt_mv(0)
                        stt_mv(1)
                        stt_mv(2)
                        stt_mv(3)
                        P.bar("vector")
                        P.op("vector", lambda v: v.nop(), s_mvd)
                        P.op("vector", lambda v, den=den, invn=invn: v.tensor_scalar(
                            sstat[:, 18:22], den, invn, None, ALU.mult))
                        P.bar("vector")
                        stt_wmv(0)
                        stt_wmv(1)
                        stt_wmv(2)
                        stt_wmv(3)
                        P.op("vector", lambda v: v.tensor_scalar(
                            sstat[:, 14:18], sstat[:, 18:22], -1.0, 1.0, ALU.mult, ALU.add))
                        P.bar("vector")
                        P.op("vector", lambda v: v.tensor_scalar(
                            sstat[:, 18:22], sstat[:, 14:18], 0.0, 1.0, ALU.max, ALU.min))
                        P.bar("vector")
                        P.op("vector", lambda v, t=t, j=j, invn=invn: v.tensor_scalar(
                            coefn[:, t, 4 * j:4 * j + 4], sstat[:, 18:22], invn,
                            None, ALU.mult), s_chain)
                    if debug and gt == 0:
                        P.wait("sync", s_chain, 2)
                        P.dma("sync", dbg["nst0"], sstat[:, 0:24], s_out)
                        P.dma("sync", dbg["wmv0"], wmv[:, 0, :, :], s_out)
                        P.dma("sync", dbg["coefn0"], coefn[:, 0, :], s_out)
                    if half == 1 and 2 <= t <= 5:
                        for hh in (2 * (t - 2), 2 * (t - 2) + 1):
                            build_head_v(hh)

                P.wait("tensor", s_chain)
                P.wait("tensor", s_ev1)
                s_dmm = P.sem(f"dmm{half}")
                s_dev = P.sem(f"dev{half}")
                s_dwr = [P.sem(f"dwr{half}_0")]
                half_sems[half]["dev"] = s_dev
                half_sems[half]["dwr"] = s_dwr
                for c in range(6):
                    reg = (c % 2) * 1024
                    if c >= 1:
                        P.wait("tensor", s_dev, c)
                    for tt in range(NT):
                        for j in range(2):
                            P.op("tensor", lambda t_, c=c, tt=tt, j=j, reg=reg: t_.matmul(
                                ps_a[:, reg + 512 * j:reg + 512 * (j + 1)],
                                F_sb[:, tt, j, 128 * c:128 * (c + 1)],
                                wmv[:, tt, 4 * j:4 * j + 4, :],
                                start=(tt == 0), stop=(tt == NT - 1)), None)
                            dps = ps_dot[:, 0:4] if j == 0 else ps_n.ap()[:, 0:4]
                            P.op("tensor", lambda t_, c=c, tt=tt, j=j, dps=dps: t_.matmul(
                                dps, F_sb[:, tt, j, 128 * c:128 * (c + 1)],
                                coefn[:, tt, 4 * j:4 * j + 4],
                                start=(tt == 0), stop=(tt == NT - 1)),
                                s_dmm if (tt == NT - 1 and j == 1) else None)
                    P.wait("scalar", s_dmm, c + 1)
                    if c >= 1:
                        P.wait("scalar", s_dwr[0], 16 * c)
                    P.op("scalar", lambda s, c=c, reg=reg: s.copy(
                        dstage[:, 0, 0:1024], ps_a[:, reg:reg + 1024]))
                    P.op("scalar", lambda s, c=c: s.copy(dstage[:, 0, 1024:1028], ps_dot[:, 0:4]))
                    P.op("scalar", lambda s, c=c: s.copy(dstage[:, 0, 1028:1032], ps_n.ap()[:, 0:4]), s_dev)
                    P.wait("sync", s_dev, c + 1)
                    P.dma("sync", d_loc[half][c], dstage[:, 0, :], s_dwr[0])
                if half == 1:
                    P.wait("gpsimd", s_proj, 8)
                    for ci in range(8):
                        s_wg.bump(16)
                        P.emit("gpsimd", lambda e, ci=ci: e.dma_start(
                            out=wg_pre[0][:, 2 * ci:2 * ci + 2, :],
                            in_=wgP[:, 2 * ci:2 * ci + 2, :]).then_inc(s_wg.h, 16))
                    s_wg.bump(16)
                    P.emit("gpsimd", lambda e: e.dma_start(
                        out=wg_pre[1][:], in_=bg).then_inc(s_wg.h, 16))
                P.wait("gpsimd", s_dwr[0])
                P.op("gpsimd", lambda g, half=half: g.collective_compute(
                    "AllReduce", ALU.add, replica_groups=[list(range(NCORE))],
                    ins=[d_loc[half]], outs=[d_red[half]]), s_cc[half])
                P.op("tensor", lambda t: t.nop(), s_endA1 if half == 0 else s_endA2)
                if half == 1:
                    P.wait("sync", s_q[0])
                    P.op("sync", lambda e: e.nop(), s_endA2)

        esB = contextlib.ExitStack()
        with esB:
            def sbB(name, shape, dt):
                return esB.enter_context(nc.sbuf_tensor(name, shape, dt))

            wg_sb = sbB("wg_sb", [128, NCH, HID], BF16)
            bias_g = sbB("bias_g", [1, HID], BF16)
            wg_pre[0] = wg_sb
            wg_pre[1] = bias_g
            memN_b = sbB("memN_b", [128, 2, 6, 512], BF16)
            memNs[1] = memN_b
            q_in = sbB("q_in", [128, 2, HID], BF16)
            xe4 = sbB("xe4", [128, 2, 4, 262], BF16)
            Fq = sbB("Fq", [128, 2, 4, FD], BF16)
            fTq = sbB("fTq", [128, 2, 24, 128], BF16)
            gate_sb = sbB("gate_sb", [128, HID], F32)
            hs_sb = sbB("hs_sb", [128, HID], F32)
            mhg = sbB("mhg", [128, HID], F32)
            ln2_sb = mhg
            y_sb = sbB("y_sb", [128, HID], F32)

            s_hs2 = P.sem("hs2")
            s_den = P.sem("den")
            s_gproj = P.sem("gproj")
            s_gate = P.sem("gate")
            s_qrd = [P.sem("qrd0"), P.sem("qrd1")]
            s_relu2 = P.sem("relu2")
            s_Fq = P.sem("Fq")
            s_ssq2 = P.sem("ssq2")
            s_numa = P.sem("numa")
            s_mhg = P.sem("mhg")
            s_attn = P.sem("attn")
            s_ln2 = P.sem("ln2")
            s_yv = P.sem("yv")
            s_ywr = [P.sem("ywr0")]

            for hh in range(8, 16):
                build_head_dma(hh)

            def b1_loads(u):
                us = u % 2
                if u < 2:
                    P.wait("sync", s_hs, 8)
                else:
                    P.wait("sync", s_yv, u - 1)
                P.dma("sync", x_sb[:, us, :], xt[u], s_x[us])
                if u >= 2:
                    P.wait("sync", s_relu2, 4 * u - 4)
                if u == 0:
                    P.wait("sync", s_endA2)
                P.dma("sync", q_in[:, us, :], q_hbm[u], s_qrd[us])
                if u >= 2:
                    P.wait("sync", s_gproj, u - 1)
                P.dma("sync", hsT[:, us], ht_hbm[u], s_ht[us])

            b1_loads(0)
            fq_te = {}
            for t in range(NT):
                slot = t % 2
                if t + 1 < NT:
                    b1_loads(t + 1)
                if debug and t in (1, 2):
                    P.wait("scalar", s_out, s_out.count)
                    P.wait("vector", s_out, s_out.count)
                if t == 0:
                    P.wait("scalar", s_endA2)
                P.wait("tensor", s_wg)
                P.wait("tensor", s_ht[slot], 16 * (4 + t // 2 + 1))
                if t == 0:
                    P.wait("tensor", half_sems[1]["dev"], 6)
                    P.wait("tensor", s_endA2)
                if t > 0:
                    P.wait("tensor", s_gate, t)
                for c in range(NCH + 1):
                    last = c == NCH
                    lhs = ones_r[:] if last else hsT[:, slot, c, :]
                    for wi in range(4):
                        rhs = (bias_g[:, 512 * wi:512 * (wi + 1)] if last
                               else wg_sb[:, c, 512 * wi:512 * (wi + 1)])
                        P.op("tensor", lambda t_, lhs=lhs, rhs=rhs, wi=wi, c=c, last=last:
                             t_.matmul(ps_a[:, 512 * wi:512 * (wi + 1)], lhs, rhs,
                                       start=(c == 0), stop=last),
                             s_gproj if (last and wi == 3) else None)
                def emit_gate(t=t):
                    P.wait("scalar", s_gproj, t + 1)
                    if t > 0:
                        P.wait("scalar", s_mhg, 4 * t)
                        P.wait("scalar", s_ln2, t)
                    P.op("scalar", lambda s: s.activation(
                        gate_sb[:], ps_a[:, 0:2048], AF.Sigmoid), s_gate)
                    if debug and t == 0:
                        P.wait("sync", s_gate, 1)
                        P.dma("sync", dbg["gate0"], gate_sb[:], s_out)
                for g in range(4):
                    gi = 4 * t + g
                    par = g % 2
                    if t == 0 and g == 2:
                        for hh in range(8, 12):
                            build_head_v(hh)
                    if t == 0 and g == 3:
                        for hh in range(12, 16):
                            build_head_v(hh)
                    P.wait("scalar", s_qrd[slot], 16 * (t // 2 + 1))
                    if gi > 1:
                        P.wait("scalar", s_Fq, gi - 1)
                    qv = q_in.ap()[:, slot, :].rearrange("p (g4 d) -> p g4 d", d=128)
                    P.op("scalar", lambda s, g=g, par=par, qv=qv: s.activation(
                        xe4[:, par, :, 3:131], qv[:, 4 * g:4 * g + 4, :], AF.Relu))
                    P.op("scalar", lambda s, g=g, par=par, qv=qv: s.activation(
                        xe4[:, par, :, 131:259], qv[:, 4 * g:4 * g + 4, :], AF.Relu, scale=-1.0), s_relu2)
                    P.wait("vector", s_relu2, gi + 1)
                    P.op("vector", lambda v, par=par: v.tensor_copy(
                        xe4[:, par, :, 0:3], xe4[:, par, :, 256:259]))
                    P.bar("vector")
                    if gi > 1:
                        P.wait("vector", s_te, fq_te.get(par, 0))
                        P.wait("vector", s_ssq2, gi - 1)
                    for j in (1, 2, 3):
                        P.op("vector", lambda v, j=j, par=par: v.tensor_tensor(
                            Fq[:, par, :, 256 * (j - 1):256 * j], xe4[:, par, :, 3:259],
                            xe4[:, par, :, 3 - j:259 - j], ALU.mult), s_Fq if j == 3 else None)
                    P.wait("scalar", s_Fq, gi + 1)
                    if gi > 1:
                        P.wait("scalar", s_mhg, gi - 1)
                    sreg = 8 * par
                    for hl in range(4):
                        P.op("scalar", lambda s, hl=hl, sreg=sreg, par=par: s.activation(
                            sq_scr[:, hl % 2, :], Fq[:, par, hl, :], AF.Square,
                            accum_out=sstat[:, sreg + hl:sreg + hl + 1]))
                        if hl == 1:
                            P.bar("scalar")
                    P.bar("scalar")
                    P.op("scalar", lambda s, sreg=sreg: s.activation(
                        sstat[:, sreg + 4:sreg + 8], sstat[:, sreg:sreg + 4], AF.Sqrt), s_ssq2)
                    if g == 0:
                        emit_gate()
                    P.wait("tensor", s_Fq, gi + 1)
                    fqswaits = [(s_numa, gi - 1)] if gi > 1 else []
                    transposes([Fq.ap()[:, par, hl, 128 * c:128 * (c + 1)]
                                for hl in range(4) for c in range(6)],
                               fTq.ap()[:, par], swaits=fqswaits)
                    fq_te[par] = s_te.count
                    if gi == 0:
                        P.wait("tensor", half_sems[1]["dev"], 6)
                    if gi > 1:
                        P.wait("tensor", s_mhg, gi - 1)
                    P.wait("tensor", s_memN, 4 * (g + 1))
                    memg = memNs[g // 2]
                    nbank = (ps_n.ap() if par == 0 else ps_a.ap()[:, 2048:2560])
                    for hl in range(4):
                        for c in range(6):
                            P.op("tensor", lambda t_, g=g, c=c, hl=hl, memg=memg, nb=nbank, par=par: t_.matmul(
                                nb[:, 128 * hl:128 * (hl + 1)], fTq[:, par, 6 * hl + c, :],
                                memg[:, g % 2, c, 128 * hl:128 * (hl + 1)],
                                start=(c == 0), stop=(c == 5)), None)
                    if gi > 0:
                        P.wait("tensor", s_den, gi)
                    for hl in range(4):
                        for c in range(6):
                            P.op("tensor", lambda t_, g=g, c=c, hl=hl, par=par: t_.matmul(
                                ps_dot[:, 4 * par + hl:4 * par + hl + 1], fTq[:, par, 6 * hl + c, :],
                                normN[:, g, c, hl:hl + 1],
                                start=(c == 0), stop=(c == 5)),
                                s_numa if (hl == 3 and c == 5) else None)
                    P.wait("vector", s_numa, gi + 1)
                    P.wait("vector", s_ssq2, gi + 1)
                    n4 = sstat.ap()[:, sreg + 4:sreg + 8]
                    den = sstat.ap()[:, 16:20]
                    P.op("vector", lambda v, n4=n4, den=den, par=par: v.scalar_tensor_tensor(
                        den, n4, EPS, ps_dot[:, 4 * par:4 * par + 4], ALU.mult, ALU.add), s_den)
                    P.bar("vector")
                    P.op("vector", lambda v, den=den: v.reciprocal(sstat[:, 20:24], den))
                    P.bar("vector")
                    P.wait("vector", s_gate, t + 1)
                    for hl in range(4):
                        hcol = 128 * (4 * g + hl)
                        P.op("vector", lambda v, hl=hl, hcol=hcol, nb=nbank: v.scalar_tensor_tensor(
                            mhg[:, hcol:hcol + 128], nb[:, 128 * hl:128 * (hl + 1)],
                            sstat[:, 20 + hl:21 + hl], gate_sb[:, hcol:hcol + 128],
                            ALU.mult, ALU.mult), s_mhg if hl == 3 else None)
                P.wait("scalar", s_x[slot], 16 * (4 + t // 2 + 1))
                if t >= 1:
                    P.wait("scalar", s_attn, t)
                P.op("scalar", lambda s, slot=slot, t=t: s.activation(
                    hs_sb[:], x_sb[:, slot, :], AF.Identity,
                    bias=msb[:, 2 * t + 1:2 * t + 2], scale=msb[:, 2 * t:2 * t + 1]), s_hs2)
                if debug and t == 0:
                    P.wait("vector", s_out, s_out.count)
                if t > 0:
                    P.wait("vector", s_ln2, t)
                P.wait("vector", s_hs2, t + 1)
                P.bar("vector")
                P.op("vector", lambda v: v.tensor_tensor(gate_sb[:], mhg[:], hs_sb[:], ALU.add), s_attn)
                P.bar("vector")
                if debug and t == 0:
                    P.wait("sync", s_attn, 1)
                    P.dma("sync", dbg["mhg0"], gate_sb[:], s_out)
                ln_stats(gate_sb.ap(), lns[:, 5:6], lns[:, 6:7])
                P.wait("scalar", s_lnd)
                P.op("scalar", lambda s: s.activation(
                    ln2_sb[:], gate_sb[:], AF.Identity, bias=lns[:, 6:7], scale=lns[:, 5:6]), s_ln2)
                P.wait("vector", s_ln2, t + 1)
                P.wait("vector", s_ywr[0], 16 * t)
                P.op("vector", lambda v, slot=slot: v.tensor_tensor(
                    y_sb[:], ln2_sb[:], x_sb[:, slot, :], ALU.add), s_yv)
                P.wait("sync", s_yv, t + 1)
                P.dma("sync", y_hbm[t], y_sb[:], s_ywr[0])
                if debug and t == 0:
                    P.dma("sync", dbg["y0"], y_sb[:], s_out)
            if debug:
                P.wait("sync", s_memN, 16)
                P.dma("sync", dbg["memN"].rearrange("p g c x -> p (g c x)")
                      .rearrange("p (g2 r) -> p g2 r", g2=2)[:, 0, :], memN_a.ap()
                      .rearrange("p g c x -> p (g c x)"), s_out)
                P.dma("sync", dbg["memN"].rearrange("p g c x -> p (g c x)")
                      .rearrange("p (g2 r) -> p g2 r", g2=2)[:, 1, :], memN_b.ap()
                      .rearrange("p g c x -> p (g c x)"), s_out)
                P.dma("sync", dbg["normN"], normN[:], s_out)
            P.op("tensor", lambda t: t.nop(), s_endB1)
            P.op("vector", lambda v: v.nop(), s_endB1)
            P.op("scalar", lambda s: s.nop(), s_endB1)
            P.wait("sync", s_ywr[0])
            P.wait("sync", s_qrd[0])
            P.wait("sync", s_qrd[1])
            if debug:
                P.wait("sync", s_out, s_out.count)
            P.op("sync", lambda e: e.nop(), s_endB1)

        x_es.close()
        esC = contextlib.ExitStack()
        with esC:
            def sbC(name, shape, dt):
                return esC.enter_context(nc.sbuf_tensor(name, shape, dt))

            sT = sbC("sT", [128, NCH, HID], BF16)
            wf_st = sbC("wf_st", [128, 2, HID], BF16)
            absw = sbC("absw", [128, HID], BF16)
            alpha_row = sbC("alpha_row", [1, HID], F32)
            inva = sbC("inva", [1, HID], F32)
            biasf_sb = sbC("biasf_sb", [1, HID], F32)
            bias_pre = sbC("bias_pre", [1, HID], BF16)
            alpha_bc = sbC("alpha_bc", [128, HID], F32)
            y_in = sbC("y_in", [128, 2, HID], F32)
            hs2b = sbC("hs2b", [128, HID], BF16)
            z_sb = sbC("z_sb", [128, HID], F32)
            ln4_sb = sbC("ln4_sb", [128, HID], F32)
            out_sb = sbC("out_sb", [128, HID], F32)

            s_wf = [P.sem("wf0"), P.sem("wf1")]
            s_bcr = P.sem("bcr")
            s_abc = P.sem("abc")
            s_bfl = P.sem("bfl")
            s_sign = P.sem("sign")
            s_amm = P.sem("amm")
            s_aev = P.sem("aev")
            s_yrd = [P.sem("yrd0"), P.sem("yrd1")]
            s_hs3 = P.sem("hs3")
            s_fmm = P.sem("fmm")
            s_z = P.sem("z")
            s_ln4 = P.sem("ln4")
            s_outv = P.sem("outv")
            s_od = [P.sem("od0")]

            P.wait("gpsimd", s_endB1)
            P.dma("gpsimd", biasf_sb[:], bfh, s_bfl)
            for c in range(NCH):
                slot = c % 2
                if c >= 2:
                    P.wait("gpsimd", s_sign, c - 1)
                P.dma("gpsimd", wf_st[:, slot, 0:1024],
                      wfT.rearrange("(c p) j -> p c j", p=128)[:, c, 0:1024], s_wf[slot])
                P.dma("gpsimd", wf_st[:, slot, 1024:2048],
                      wfT.rearrange("(c p) j -> p c j", p=128)[:, c, 1024:2048], s_wf[slot])
                P.wait("scalar", s_wf[slot], 32 * (c // 2 + 1))
                if c == 0:
                    P.wait("scalar", s_endB1, 4)
                if c >= 1:
                    P.wait("scalar", s_amm, c)
                P.op("scalar", lambda s, c=c, slot=slot: s.activation(
                    sT[:, c, :], wf_st[:, slot, :], AF.Sign))
                P.op("scalar", lambda s, c=c, slot=slot: s.activation(
                    absw[:], wf_st[:, slot, :], AF.Abs), s_sign)
                P.wait("tensor", s_sign, c + 1)
                if c == 0:
                    P.wait("tensor", s_endB1, 4)
                    P.wait("tensor", s_z, 0)
                for wi in range(4):
                    P.op("tensor", lambda t_, wi=wi, c=c: t_.matmul(
                        ps_a.ap()[0:1, 512 * wi:512 * (wi + 1)], ones_c[:],
                        absw[:, 512 * wi:512 * (wi + 1)],
                        start=(c == 0), stop=(c == NCH - 1)),
                        s_amm if wi == 3 else None)
            P.wait("scalar", s_amm, NCH)
            for wi in range(4):
                P.op("scalar", lambda s, wi=wi: s.activation(
                    alpha_row[:, 512 * wi:512 * (wi + 1)], ps_a.ap()[0:1, 512 * wi:512 * (wi + 1)],
                    AF.Identity, scale=1.0 / HID), s_aev)
            P.wait("vector", s_aev, 4)
            P.wait("vector", s_endB1, 4)
            P.op("vector", lambda v: v.reciprocal(inva[:], alpha_row[:]))
            P.wait("vector", s_bfl, 16)
            P.bar("vector")
            P.op("vector", lambda v: v.tensor_tensor(bias_pre[:], biasf_sb[:], inva[:], ALU.mult), s_aev)
            def emit_bc():
                P.wait("tensor", s_aev, 5)
                for wi in range(4):
                    if wi > 0:
                        P.wait("tensor", s_abc, wi)
                    P.op("tensor", lambda t_, wi=wi: t_.matmul(
                        ps_n[:, 0:512], ones_rf[:],
                        alpha_row[:, 512 * wi:512 * (wi + 1)], start=True, stop=True),
                        s_bcr)
                    P.wait("scalar", s_bcr, wi + 1)
                    P.op("scalar", lambda s, wi=wi: s.copy(
                        alpha_bc[:, 512 * wi:512 * (wi + 1)], ps_n[:, 0:512]), s_abc)
            if debug:
                P.wait("sync", s_aev, 4)
                P.dma("sync", dbg["alpha"], alpha_row[:], s_out)

            def b2_yload(t):
                slot = t % 2
                if t == 0:
                    P.wait("sync", s_endB1, 4)
                if t >= 2:
                    P.wait("sync", s_outv, t - 1)
                if t == 0:
                    P.wait("sync", s_ywr[0], 16 * 8)
                P.dma("sync", y_in[:, slot, :], y_hbm[t], s_yrd[slot])

            def b2_ln3(t):
                slot = t % 2
                if debug and t == 1:
                    P.wait("vector", s_out, s_out.count)
                P.wait("vector", s_yrd[slot], 16 * (t // 2 + 1))
                ln_stats(y_in.ap()[:, slot, :], lns[:, 5:6], lns[:, 6:7])
                P.wait("scalar", s_lnd)
                if t >= 1:
                    P.wait("scalar", s_te, s_te.count)
                P.op("scalar", lambda s, slot=slot: s.activation(
                    hs2b[:], y_in[:, slot, :], AF.Identity,
                    bias=lns[:, 6:7], scale=lns[:, 5:6]), s_hs3)

            def b2_ln4out(v):
                vslot = v % 2
                P.wait("vector", s_z, v + 1)
                P.bar("vector")
                ln_stats(z_sb.ap(), lns[:, 16:17], lns[:, 17:18])
                P.wait("scalar", s_lnd)
                P.op("scalar", lambda s: s.activation(
                    ln4_sb[:], z_sb[:], AF.Identity, bias=lns[:, 17:18], scale=lns[:, 16:17]), s_ln4)
                P.wait("vector", s_ln4, v + 1)
                if v >= 1:
                    P.wait("vector", s_od[0], 16 * v)
                P.op("vector", lambda v_, vslot=vslot: v_.tensor_tensor(
                    out_sb[:], ln4_sb[:], y_in[:, vslot, :], ALU.add), s_outv)
                P.wait("sync", s_outv, v + 1)
                P.dma("sync", outh.rearrange("(n p) d -> n p d", p=128)[v], out_sb[:], s_od[0])

            b2_yload(0)
            b2_ln3(0)
            for t in range(NT):
                slot = t % 2
                P.wait("tensor", s_hs3, t + 1)
                b2swaits = [(s_fmm, t - 1)] if t >= 2 else []
                transposes([hs2b.ap()[:, 128 * c:128 * (c + 1)] for c in range(NCH)],
                           hsT.ap()[:, slot], swaits=b2swaits)
                if t > 0:
                    P.wait("tensor", s_z, t)
                for c in range(NCH + 1):
                    last = c == NCH
                    if t == 0 and c == 0:
                        P.wait("tensor", s_aev, 4)
                    if t == 0 and last:
                        P.wait("tensor", s_aev, 5)
                    lhs = ones_r[:] if last else hsT[:, slot, c, :]
                    for wi in range(4):
                        rhs = (bias_pre[:, 512 * wi:512 * (wi + 1)] if last
                               else sT[:, c, 512 * wi:512 * (wi + 1)])
                        P.op("tensor", lambda t_, lhs=lhs, rhs=rhs, wi=wi, c=c, last=last:
                             t_.matmul(ps_a[:, 512 * wi:512 * (wi + 1)], lhs, rhs,
                                       start=(c == 0), stop=last),
                             s_fmm if (last and wi == 3) else None)
                if t == 0:
                    emit_bc()
                if t > 0:
                    b2_ln4out(t - 1)
                if t + 1 < NT:
                    b2_yload(t + 1)
                    b2_ln3(t + 1)
                P.wait("vector", s_fmm, t + 1)
                P.wait("vector", s_abc, 4)
                if debug and t == 1:
                    P.wait("vector", s_out, s_out.count)
                if t > 0:
                    P.wait("vector", s_ln4, t)
                P.op("vector", lambda v: v.tensor_tensor(z_sb[:], ps_a[:, 0:2048], alpha_bc[:], ALU.mult), s_z)
                if debug and t == 0:
                    P.wait("sync", s_z, 1)
                    P.dma("sync", dbg["z0"], z_sb[:], s_out)
            b2_ln4out(NT - 1)

            P.wait("gpsimd", s_od[0])
            if debug:
                P.wait("gpsimd", s_out)

        blk = es.enter_context(nc.Block())
        for s in P.sems:
            s.h = es.enter_context(nc.semaphore(s.name))

        @blk.tensor
        def _(e):
            for f in P.q["tensor"]:
                f(e)

        @blk.vector
        def _(e):
            for f in P.q["vector"]:
                f(e)

        @blk.scalar
        def _(e):
            for f in P.q["scalar"]:
                f(e)

        @blk.gpsimd
        def _(e):
            for f in P.q["gpsimd"]:
                f(e)

        @blk.sync
        def _(e):
            for f in P.q["sync"]:
                f(e)

    return nc, dbg


_NC_CACHE = {}


def _get_nc(debug=False):
    key = bool(debug)
    if key not in _NC_CACHE:
        _NC_CACHE[key] = build_nc(debug)
    return _NC_CACHE[key]


def make_in_maps(inputs):
    x = np.ascontiguousarray(np.asarray(inputs["L0_attn_hidden_states"], np.float32).reshape(8192, HID))
    kc = np.asarray(inputs["L0_attn_Wk"], np.float32).T.reshape(NCH, 128, 512)
    vc = np.asarray(inputs["L0_attn_Wv"], np.float32).T.reshape(NCH, 128, 512)
    mc = np.asarray(inputs["L0_attn_Wmb"], np.float32).T.reshape(NCH, 128, HID)
    qc = np.asarray(inputs["L0_attn_Wq"], np.float32).T.reshape(NCH, 128, HID)
    mem = np.asarray(inputs["L0_attn_memory"], np.float32)
    com = {
        "WfT": np.ascontiguousarray(np.asarray(inputs["L0_ffn_weight"], np.float32).T),
        "WgP": np.ascontiguousarray(
            np.asarray(inputs["L0_attn_Wg"], np.float32).T.reshape(NCH, 128, HID)
            .transpose(1, 0, 2)),
        "memP4": np.ascontiguousarray(
            mem.reshape(2, 2, 4, 6, 128, 128).transpose(0, 1, 4, 3, 2, 5)
            .reshape(4, 128, 6, 512)),
        "memPh": np.ascontiguousarray(mem.reshape(16, 6, 128, 128).transpose(0, 2, 1, 3)),
        "bk": np.asarray(inputs["L0_attn_bk"], np.float32).reshape(1, 512),
        "bv": np.asarray(inputs["L0_attn_bv"], np.float32).reshape(1, 512),
        "bm": np.asarray(inputs["L0_attn_bmb"], np.float32).reshape(1, HID),
        "bq": np.asarray(inputs["L0_attn_bq"], np.float32).reshape(1, HID),
        "bg": np.asarray(inputs["L0_attn_bg"], np.float32).reshape(1, HID),
        "bf": np.asarray(inputs["L0_ffn_bias"], np.float32).reshape(1, HID),
        "norm96": np.ascontiguousarray(np.asarray(inputs["L0_attn_norm"], np.float32).reshape(96, 128)),
        "ident": np.eye(128, dtype=np.float32),
    }
    for h in range(2):
        com[f"Wa{h}"] = np.ascontiguousarray(np.concatenate(
            [kc[:, :, 256 * h:256 * (h + 1)], vc[:, :, 256 * h:256 * (h + 1)],
             mc[:, :, 1024 * h:1024 * (h + 1)], qc[:, :, 1024 * h:1024 * (h + 1)]],
            axis=2).transpose(1, 0, 2))
    in_maps = []
    for c in range(NCORE):
        m = dict(com)
        m["x_part"] = np.ascontiguousarray(x[c * TOK:(c + 1) * TOK])
        in_maps.append(m)
    return in_maps


def kernel(**inputs):
    nc, _ = _get_nc(False)
    in_maps = make_in_maps(inputs)
    res = run_bass_kernel_spmd(nc, in_maps, core_ids=list(range(NCORE)))
    out = np.concatenate([r["out_part"] for r in res.results], axis=0)
    return out.reshape(2, 4096, HID)
